# version 09
# baseline (speedup 1.0000x reference)
import collections
import contextlib
import math
import numpy as np
import ml_dtypes

import concourse.bass as bass
import concourse.mybir as mybir
from concourse.bass_utils import run_bass_kernel_spmd

F32 = mybir.dt.float32
BF16 = mybir.dt.bfloat16
AF = mybir.ActivationFunctionType
ALU = mybir.AluOpType
AX = mybir.AxisListType

L, DM, DI, NH, HP, NS, PROJ = 4096, 1024, 2048, 16, 128, 64, 4256
T, Q = 256, 128
NCH = T // 128
NT = L // T
TAIL_DEFER = True
TAIL_POPS = {0: 1, 1: 1, 2: 1, 3: 1, 4: 1, 5: 1, 6: 1, 7: 1}
MAGIC = 1.5 * 2.0 ** 23
PI = math.pi
INV2PI = 1.0 / (2.0 * math.pi)

ENGS = ("sy", "pe", "dv", "ac", "po")
DMA_CH = ("dC", "dW8", "dW0", "dW1", "dU0", "dU1", "dV0", "dV1", "dO",
          "dF0", "dF1", "dT", "dWo", "dG0", "dG1", "dDB")
PO_CH = {"dT", "dWo", "dG0", "dG1"}


def _maxm(marks):
    best = {}
    stack = list(marks)
    while stack:
        m = stack.pop()
        if m is None:
            continue
        if isinstance(m, list):
            stack.extend(m)
            continue
        s_, v = m
        if v and (s_ not in best or v > best[s_]):
            best[s_] = v
    return [(k, v) for k, v in best.items()]


class Sched:
    def __init__(self, nc):
        self.nc = nc
        self.q = {0: {e: [] for e in ENGS}, 1: {e: [] for e in ENGS}}
        self.cnt = {}
        self.sems = {}
        self.phase = 0

    def new_sem(self, name, st):
        self.sems[name] = st.enter_context(self.nc.semaphore(name))
        self.cnt[name] = 0

    SERIAL = ("dv", "ac", "po")

    def emit(self, eng, fn, wait=(), sig=None):
        wait = list(wait)
        if eng in self.SERIAL:
            wait.append((eng, self.cnt[eng]))
            if sig is None or sig == eng:
                sig = eng
        markv = None
        inc = 0
        if sig is not None:
            inc = 16 if sig in DMA_CH else 1
            self.cnt[sig] += inc
            markv = self.cnt[sig]
        self.q[self.phase][eng].append((_maxm(wait), fn, sig, inc))
        return (sig, markv) if sig is not None else None

    def mark(self, sig):
        return (sig, self.cnt[sig])

    def run(self, phase, eng_name, eng):
        for waits, fn, sig, inc in self.q[phase][eng_name]:
            for s_, v in waits:
                eng.wait_ge(self.sems[s_], v)
            ins = fn(eng)
            if sig is not None:
                ins.then_inc(self.sems[sig], inc)


def build_program(dbg=()):
    nc = bass.Bass("TRN2", target_bir_lowering=False, detect_race_conditions=False)
    st = contextlib.ExitStack()

    d_u = nc.dram_tensor("u", [L, DM], F32, kind="ExternalInput").ap()
    d_out = nc.dram_tensor("out", [L, DM], F32, kind="ExternalOutput").ap()
    d_win = nc.dram_tensor("w_in", [DM, PROJ], F32, kind="ExternalInput").ap()
    d_thw = nc.dram_tensor("w_th", [DM, 32], F32, kind="ExternalInput").ap()
    d_wout = nc.dram_tensor("w_out", [DI, DM], F32, kind="ExternalInput").ap()
    d_wffn = nc.dram_tensor("w_ffn", [DM, DM], F32, kind="ExternalInput").ap()
    d_idbf = nc.dram_tensor("idbf", [128, 128], BF16, kind="ExternalInput").ap()
    d_idf = nc.dram_tensor("idf", [128, 128], F32, kind="ExternalInput").ap()
    d_tri = nc.dram_tensor("tri", [128, 128], F32, kind="ExternalInput").ap()
    d_oh48 = nc.dram_tensor("oh48", [48, NH * 128], BF16, kind="ExternalInput").ap()
    d_pp48 = nc.dram_tensor("pp48", [48, (NH // 2) * 128], BF16, kind="ExternalInput").ap()
    d_scale_big = nc.dram_tensor("scale_big", [128, DM], BF16, kind="ExternalInput").ap()
    d_bcol = nc.dram_tensor("bcol", [128, 8], BF16, kind="ExternalInput").ap()
    d_brdtb = nc.dram_tensor("br_dtb", [128, NH], F32, kind="ExternalInput").ap()
    d_brD = nc.dram_tensor("br_D", [128, NH], F32, kind="ExternalInput").ap()
    d_brbc = nc.dram_tensor("br_bc", [128, 128], F32, kind="ExternalInput").ap()
    d_brbcb = nc.dram_tensor("br_bcb", [128, 128], F32, kind="ExternalInput").ap()
    d_brfb = nc.dram_tensor("br_fb", [128, DM], F32, kind="ExternalInput").ap()
    d_alog = nc.dram_tensor("alog", [NH, 1], F32, kind="ExternalInput").ap()
    d_oh8 = nc.dram_tensor("oh8", [8, 8 * 128], F32, kind="ExternalInput").ap()
    d_winb = nc.dram_tensor("w_in_bf", [DM, PROJ], BF16, kind="Internal").ap()
    d_bw = nc.dram_tensor("bw_row", [1, PROJ], BF16, kind="Internal").ap()

    dbg_set = {d[0] for d in dbg}
    dbg_d = {}
    for d in dbg:
        name, shape = d[0], d[1]
        dt = BF16 if (len(d) > 2 and d[2] == "bf16") else F32
        dbg_d[name] = nc.dram_tensor("dbg_" + name, list(shape), dt, kind="ExternalOutput").ap()

    SB = {}

    def sb(name, shape, dt):
        h = st.enter_context(nc.sbuf_tensor(name, shape, dt))
        SB[name] = h
        return h

    wout_sb = sb("wout_sb", [128, 16 * DM], BF16)
    wffn_sb = sb("wffn_sb", [128, 8 * DM], BF16)
    thw_sb = sb("thw_sb", [128, 8 * 32], BF16)
    win_buf = [sb(f"win{b}", [128, 8 * 512 + 512], BF16) for b in range(2)]
    win8 = [sb("win8a", [128, 8 * 160], BF16)]
    id_bf = sb("id_bf", [128, 128], BF16)
    id_f = sb("id_f", [128, 128], F32)
    tri_sb = sb("tri_sb", [128, 128], F32)
    oh48_sb = sb("oh48_sb", [48, NH * 128], BF16)
    pp48_sb = sb("pp48_sb", [48, (NH // 2) * 128], BF16)
    scale_big = sb("scale_big_sb", [128, DM], BF16)
    bcol_sb = sb("bcol_sb", [128, 8], BF16)
    ones1b = sb("ones1b", [1, 128], BF16)
    bwT_sb = sb("bwT_sb", [128, 34], BF16)
    bwT_acc = sb("bwT_acc", [128, 34], F32)
    bw8_st = sb("bw8_st", [1, 160], BF16)
    br_dtb = sb("br_dtb_sb", [128, NH], F32)
    br_D = sb("br_D_sb", [128, NH], F32)
    br_bc = sb("br_bc_sb", [128, 128], F32)
    br_bcb = sb("br_bcb_sb", [128, 128], F32)
    br_fb = sb("br_fb_sb", [128, DM], F32)
    br_ws = sb("br_ws_sb", [128, DM], F32)
    acol = sb("acol", [NH, 1], F32)
    ones32 = sb("ones32", [32, Q], F32)

    uA = [sb(f"uA{b}", [128, DM], F32) for b in range(2)]
    uF = [sb(f"uF{b}", [128, DM], F32) for b in range(2)]
    x_act = sb("x_act", [128, NCH * DI], BF16)
    siluz = sb("siluz", [128, NCH * DI], BF16)
    t2_tok = [sb(f"t2tok{b}", [128, DM], BF16) for b in range(2)]
    t2T = [sb(f"t2T{k}", [128, DM], BF16) for k in range(2)]
    xinT = [sb(f"xinT{k}", [128, DM], BF16) for k in range(NCH)]
    bcdl = [sb(f"bcdl{b}", [128, NCH * 160], F32) for b in range(2)]
    dl_sb = [sb(f"dl{b}", [128, NCH * 32], F32) for b in range(2)]
    theta_tok = sb("theta_tok", [128, NCH * 32], F32)
    stats = [sb(f"stats{b}", [128, 64], F32) for b in range(2)]
    sq_scr = sb("sq_scr", [128, DM], BF16)
    sq_scrF = sb("sq_scrF", [128, DM], F32)
    BCbf = [sb(f"BCbf{b}", [128, NCH * 128], BF16) for b in range(2)]
    BCT = [sb(f"BCT{b}", [128, NCH * 128], BF16) for b in range(2)]
    dtT_sb = [sb(f"dtT{b}", [16, NCH * 128], F32) for b in range(2)]
    lamT_sb = [sb(f"lamT{b}", [16, NCH * 128], F32) for b in range(2)]
    termsT_sb = sb("termsT", [32, NCH * 128], F32)
    lamt_sb = sb("lamt", [16, NCH * 128], F32)
    lamrev_sb = sb("lamrev", [16, NCH * 128], F32)
    lhl = [sb(f"lhl{b}", [48, NCH * 128], BF16) for b in range(2)]
    toks = [sb(f"toks{b}", [128, NCH * 48], F32) for b in range(2)]
    phiT = sb("phiT", [32, T], F32)
    phicarry = sb("phicarry", [32, 1], F32)
    phitok = sb("phitok", [128, NCH * 32], F32)
    cs_tile = sb("cs_tile", [128, NCH * 96], F32)
    bctmp = sb("bctmp", [128, 128], F32)
    ropeA = sb("ropeA", [128, 128], F32)
    ropeB = sb("ropeB", [128, 128], F32)
    wT_sb = sb("wT_sb", [16, T], F32)
    dtlT_sb = sb("dtlT_sb", [16, T], F32)
    rT_sb = sb("rT_sb", [16, T], F32)
    lnrT_sb = sb("lnrT_sb", [16, T], F32)
    patchcol = sb("patchcol", [16, 1], F32)
    wtok = sb("wtok", [128, NCH * NH], F32)
    dvec = sb("dvec", [128, NCH * NH], F32)
    sdec_sb = sb("sdec_sb", [128, NCH * NH], F32)
    subtok = sb("subtok", [128, NCH * NH], F32)
    dgtmp = sb("dgtmp", [128, NS], F32)
    Gmask = sb("GmaskSB", [128, 128], BF16)
    nE_sb = [sb(f"nE{b}", [128, 128], F32) for b in range(4)]
    expE = [sb(f"expE{b}", [128, 128], BF16) for b in range(4)]
    Mt_sb = [sb(f"Mt{b}", [128, 128], BF16) for b in range(4)]
    edec2 = [sb(f"edec2{b}", [128, 128], F32) for b in range(2)]
    C2_sb = sb("C2_sb", [128, 128], BF16)
    Cdec2 = [sb(f"Cdec2{b}", [128, 128], BF16) for b in range(2)]
    Bw_sb = [sb(f"Bw{b}", [128, NS], BF16) for b in range(4)]
    H2f = [sb(f"H2f_{p}", [128, 128], F32) for p in range(8)]
    H2b = [sb(f"H2b_{p}", [128, 128], BF16) for p in range(8)]
    yd_sb = [sb(f"yd{b}", [128, 128], F32) for b in range(2)]
    yg_pool = [sb(f"ygp{b}", [128, 128], BF16) for b in range(4)]
    ygT = sb("ygT", [128, 16 * 128], BF16)
    mix_sb = sb("mix_sb", [128, DM], F32)
    x2_sb = sb("x2_sb", [128, DM], F32)
    c4_sb = sb("c4_sb", [128, DM], F32)
    xq_tok = sb("xq_tok", [128, DM], BF16)
    xqT = sb("xqT", [128, DM], BF16)
    ffn_sb = sb("ffn_sb", [128, DM], F32)
    scal = sb("scal", [128, 64], F32)
    sm16 = sb("sm16", [128, 96], F32)
    qst = sb("qst", [128, 16], F32)
    fsm = sb("fsm", [128, 16], F32)

    stg_w = [uA[0], uA[1]]
    stg_q = xq_tok
    stg_s = sb("stg_s", [128, 16], F32)
    stg_wt = sb("stg_wt", [8, 128], F32)

    ps_mm = [st.enter_context(nc.psum_tensor(f"ps_mm{b}", [128, 512], F32)) for b in range(2)]
    ps_trb = st.enter_context(nc.psum_tensor("ps_trb", [128, 1024], BF16))
    ps_trf = st.enter_context(nc.psum_tensor("ps_trf", [128, 512], F32))
    ps_br = [st.enter_context(nc.psum_tensor(f"ps_br{b}", [128, 512], F32)) for b in range(2)]
    ps_Y = [st.enter_context(nc.psum_tensor(f"ps_Y{b}", [128, 512], F32)) for b in range(2)]

    s = Sched(nc)
    for sem in ("pe", "dv", "ac", "po") + DMA_CH:
        s.new_sem(sem, st)

    def DMA(qname, out, in_, wait=(), slow=False):
        e = "po" if qname in PO_CH else "sy"
        if slow:
            return s.emit(e, lambda E, o=out, i=in_: E.dma_start(out=o, in_=i, allow_slow_non_contiguous=True), wait=wait, sig=qname)
        return s.emit(e, lambda E, o=out, i=in_: E.dma_start(out=o, in_=i), wait=wait, sig=qname)

    def OP(eng, fn, wait=(), sig=True):
        return s.emit(eng, fn, wait=wait, sig=(eng if sig else None))

    def SNAP(tag, src, wait):
        if tag in dbg_set:
            M["dbg_last"] = DMA("dDB", dbg_d[tag], src, wait=list(wait) + [M.get("dbg_last")])

    class Bank:
        def __init__(self):
            self.readers = []
            self.writer = None

        def pre_write(self):
            w = list(self.readers)
            self.readers = []
            return w

        def post_write(self, mark):
            self.writer = mark

        def read(self, mark):
            self.readers.append(mark)

    BK = {"trb": Bank(), "trf": Bank(), "mm0": Bank(), "mm1": Bank(),
          "br0": Bank(), "br1": Bank(), "Y0": Bank(), "Y1": Bank()}

    s.phase = 0
    for o, i in [
        (id_bf[:], d_idbf), (id_f[:], d_idf), (tri_sb[:], d_tri),
        (oh48_sb[:], d_oh48), (pp48_sb[:], d_pp48),
        (scale_big[:], d_scale_big), (bcol_sb[:], d_bcol),
        (br_dtb[:], d_brdtb), (br_D[:], d_brD), (br_bc[:], d_brbc),
        (br_bcb[:], d_brbcb), (br_fb[:], d_brfb), (acol[:], d_alog),
    ]:
        DMA("dC", o, i)
    m_consts = s.mark("dC")

    m_oh8 = DMA("dC", uF[0][0:8, 0:1024], d_oh8, wait=[m_consts])
    m_thw = DMA("dT", thw_sb[:].rearrange("p (c m) -> p c m", c=8), d_thw.rearrange("(c p) m -> p c m", p=128))
    m_wout = DMA("dWo", wout_sb[:].rearrange("p (c m) -> p c m", c=16), d_wout.rearrange("(c p) m -> p c m", p=128))

    m_bz = OP("dv", lambda E: E.memset(bwT_acc[:], 0.0))
    pst1 = pst2 = None
    m_bwadd = None
    for fc in range(8):
        m_l1_ = DMA("dG0", win_buf[0][:, 0:4096], d_win[fc * 128:(fc + 1) * 128, 0:4096], wait=[pst1, m_bwadd])
        m_l2_ = DMA("dG1", win8[0][:, 0:160], d_win[fc * 128:(fc + 1) * 128, 4096:4256], wait=[pst2])
        wv0 = [m_l1_, m_consts, m_bz] + BK["trf"].pre_write()
        for sub in range(32):
            m_bwmm = s.emit("pe", lambda E, fc=fc, sub=sub: E.matmul(
                ps_trf[0:128, 440 + sub:441 + sub], win_buf[0][:, sub * 128:(sub + 1) * 128],
                bcol_sb[:, fc:fc + 1], start=True, stop=True),
                wait=(wv0 if sub == 0 else ()), sig="pe" if sub == 31 else None)
        BK["trf"].post_write(m_bwmm)
        m_bwadd = OP("dv", lambda E: E.tensor_tensor(out=bwT_acc[:, 0:32], in0=bwT_acc[:, 0:32], in1=ps_trf[:, 440:472], op=ALU.add), wait=[m_bwmm])
        BK["trf"].read(m_bwadd)
        pst1 = DMA("dW0", d_winb[fc * 128:(fc + 1) * 128, 0:4096], win_buf[0][:, 0:4096], wait=[m_l1_])
        pst2 = DMA("dW1", d_winb[fc * 128:(fc + 1) * 128, 4096:4256], win8[0][:, 0:160], wait=[m_l2_])
    m_winb = [pst1, pst2]
    m_win8 = DMA("dW8", win8[0][:].rearrange("p (c m) -> p c m", c=8), d_winb[:, 4096:4256].rearrange("(c p) m -> p c m", p=128), wait=[pst2])
    m_bw8add = None
    for fc in range(8):
        wv8 = [m_win8] + BK["trf"].pre_write()
        m_bw8a = s.emit("pe", lambda E, fc=fc: E.matmul(
            ps_trf[0:128, 472:473], win8[0][:, fc * 160:fc * 160 + 128],
            bcol_sb[:, fc:fc + 1], start=True, stop=True), wait=wv8, sig=None)
        m_bw8b = s.emit("pe", lambda E, fc=fc: E.matmul(
            ps_trf[0:32, 473:474], win8[0][:, fc * 160 + 128:fc * 160 + 160],
            bcol_sb[:, fc:fc + 1], start=True, stop=True), sig="pe")
        BK["trf"].post_write(m_bw8b)
        OP("dv", lambda E: E.tensor_tensor(out=bwT_acc[:, 32:33], in0=bwT_acc[:, 32:33], in1=ps_trf[:, 472:473], op=ALU.add), wait=[m_bw8b], sig=False)
        m_bw8add = OP("dv", lambda E: E.tensor_tensor(out=bwT_acc[0:32, 33:34], in0=bwT_acc[0:32, 33:34], in1=ps_trf[0:32, 473:474], op=ALU.add))
        BK["trf"].read(m_bw8add)
    m_bwe = OP("dv", lambda E: E.tensor_copy(out=bwT_sb[:], in_=bwT_acc[:]), wait=[m_bw8add])
    m_bws1 = DMA("dG0", d_bw[0:1, 0:4224].rearrange("r (j p) -> p (r j)", p=128), bwT_sb[:, 0:33], wait=[m_bwe], slow=True)
    m_bws2 = DMA("dG1", d_bw[0:1, 4224:4256].rearrange("r (j p) -> p (r j)", p=32), bwT_sb[0:32, 33:34], wait=[m_bwe], slow=True)
    m_bw8st = DMA("dW8", bw8_st[:], d_bw[0:1, 4096:4256], wait=[m_bws1, m_bws2])

    m_ae = OP("ac", lambda E: E.activation(out=acol[:], in_=acol[:], func=AF.Exp), wait=[m_consts])
    m_acol = OP("dv", lambda E: E.tensor_scalar(out=acol[:], in0=acol[:], scalar1=-1.0, scalar2=None, op0=ALU.mult), wait=[m_ae])

    OP("po", lambda E: E.memset(ones32[:], 1.0), wait=[], sig=False)
    OP("po", lambda E: E.memset(phicarry[:], 0.0), sig=False)
    for p in range(8):
        OP("po", lambda E, p=p: E.memset(H2f[p][:], 0.0), sig=False)
        OP("po", lambda E, p=p: E.memset(H2b[p][:], 0.0), sig=False)
    OP("po", lambda E: E.memset(toks[0][:], 0.0), sig=False)
    OP("po", lambda E: E.memset(toks[1][:], 0.0), sig=False)
    OP("po", lambda E: E.memset(lhl[0][:], 0.0), sig=False)
    OP("po", lambda E: E.memset(lhl[1][:], 0.0), sig=False)
    OP("po", lambda E: E.memset(ones1b[:], 1.0), sig=False)
    m_initpo = OP("po", lambda E: E.memset(patchcol[:], 0.0))

    m_qread = [None, None]
    m_wffn_ev = None
    for oc in range(8):
        b = oc % 2
        m_lw = DMA("dF0" if b == 0 else "dF1", stg_w[b][:], d_wffn[oc * 128:(oc + 1) * 128, :], wait=[m_qread[b]])
        OP("dv", lambda E, b=b: E.tensor_reduce(out=scal[:, 0:1], in_=stg_w[b][:], axis=AX.X, op=ALU.max, apply_absolute_value=True), wait=[m_lw], sig=False)
        OP("dv", lambda E: E.tensor_scalar(out=scal[:, 1:2], in0=scal[:, 0:1], scalar1=1e-8, scalar2=None, op0=ALU.max), sig=False)
        OP("dv", lambda E: E.reciprocal(out=scal[:, 2:3], in_=scal[:, 1:2]), sig=False)
        m_qs = OP("dv", lambda E: E.tensor_scalar(out=scal[:, 3:4], in0=scal[:, 2:3], scalar1=127.0, scalar2=None, op0=ALU.mult))
        OP("dv", lambda E, oc=oc: E.tensor_scalar(out=stg_s[:, oc:oc + 1], in0=scal[:, 1:2], scalar1=1.0 / 127.0, scalar2=None, op0=ALU.mult), sig=False)
        OP("dv", lambda E, b=b: E.tensor_scalar(out=sq_scrF[:], in0=stg_w[b][:], scalar1=scal[:, 3:4], scalar2=MAGIC, op0=ALU.mult, op1=ALU.add), wait=[m_qs], sig=False)
        m_qq = OP("dv", lambda E: E.tensor_scalar(out=stg_q[:], in0=sq_scrF[:], scalar1=MAGIC, scalar2=None, op0=ALU.subtract))
        m_qread[b] = m_qq
        wv = [m_qq, m_consts] + BK["trb"].pre_write()
        for fc in range(8):
            m_tr = s.emit("pe", lambda E, fc=fc: E.transpose(ps_trb[:, fc * 128:(fc + 1) * 128], stg_q[:, fc * 128:(fc + 1) * 128], id_bf[:]),
                          wait=(wv if fc == 0 else ()), sig="pe" if fc == 7 else None)
        BK["trb"].post_write(m_tr)
        dst = wffn_sb[:].rearrange("p (f o) -> p f o", f=8)[:, :, oc * 128:(oc + 1) * 128]
        m_wffn_ev = OP("dv", lambda E, dst=dst: E.tensor_copy(out=dst, in_=ps_trb[:].rearrange("p (f t) -> p f t", f=8)), wait=[m_tr])
        BK["trb"].read(m_wffn_ev)
    m_wst = s.emit("pe", lambda E: E.transpose(ps_trf[0:8, 0:128], stg_s[:, 0:8], id_f[:]), wait=[m_wffn_ev] + BK["trf"].pre_write(), sig="pe")
    BK["trf"].post_write(m_wst)
    m_wse = OP("dv", lambda E: E.tensor_copy(out=stg_wt[:], in_=ps_trf[0:8, 0:128]), wait=[m_wst])
    BK["trf"].read(m_wse)
    m_brws = None
    for ocb in range(8):
        bn = BK[f"mm{ocb % 2}"]
        mmm = s.emit("pe", lambda E, ocb=ocb: E.matmul(ps_mm[ocb % 2][:, 0:128], uF[0][0:8, ocb * 128:(ocb + 1) * 128], stg_wt[:], start=True, stop=True),
                     wait=[m_wse, m_oh8] + bn.pre_write(), sig="pe")
        bn.post_write(mmm)
        m_brws = OP("dv", lambda E, ocb=ocb: E.tensor_copy(out=br_ws[:, ocb * 128:(ocb + 1) * 128], in_=ps_mm[ocb % 2][:, 0:128]), wait=[mmm])
        bn.read(m_brws)

    s.phase = 1

    M = {
        "uAfree": [None, None], "uFfree": [None, None],
        "win8read": [None, None], "winread": [None, None],
        "win8load": [None, None],
        "scan_x": [None] * 4, "scan_z": [None] * 4,
        "t2tok": [None, None],
        "patch": None, "patch_zero": None,
        "outst": None, "pcarry": None,
        "lhl_free": [None, None], "smT_free": [None, None], "toks_free": [None, None],
        "BC_free": [None, None], "BCT_free": [None, None],
        "bcdl_free": [None, None], "dl_free": [None, None], "stats_free": [None, None],
        "mixfree": None, "x2free": None, "ffnfree": None,
        "ygTfree": None, "xqTfree": None,
        "ydfree": [None, None], "ygfree": [None] * 4,
        "tailq": collections.deque(),
        "E_free": [None] * 4, "expE_free": [None] * 4, "Mt_free": [None] * 4,
        "edec2_free": [None, None], "Cdec2_free": [None, None], "Bw_free": [None] * 4,
        "Gmask_free": None, "C2_free": None,
        "sinlast": None, "coslast": None,
        "BCT_m": [[None] * 4, [None] * 4],
        "lhl_m": [[None] * 4, [None] * 4],
        "xact_m": [None] * 4, "siluz_m": [None] * 4,
        "phiscan": None,
        "xinT_m": [None] * 4,
        "theta_m": [None] * 4,
        "t2T_m": [None] * 4,
        "dlm": [None] * 4,
        "bcdl_m": [None] * 4,
        "t2T_free": [None] * 2,
        "xinT_free": [None] * 4,
    }
    STATS = {}

    def stat_cols(p, k):
        b = stats[p]
        return {n: b[:, k * 16 + j: k * 16 + j + 1] for j, n in enumerate(
            ["mu1", "rstd1", "sa", "sbv", "irstd2", "t0", "t1", "t2", "t3", "t4", "t5"])}

    def emit_stats(i, k, usrc, m_uload):
        p = i % 2
        c = stat_cols(p, k)
        wfree = [M["stats_free"][p]] if k == 0 else []
        m1 = OP("dv", lambda E: E.tensor_reduce(out=c["t0"], in_=usrc, axis=AX.X, op=ALU.add), wait=[m_uload] + wfree)
        m2 = OP("ac", lambda E: E.activation(out=sq_scr[:], in_=usrc, func=AF.Square, accum_out=c["t1"]), wait=[m_uload] + wfree)
        m_mu = OP("dv", lambda E: E.tensor_scalar(out=c["mu1"], in0=c["t0"], scalar1=1.0 / DM, scalar2=None, op0=ALU.mult), wait=[m1])
        OP("dv", lambda E: E.tensor_scalar(out=c["t2"], in0=c["t1"], scalar1=1.0 / DM, scalar2=None, op0=ALU.mult), wait=[m2], sig=False)
        OP("dv", lambda E: E.tensor_tensor(out=c["t3"], in0=c["mu1"], in1=c["mu1"], op=ALU.mult), sig=False)
        OP("dv", lambda E: E.tensor_tensor(out=c["t4"], in0=c["t2"], in1=c["t3"], op=ALU.subtract), sig=False)
        m2b = OP("dv", lambda E: E.tensor_scalar(out=c["t5"], in0=c["t4"], scalar1=1.0, scalar2=1e-5, op0=ALU.mult, op1=ALU.add))
        m3l = OP("ac", lambda E: E.activation(out=c["t2"], in_=c["t5"], func=AF.Ln), wait=[m2b])
        m4 = OP("ac", lambda E: E.activation(out=c["rstd1"], in_=c["t2"], func=AF.Exp, scale=-0.5), wait=[m3l])
        OP("dv", lambda E: E.tensor_tensor(out=c["t3"], in0=c["rstd1"], in1=c["rstd1"], op=ALU.mult), wait=[m4], sig=False)
        OP("dv", lambda E: E.tensor_tensor(out=c["t5"], in0=c["t4"], in1=c["t3"], op=ALU.mult), sig=False)
        m5 = OP("dv", lambda E: E.tensor_scalar(out=c["t5"], in0=c["t5"], scalar1=1.0, scalar2=1e-6, op0=ALU.mult, op1=ALU.add))
        m5l = OP("ac", lambda E: E.activation(out=c["t2"], in_=c["t5"], func=AF.Ln), wait=[m5])
        m6 = OP("ac", lambda E: E.activation(out=c["irstd2"], in_=c["t2"], func=AF.Exp, scale=0.5))
        m8 = OP("ac", lambda E: E.activation(out=c["t3"], in_=c["t2"], func=AF.Exp, scale=-0.5))
        OP("dv", lambda E: E.tensor_tensor(out=c["sa"], in0=c["rstd1"], in1=c["t3"], op=ALU.mult), wait=[m8], sig=False)
        m9 = OP("dv", lambda E: E.tensor_tensor(out=c["sbv"], in0=c["sa"], in1=c["irstd2"], op=ALU.mult))
        STATS[(i, k)] = (c, m9, m6, m_mu)

    def emit_t2_xinT(i, k, usrc):
        c, m_sa, _, m_mu = STATS[(i, k)]
        tb = k % 2
        m_t2 = OP("dv", lambda E, tb=tb: E.tensor_scalar(out=t2_tok[tb][:], in0=usrc, scalar1=c["mu1"], scalar2=None, op0=ALU.subtract),
                  wait=[m_mu, M["t2tok"][tb]])
        wv = [m_t2] + BK["trb"].pre_write()
        for fc in range(8):
            m_tr = s.emit("pe", lambda E, fc=fc, tb=tb: E.transpose(ps_trb[:, fc * 128:(fc + 1) * 128], t2_tok[tb][:, fc * 128:(fc + 1) * 128], id_bf[:]),
                          wait=(wv if fc == 0 else ()), sig="pe" if fc == 7 else None)
        BK["trb"].post_write(m_tr)
        M["t2tok"][tb] = m_tr
        m_ev1 = OP("ac", lambda E, k=k: E.activation(out=t2T[k % 2][:], in_=ps_trb[:], func=AF.Copy), wait=[m_tr, M["t2T_free"][k % 2]])
        m_ev2 = OP("dv", lambda E, k=k: E.tensor_tensor(out=xinT[k][:], in0=ps_trb[:], in1=scale_big[:], op=ALU.mult), wait=[m_tr, m_ev1, M["xinT_free"][k]])
        BK["trb"].read(m_ev1)
        BK["trb"].read(m_ev2)
        M["xinT_m"][k] = m_ev2
        M["t2T_m"][k] = m_ev1

    def emit_m8(i, k):
        p = i % 2
        c, m_sa = STATS[(i, k)][0], STATS[(i, k)][1]
        bn = BK["mm0"]
        wv = [M["xinT_m"][k], m_win8] + bn.pre_write()
        for fc in range(8):
            m_mm = s.emit("pe", lambda E, fc=fc, k=k, p=p: E.matmul(
                ps_mm[0][:, 0:160], xinT[k][:, fc * 128:(fc + 1) * 128],
                win8[0][:, fc * 160:(fc + 1) * 160], start=(fc == 0), stop=False),
                wait=(wv if fc == 0 else ()), sig="pe" if fc == 7 else None)
        m_mm = s.emit("pe", lambda E: E.matmul(ps_mm[0][:, 0:160], ones1b[0:1, 0:128], bw8_st[0:1, 0:160], start=False, stop=True),
                      wait=[m_bw8st], sig="pe")
        bn.post_write(m_mm)
        wdst = [m_mm, m_sa] + ([M["bcdl_free"][p]] if k == 0 else [])
        m_ev = OP("dv", lambda E, k=k, p=p, c=c: E.tensor_scalar(out=bcdl[p][:, k * 160:(k + 1) * 160], in0=ps_mm[0][:, 0:160], scalar1=c["sa"], scalar2=None, op0=ALU.mult), wait=wdst)
        bn.read(m_ev)
        M["bcdl_m"][k] = m_ev
        dtr = bcdl[p][:, k * 160 + 128: k * 160 + 144]
        lmr = bcdl[p][:, k * 160 + 144: k * 160 + 160]
        wdl = [M["dl_free"][p]] if k == 0 else []
        m_a = OP("dv", lambda E, dtr=dtr: E.tensor_tensor(out=sm16[:, 0:16], in0=dtr, in1=br_dtb[:], op=ALU.add), wait=[m_ev, M["sinlast"]] + wdl)
        m_b = OP("ac", lambda E: E.activation(out=sm16[:, 16:32], in_=sm16[:, 0:16], func=AF.Abs), wait=[m_a])
        m_c2 = OP("ac", lambda E: E.activation(out=sm16[:, 16:32], in_=sm16[:, 16:32], func=AF.Exp, scale=-1.0))
        m_d = OP("ac", lambda E: E.activation(out=sm16[:, 16:32], in_=sm16[:, 16:32], func=AF.Ln, bias=1.0))
        m_e = OP("ac", lambda E: E.activation(out=sm16[:, 0:16], in_=sm16[:, 0:16], func=AF.Relu))
        m_dt = OP("dv", lambda E, k=k, p=p: E.tensor_tensor(out=dl_sb[p][:, k * 32:k * 32 + 16], in0=sm16[:, 0:16], in1=sm16[:, 16:32], op=ALU.add), wait=[m_d, m_e])
        m_l1_ = OP("ac", lambda E, lmr=lmr: E.activation(out=sm16[:, 32:48], in_=lmr, func=AF.Exp, scale=-1.0), wait=[m_ev, M["coslast"]])
        m_l2_ = OP("dv", lambda E: E.tensor_scalar(out=sm16[:, 32:48], in0=sm16[:, 32:48], scalar1=1.0, scalar2=None, op0=ALU.add), wait=[m_l1_])
        m_lm = OP("dv", lambda E, k=k, p=p: E.reciprocal(out=dl_sb[p][:, k * 32 + 16:k * 32 + 32], in_=sm16[:, 32:48]))
        M["dlm"][k] = (m_dt, m_lm)

    def emit_patch(i):
        p = i % 2
        m_dt, m_lm = M["dlm"][0]
        dtv = dl_sb[p][:, 0:16]
        lmv = dl_sb[p][:, 16:32]
        m1 = OP("dv", lambda E: E.tensor_tensor(out=sm16[:, 32:48], in0=dtv, in1=lmv, op=ALU.mult), wait=[m_dt, m_lm, M["coslast"]])
        m2 = OP("dv", lambda E: E.tensor_tensor(out=sm16[:, 48:64], in0=dtv, in1=sm16[:, 32:48], op=ALU.subtract))
        wv = [m2] + BK["trf"].pre_write()
        m3 = s.emit("pe", lambda E: E.transpose(ps_trf[0:16, 0:128], sm16[:, 48:64], id_f[:]), wait=wv, sig="pe")
        BK["trf"].post_write(m3)
        m4 = OP("dv", lambda E: E.tensor_copy(out=patchcol[:], in_=ps_trf[0:16, 0:1]), wait=[m3])
        BK["trf"].read(m4)
        M["patch"] = m4

    def emit_A1(i):
        p = i % 2
        wv = [M["uAfree"][0]]
        if i == 0:
            wv += m_winb + [m_consts, m_thw, m_wout, m_acol, m_initpo, m_brws, m_oh8, m_win8]
        m_u0 = DMA("dU0", uA[0][:], d_u[i * T:i * T + 128, :], wait=wv)
        emit_stats(i, 0, uA[0][:], m_u0)
        emit_t2_xinT(i, 0, uA[0][:])
        M["uAfree"][0] = [s.mark("dv"), s.mark("ac")]
        emit_m8(i, 0)
        emit_patch(i)

    def emit_theta(i, k):
        bn = BK["mm1"]
        c, m_sa = STATS[(i, k)][0], STATS[(i, k)][1]
        wv = [M["t2T_m"][k]] + bn.pre_write()
        for fc in range(8):
            m_mm = s.emit("pe", lambda E, fc=fc, k=k: E.matmul(
                ps_mm[1][:, 0:32], t2T[k % 2][:, fc * 128:(fc + 1) * 128],
                thw_sb[:, fc * 32:(fc + 1) * 32], start=(fc == 0), stop=(fc == 7)),
                wait=(wv if fc == 0 else ()), sig="pe" if fc == 7 else None)
        bn.post_write(m_mm)
        M["t2T_free"][k % 2] = m_mm
        m_ev = OP("dv", lambda E, k=k, c=c: E.tensor_scalar(
            out=theta_tok[:, k * 32:(k + 1) * 32], in0=ps_mm[1][:, 0:32],
            scalar1=c["sbv"], scalar2=None, op0=ALU.mult), wait=[m_mm, m_sa])
        bn.read(m_ev)
        M["theta_m"][k] = m_ev

    def trf_transpose(out_ap, src_ap, kdim, wait):
        wv = list(wait) + BK["trf"].pre_write()
        b0 = src_ap.base_partition() if kdim != 128 else 0
        ident = id_f[:] if kdim == 128 else id_f[b0:b0 + kdim, b0:b0 + kdim]
        m = s.emit("pe", lambda E: E.transpose(out_ap, src_ap, ident), wait=wv, sig="pe")
        BK["trf"].post_write(m)
        return m

    def emit_smalls_T(i, k):
        p = i % 2
        m_dt, m_lm = M["dlm"][k]
        m_theta = M["theta_m"][k]
        m_s1 = OP("dv", lambda E, p=p, k=k: E.tensor_reduce(out=scal[:, 32:33], in_=dl_sb[p][:, k * 32:k * 32 + 16], axis=AX.X, op=ALU.add), wait=[m_dt])
        m_s2 = OP("dv", lambda E: E.tensor_scalar(out=scal[:, 33:34], in0=scal[:, 32:33], scalar1=1.0 / NH, scalar2=None, op0=ALU.mult))
        m_tm = OP("dv", lambda E, k=k: E.tensor_scalar(out=sm16[:, 0:32], in0=theta_tok[:, k * 32:(k + 1) * 32], scalar1=scal[:, 33:34], scalar2=None, op0=ALU.mult), wait=[m_s2, m_theta, M["sinlast"]])
        wfree = [M["smT_free"][p]] if k == 0 else []
        m_t1 = trf_transpose(ps_trf[0:16, 0:128], dl_sb[p][:, k * 32:k * 32 + 16], 128, [m_dt])
        m_e1 = OP("dv", lambda E, p=p, k=k: E.tensor_copy(out=dtT_sb[p][:, k * 128:(k + 1) * 128], in_=ps_trf[0:16, 0:128]), wait=[m_t1] + wfree)
        BK["trf"].read(m_e1)
        m_t2_ = trf_transpose(ps_trf[0:16, 128:256], dl_sb[p][:, k * 32 + 16:k * 32 + 32], 128, [m_lm])
        m_e2 = OP("dv", lambda E, p=p, k=k: E.tensor_copy(out=lamT_sb[p][:, k * 128:(k + 1) * 128], in_=ps_trf[0:16, 128:256]), wait=[m_t2_])
        BK["trf"].read(m_e2)
        m_t3 = trf_transpose(ps_trf[0:32, 256:384], sm16[:, 0:32], 128, [m_tm])
        m_e3 = OP("dv", lambda E, p=p, k=k: E.tensor_copy(out=termsT_sb[:, k * 128:(k + 1) * 128], in_=ps_trf[0:32, 256:384]), wait=[m_t3])
        BK["trf"].read(m_e3)
        m_sc = OP("dv", lambda E, p=p, k=k: E.tensor_tensor_scan(
            out=lamt_sb[:, k * 128:(k + 1) * 128], data0=ones32[0:16, :], data1=dtT_sb[p][:, k * 128:(k + 1) * 128],
            initial=0.0, op0=ALU.mult, op1=ALU.add), wait=[m_e1])
        m_lam = OP("dv", lambda E, k=k: E.tensor_scalar(out=lamt_sb[:, k * 128:(k + 1) * 128], in0=lamt_sb[:, k * 128:(k + 1) * 128], scalar1=acol[:], scalar2=None, op0=ALU.mult), wait=[m_sc])
        m_lr = OP("dv", lambda E, k=k: E.tensor_scalar(
            out=lamrev_sb[:, k * 128:(k + 1) * 128], in0=lamt_sb[:, k * 128:(k + 1) * 128],
            scalar1=lamt_sb[:, k * 128 + 127:k * 128 + 128], scalar2=-1.0, op0=ALU.subtract, op1=ALU.mult), wait=[m_lam])
        wl = [M["lhl_free"][p]] if k == 0 else []
        m_hi = OP("dv", lambda E, p=p, k=k: E.tensor_copy(out=lhl[p][0:16, k * 128:(k + 1) * 128], in_=lamt_sb[:, k * 128:(k + 1) * 128]), wait=wl + [m_lam])
        m_lo = OP("dv", lambda E, p=p, k=k: E.tensor_tensor(out=lhl[p][32:48, k * 128:(k + 1) * 128], in0=lamt_sb[:, k * 128:(k + 1) * 128], in1=lhl[p][0:16, k * 128:(k + 1) * 128], op=ALU.subtract))
        M["lhl_m"][p][k] = m_lo
        wt = [M["toks_free"][p]] if k == 0 else []
        m_t4 = trf_transpose(ps_trf[:, 0:16], lamt_sb[:, k * 128:(k + 1) * 128], 16, [m_lam])
        m_e4 = OP("dv", lambda E, p=p, k=k: E.tensor_copy(out=toks[p][:, k * 48:k * 48 + 16], in_=ps_trf[:, 0:16]), wait=[m_t4] + wt)
        BK["trf"].read(m_e4)
        m_t5 = trf_transpose(ps_trf[:, 16:32], lamrev_sb[:, k * 128:(k + 1) * 128], 16, [m_lr])
        m_e5 = OP("dv", lambda E, p=p, k=k: E.tensor_copy(out=toks[p][:, k * 48 + 16:k * 48 + 32], in_=ps_trf[:, 16:32]), wait=[m_t5])
        BK["trf"].read(m_e5)

    def emit_phi_cs(i, k):
        p = i % 2
        init = phicarry[:] if k == 0 else phiT[:, k * 128 - 1:k * 128]
        wv = [s.mark("dv")]
        if k == 0 and M["pcarry"]:
            wv.append(M["pcarry"])
        if k > 0:
            wv.append(M["phiscan"])
        m_sc = OP("dv", lambda E, k=k, init=init: E.tensor_tensor_scan(
            out=phiT[:, k * 128:(k + 1) * 128], data0=ones32[:], data1=termsT_sb[:, k * 128:(k + 1) * 128],
            initial=init, op0=ALU.mult, op1=ALU.add), wait=wv)
        M["phiscan"] = m_sc
        if k == NCH - 1:
            M["pcarry"] = OP("dv", lambda E: E.tensor_copy(out=phicarry[:], in_=phiT[:, T - 1:T]), wait=[m_sc])
        m_tr = trf_transpose(ps_trf[:, 0:32], phiT[0:32, k * 128:(k + 1) * 128], 32, [m_sc])
        m_ev = OP("dv", lambda E, k=k: E.tensor_copy(out=phitok[:, k * 32:(k + 1) * 32], in_=ps_trf[:, 0:32]), wait=[m_tr])
        BK["trf"].read(m_ev)
        ph = phitok[:, k * 32:(k + 1) * 32]
        cb = cs_tile[:, k * 96:k * 96 + 96]
        OP("dv", lambda E, ph=ph: E.tensor_scalar(out=sm16[:, 0:32], in0=ph, scalar1=INV2PI, scalar2=None, op0=ALU.mult), wait=[m_ev, M["sinlast"]], sig=False)
        OP("dv", lambda E: E.tensor_scalar(out=sm16[:, 0:32], in0=sm16[:, 0:32], scalar1=MAGIC, scalar2=MAGIC, op0=ALU.add, op1=ALU.subtract), sig=False)
        OP("dv", lambda E, ph=ph: E.scalar_tensor_tensor(out=sm16[:, 0:32], in0=sm16[:, 0:32], scalar=-2.0 * PI, in1=ph, op0=ALU.mult, op1=ALU.add), sig=False)
        m_sr = OP("dv", lambda E: E.tensor_scalar(out=sm16[:, 0:32], in0=sm16[:, 0:32], scalar1=PI, scalar2=-PI, op0=ALU.min, op1=ALU.max))
        m_sin = OP("ac", lambda E, cb=cb: E.activation(out=cb[:, 32:64], in_=sm16[:, 0:32], func=AF.Sin), wait=[m_sr])
        M["sinlast"] = m_sin
        OP("dv", lambda E, ph=ph: E.tensor_scalar(out=sm16[:, 32:64], in0=ph, scalar1=INV2PI, scalar2=0.25, op0=ALU.mult, op1=ALU.add), wait=[M["coslast"]], sig=False)
        OP("dv", lambda E: E.tensor_scalar(out=sm16[:, 64:96], in0=sm16[:, 32:64], scalar1=MAGIC, scalar2=MAGIC, op0=ALU.add, op1=ALU.subtract), sig=False)
        OP("dv", lambda E: E.tensor_tensor(out=sm16[:, 64:96], in0=sm16[:, 32:64], in1=sm16[:, 64:96], op=ALU.subtract), sig=False)
        OP("dv", lambda E: E.tensor_scalar(out=sm16[:, 64:96], in0=sm16[:, 64:96], scalar1=2.0 * PI, scalar2=PI, op0=ALU.mult, op1=ALU.min), sig=False)
        m_cr = OP("dv", lambda E: E.tensor_scalar(out=sm16[:, 64:96], in0=sm16[:, 64:96], scalar1=-PI, scalar2=None, op0=ALU.max))
        m_cos = OP("ac", lambda E, cb=cb: E.activation(out=cb[:, 0:32], in_=sm16[:, 64:96], func=AF.Sin), wait=[m_cr])
        M["coslast"] = m_cos
        m_ns = OP("dv", lambda E, cb=cb: E.tensor_scalar(out=cb[:, 64:96], in0=cb[:, 32:64], scalar1=-1.0, scalar2=None, op0=ALU.mult), wait=[m_sin])
        return (m_sin, m_cos, m_ns)

    def _rep4(off, w):
        ap = cs_tile[:, off:off + w]
        return bass.AP(ap.tensor, ap.offset, [list(ap.ap[0]), [0, 4], [1, w]])

    def _swapview():
        ap = bctmp[:, 0:128]
        return bass.AP(ap.tensor, ap.offset + 32, [list(ap.ap[0]), [64, 2], [-32, 2], [1, 32]])

    def _negssview(off):
        ap = cs_tile[:, 0:128]
        return bass.AP(ap.tensor, ap.offset + off + 64, [list(ap.ap[0]), [0, 2], [-32, 2], [1, 32]])

    def emit_BC(i, k, m_cs):
        p = i % 2
        m_bcdl = M["bcdl_m"][k]
        Braw = bcdl[p][:, k * 160:k * 160 + 64]
        Craw = bcdl[p][:, k * 160 + 64:k * 160 + 128]
        OP("dv", lambda E, Braw=Braw: E.tensor_tensor(out=dgtmp[:], in0=Braw, in1=Braw, op=ALU.mult), wait=[m_bcdl], sig=False)
        OP("dv", lambda E: E.tensor_reduce(out=scal[:, 34:35], in_=dgtmp[:], axis=AX.X, op=ALU.add), sig=False)
        OP("dv", lambda E, Craw=Craw: E.tensor_tensor(out=dgtmp[:], in0=Craw, in1=Craw, op=ALU.mult), sig=False)
        OP("dv", lambda E: E.tensor_reduce(out=scal[:, 35:36], in_=dgtmp[:], axis=AX.X, op=ALU.add), sig=False)
        OP("dv", lambda E: E.tensor_scalar(out=scal[:, 36:37], in0=scal[:, 34:35], scalar1=1.0 / NS, scalar2=1e-6, op0=ALU.mult, op1=ALU.add), sig=False)
        m_r1a = OP("ac", lambda E: E.activation(out=scal[:, 36:37], in_=scal[:, 36:37], func=AF.Ln), wait=[m_r1w := s.mark("dv")])
        m_r2 = OP("ac", lambda E: E.activation(out=scal[:, 38:39], in_=scal[:, 36:37], func=AF.Exp, scale=-0.5))
        m_r3w = OP("dv", lambda E: E.tensor_scalar(out=scal[:, 37:38], in0=scal[:, 35:36], scalar1=1.0 / NS, scalar2=1e-6, op0=ALU.mult, op1=ALU.add))
        m_r3a = OP("ac", lambda E: E.activation(out=scal[:, 37:38], in_=scal[:, 37:38], func=AF.Ln), wait=[m_r3w])
        m_r4 = OP("ac", lambda E: E.activation(out=scal[:, 39:40], in_=scal[:, 37:38], func=AF.Exp, scale=-0.5))
        OP("dv", lambda E, Braw=Braw: E.tensor_scalar(out=bctmp[:, 0:64], in0=Braw, scalar1=scal[:, 38:39], scalar2=None, op0=ALU.mult), wait=[m_r2], sig=False)
        OP("dv", lambda E, Craw=Craw: E.tensor_scalar(out=bctmp[:, 64:128], in0=Craw, scalar1=scal[:, 39:40], scalar2=None, op0=ALU.mult), wait=[m_r4], sig=False)
        OP("dv", lambda E: E.tensor_tensor(out=ropeA[:], in0=bctmp[:], in1=br_bc[:], op=ALU.mult), sig=False)
        m_n4 = OP("dv", lambda E: E.tensor_tensor(out=bctmp[:], in0=ropeA[:], in1=br_bcb[:], op=ALU.add))
        m_T1 = OP("dv", lambda E, k=k: E.tensor_tensor(out=ropeA[:], in0=bctmp[:], in1=_rep4(k * 96, 32), op=ALU.mult), wait=[m_n4] + list(m_cs))
        m_T2 = OP("dv", lambda E, k=k: E.tensor_tensor(out=ropeB[:], in0=_swapview(), in1=_negssview(k * 96), op=ALU.mult))
        wbc = [M["BC_free"][p]] if k == 0 else []
        m_bc = OP("dv", lambda E, p=p, k=k: E.tensor_tensor(out=BCbf[p][:, k * 128:(k + 1) * 128], in0=ropeA[:], in1=ropeB[:], op=ALU.add), wait=wbc)
        wv = [m_bc] + BK["trb"].pre_write()
        m_tr = s.emit("pe", lambda E, p=p, k=k: E.transpose(ps_trb[:, 0:128], BCbf[p][:, k * 128:(k + 1) * 128], id_bf[:]), wait=wv, sig="pe")
        BK["trb"].post_write(m_tr)
        wct = [M["BCT_free"][p]] if k == 0 else []
        m_ev = OP("dv", lambda E, p=p, k=k: E.tensor_copy(out=BCT[p][:, k * 128:(k + 1) * 128], in_=ps_trb[:, 0:128]), wait=[m_tr] + wct)
        BK["trb"].read(m_ev)
        M["BCT_m"][p][k] = m_ev

    def emit_inproj(i):
        p = i % 2
        for mj in range(8):
            b = mj % 2
            m_wl = DMA("dW0" if b == 0 else "dW1", win_buf[b][:, 0:4096].rearrange("p (c m) -> p c m", c=8), d_winb[:, mj * 512:(mj + 1) * 512].rearrange("(c p) m -> p c m", p=128),
                       wait=[M["winread"][b]] + (m_winb if i == 0 else []))
            m_wb = DMA("dG0" if b == 0 else "dG1", win_buf[b][0:1, 4096:4608], d_bw[0:1, mj * 512:(mj + 1) * 512],
                       wait=[M["winread"][b]] + ([m_bw8st] if i == 0 else []))
            last_mm = None
            for k in range(NCH):
                cst, m_sa = STATS[(i, k)][0], STATS[(i, k)][1]
                bn = BK[f"mm{k % 2}"]
                wv = [m_wl, M["xinT_m"][k]] + bn.pre_write()
                for fc in range(8):
                    m_mmf = s.emit("pe", lambda E, k=k, fc=fc, b=b: E.matmul(
                        ps_mm[k % 2][:, 0:512], xinT[k][:, fc * 128:(fc + 1) * 128],
                        win_buf[b][:, fc * 512:(fc + 1) * 512], start=(fc == 0), stop=False),
                        wait=(wv if fc == 0 else ()), sig="pe" if fc == 7 else None)
                m_mm = s.emit("pe", lambda E, k=k, b=b: E.matmul(
                    ps_mm[k % 2][:, 0:512], ones1b[0:1, 0:128], win_buf[b][0:1, 4096:4608], start=False, stop=True),
                    wait=[m_wb] + ([m_initpo] if i == 0 and mj == 0 and k == 0 else []), sig="pe")
                bn.post_write(m_mm)
                last_mm = m_mm
                if mj == 7:
                    M["xinT_free"][k] = m_mmf
                if mj < 4:
                    dst = x_act[:, k * 2048 + mj * 512: k * 2048 + (mj + 1) * 512]
                    gate = M["scan_x"][k] or []
                else:
                    dst = siluz[:, k * 2048 + (mj - 4) * 512: k * 2048 + (mj - 3) * 512]
                    gate = M["scan_z"][k] or []
                m_ev = OP("ac", lambda E, dst=dst, k=k, cst=cst: E.activation(out=dst, in_=ps_mm[k % 2][:, 0:512], func=AF.Silu, scale=cst["sa"]),
                          wait=[m_mm, m_sa] + list(gate))
                bn.read(m_ev)
                if mj == 3:
                    M["xact_m"][k] = m_ev
                if mj == 7:
                    M["siluz_m"][k] = m_ev
            M["winread"][b] = last_mm

    def emit_A2(i):
        for k in range(1, NCH):
            m_u = DMA("dU1", uA[k % 2][:], d_u[i * T + k * 128:i * T + (k + 1) * 128, :], wait=[M["uAfree"][k % 2]])
            emit_stats(i, k, uA[k % 2][:], m_u)
            emit_t2_xinT(i, k, uA[k % 2][:])
            M["uAfree"][k % 2] = [s.mark("dv"), s.mark("ac")]
            emit_m8(i, k)
        emit_inproj(i)
        for k in range(NCH):
            emit_theta(i, k)
            emit_smalls_T(i, k)
            m_cs = emit_phi_cs(i, k)
            emit_BC(i, k, m_cs)
        p = i % 2
        M["bcdl_free"][p] = s.mark("dv")
        if i != 0:
            return
        SNAP("x_act", x_act[:], [s.mark("ac")])
        SNAP("siluz", siluz[:], [s.mark("ac")])
        SNAP("bcdl", bcdl[p][:], [s.mark("dv")])
        SNAP("dl", dl_sb[p][:], [s.mark("dv"), s.mark("ac")])
        SNAP("theta", theta_tok[:], [s.mark("dv")])
        SNAP("phitok", phitok[:], [s.mark("dv")])
        SNAP("cs", cs_tile[:], [s.mark("dv"), s.mark("ac")])
        SNAP("BCbf", BCbf[p][:], [s.mark("dv")])
        SNAP("lampk", lamt_sb[:], [s.mark("dv")])
        SNAP("toks_a", toks[p][:], [s.mark("dv")])
        SNAP("xinT0", xinT[0][:], [s.mark("dv")])
        SNAP("uA0", uA[0][:], [s.mark("dv")])
        SNAP("t2T0", t2T[0][:], [s.mark("ac")])
        SNAP("stats0", stats[0][:], [s.mark("dv"), s.mark("ac")])
        SNAP("scaleb", scale_big[:], [s.mark("dv")])
        SNAP("idbf_t", id_bf[:], [s.mark("dv")])
        SNAP("winb0", win_buf[0][:], [s.mark("pe")])

    def emit_F(j, is_last):
        q = j % 2
        m_w1 = OP("dv", lambda E, q=q: E.tensor_tensor(out=wT_sb[:], in0=dtT_sb[q][:], in1=lamT_sb[q][:], op=ALU.mult))
        OP("dv", lambda E, q=q: E.tensor_tensor(out=dtlT_sb[:], in0=dtT_sb[q][:], in1=wT_sb[:], op=ALU.subtract), sig=False)
        m_r1 = OP("dv", lambda E: E.tensor_tensor(out=rT_sb[:, 0:T - 1], in0=wT_sb[:, 0:T - 1], in1=dtlT_sb[:, 1:T], op=ALU.add))
        wpatch = [M["patch_zero"] if is_last else M["patch"]]
        m_r2 = OP("dv", lambda E: E.tensor_tensor(out=rT_sb[:, T - 1:T], in0=wT_sb[:, T - 1:T], in1=patchcol[:], op=ALU.add), wait=wpatch)
        m_rcl = OP("dv", lambda E: E.tensor_scalar(out=rT_sb[:], in0=rT_sb[:], scalar1=1e-38, scalar2=None, op0=ALU.max), wait=[m_r1, m_r2])
        m_ln = OP("ac", lambda E: E.activation(out=lnrT_sb[:], in_=rT_sb[:], func=AF.Ln), wait=[m_rcl])
        if j == 0:
            SNAP("fin1", wT_sb[:], [s.mark("dv")])
            SNAP("fin2", lnrT_sb[:], [s.mark("ac"), s.mark("dv")])

        for c in range(NCH):
            m_t = trf_transpose(ps_trf[:, 0:16], lnrT_sb[:, c * 128:(c + 1) * 128], 16, [m_ln])
            m_lt = OP("dv", lambda E, q=q, c=c: E.tensor_copy(out=toks[q][:, c * 48 + 32:c * 48 + 48], in_=ps_trf[:, 0:16]), wait=[m_t])
            BK["trf"].read(m_lt)
            m_st = OP("dv", lambda E, q=q, c=c: E.tensor_tensor(out=subtok[:, c * 16:(c + 1) * 16], in0=toks[q][:, c * 48:c * 48 + 16], in1=toks[q][:, c * 48 + 32:c * 48 + 48], op=ALU.subtract), wait=[m_lt])
            m_sd1 = OP("dv", lambda E, q=q, c=c: E.tensor_tensor(out=fsm[:], in0=toks[q][:, c * 48 + 16:c * 48 + 32], in1=toks[q][:, c * 48 + 32:c * 48 + 48], op=ALU.add))
            m_sd = OP("ac", lambda E, c=c: E.activation(out=sdec_sb[:, c * 16:(c + 1) * 16], in_=fsm[:], func=AF.Exp), wait=[m_sd1])
            m_wt = OP("dv", lambda E, q=q, c=c: E.tensor_tensor(out=wtok[:, c * 16:(c + 1) * 16], in0=dl_sb[q][:, c * 32:c * 32 + 16], in1=dl_sb[q][:, c * 32 + 16:c * 32 + 32], op=ALU.mult))
            OP("dv", lambda E, q=q, c=c: E.tensor_tensor(out=dgtmp[:], in0=BCbf[q][:, c * 128:c * 128 + 64], in1=BCbf[q][:, c * 128 + 64:c * 128 + 128], op=ALU.mult), sig=False)
            m_dg2 = OP("dv", lambda E: E.tensor_reduce(out=scal[:, 40:41], in_=dgtmp[:], axis=AX.X, op=ALU.add))
            m_dv = OP("dv", lambda E, c=c: E.scalar_tensor_tensor(out=dvec[:, c * 16:(c + 1) * 16], in0=wtok[:, c * 16:(c + 1) * 16], scalar=scal[:, 40:41], in1=br_D[:], op0=ALU.mult, op1=ALU.add), wait=[m_dg2, m_wt])
            m_c2a = OP("po", lambda E, q=q, c=c: E.tensor_copy(out=C2_sb[0:64, :], in_=BCT[q][64:128, c * 128:(c + 1) * 128]), wait=[M["C2_free"], M["BCT_m"][q][c]])
            m_c2b = OP("po", lambda E, q=q, c=c: E.tensor_copy(out=C2_sb[64:128, :], in_=BCT[q][64:128, c * 128:(c + 1) * 128]))
            wv = BK["trf"].pre_write()
            m_G = s.emit("pe", lambda E, q=q, c=c: E.matmul(ps_trf[:, 0:128], BCT[q][0:64, c * 128:(c + 1) * 128], C2_sb[0:64, :], start=True, stop=True), wait=wv + [m_c2a, M["BCT_m"][q][c]], sig="pe")
            BK["trf"].post_write(m_G)
            m_gm = OP("dv", lambda E: E.tensor_tensor(out=Gmask[:], in0=ps_trf[:, 0:128], in1=tri_sb[:], op=ALU.mult), wait=[m_G, M["Gmask_free"]])
            BK["trf"].read(m_gm)

            marks_yg = []
            PIPE = [None] * 8
            YGQ = [None] * 8

            def emit_ygtr(tp):
                g0, g1 = YGQ[tp]
                gx0, gx1 = (tp % 2) * 2, (tp % 2) * 2 + 1
                wv = ([] if tp % 4 else BK["trb"].pre_write())
                m_t0 = s.emit("pe", lambda E_, tp=tp, gx0=gx0: E_.transpose(ps_trb[:, (tp % 4) * 256:(tp % 4) * 256 + 128], yg_pool[gx0][:], id_bf[:]), wait=[g0] + wv, sig="pe")
                m_t1 = s.emit("pe", lambda E_, tp=tp, gx1=gx1: E_.transpose(ps_trb[:, (tp % 4) * 256 + 128:(tp % 4) * 256 + 256], yg_pool[gx1][:], id_bf[:]), wait=[g1], sig="pe")
                BK["trb"].post_write(m_t1)
                M["ygfree"][gx0] = m_t0
                M["ygfree"][gx1] = m_t1
                if tp in (3, 7):
                    half = 0 if tp == 3 else 1
                    wyt = [M["ygTfree"]] if tp == 3 else []
                    m_evy = OP("dv", lambda E_, c=c, half=half: E_.tensor_copy(out=ygT[:, half * 1024:(half + 1) * 1024], in_=ps_trb[:]), wait=[m_t1] + wyt)
                    BK["trb"].read(m_evy)
                    marks_yg.append(m_evy)

            def emit_burst(pr):
                bk = pr % 2
                bbn = BK[f"br{bk}"]
                h0, h1 = 2 * pr, 2 * pr + 1
                wb = bbn.pre_write()
                m_b0 = s.emit("pe", lambda E, q=q, c=c, h0=h0, bk=bk: E.matmul(ps_br[bk][:, 0:128], oh48_sb[:, h0 * 128:(h0 + 1) * 128], lhl[q][:, c * 128:(c + 1) * 128], start=True, stop=True), wait=wb + [M["lhl_m"][q][c]], sig="pe")
                m_b1 = s.emit("pe", lambda E, q=q, c=c, h1=h1, bk=bk: E.matmul(ps_br[bk][:, 128:256], oh48_sb[:, h1 * 128:(h1 + 1) * 128], lhl[q][:, c * 128:(c + 1) * 128], start=True, stop=True), sig="pe")
                m_b2 = s.emit("pe", lambda E, q=q, c=c, pr=pr, bk=bk: E.matmul(ps_br[bk][:, 256:384], pp48_sb[:, pr * 128:(pr + 1) * 128], lhl[q][:, c * 128:(c + 1) * 128], start=True, stop=True), sig="pe")
                bbn.post_write(m_b2)
                return dict(bk=bk, bbn=bbn, h0=h0, h1=h1, m_b0=m_b0, m_b1=m_b1, m_b2=m_b2)

            def emit_chain(pr, P):
                bk, bbn, h0, h1 = P["bk"], P["bbn"], P["h0"], P["h1"]
                m_Mt = [None, None]
                m_E = [None, None]
                for e_ix, (hh, m_bb) in enumerate([(h0, P["m_b0"]), (h1, P["m_b1"])]):
                    r_ix = (pr * 2 + e_ix) % 4
                    m_E[e_ix] = OP("ac", lambda E_, c=c, hh=hh, bk=bk, e_ix=e_ix, r_ix=r_ix: E_.activation(
                        out=nE_sb[r_ix][:], in_=ps_br[bk][:, e_ix * 128:(e_ix + 1) * 128], func=AF.Relu,
                        scale=-1.0, bias=subtok[:, c * 16 + hh:c * 16 + hh + 1]),
                        wait=[P["m_b2"], m_st, M["E_free"][r_ix]])
                    bbn.read(m_E[e_ix])
                    m_x = OP("ac", lambda E_, r_ix=r_ix: E_.activation(out=expE[r_ix][:], in_=nE_sb[r_ix][:], func=AF.Exp, scale=-1.0), wait=[M["expE_free"][r_ix]])
                    M["E_free"][r_ix] = m_x
                    m_Mt[e_ix] = OP("dv", lambda E_, e_ix=e_ix, r_ix=r_ix: E_.tensor_tensor(out=Mt_sb[r_ix][:], in0=expE[r_ix][:], in1=Gmask[:], op=ALU.mult), wait=[m_x, m_gm, M["Mt_free"][r_ix]])
                    M["expE_free"][r_ix] = m_Mt[e_ix]
                m_ed = OP("ac", lambda E_, bk=bk: E_.activation(out=edec2[bk][:], in_=ps_br[bk][:, 256:384], func=AF.Exp), wait=[P["m_b2"], m_E[0], m_E[1], M["edec2_free"][bk]])
                bbn.read(m_ed)
                m_cd = OP("po", lambda E_, bk=bk: E_.tensor_tensor(out=Cdec2[bk][:], in0=C2_sb[:], in1=edec2[bk][:], op=ALU.mult), wait=[m_ed, m_c2b, M["Cdec2_free"][bk]])
                m_Bw = [None, None]
                for e_ix, hh in enumerate([h0, h1]):
                    bw = bk * 2 + e_ix
                    m_Bw[e_ix] = OP("dv", lambda E_, q=q, c=c, hh=hh, bw=bw: E_.tensor_scalar(
                        out=Bw_sb[bw][:], in0=BCbf[q][:, c * 128:c * 128 + 64],
                        scalar1=sdec_sb[:, c * 16 + hh:c * 16 + hh + 1], scalar2=None, op0=ALU.mult), wait=[m_sd, M["Bw_free"][bw]])
                P.update(m_Mt=m_Mt, m_ed=m_ed, m_cd=m_cd, m_Bw=m_Bw)

            for pp in (0, 1):
                PIPE[pp] = emit_burst(pp)
                emit_chain(pp, PIPE[pp])

            for pr in range(8):
                P = PIPE[pr]
                bk = P["bk"]
                ybn = BK[f"Y{bk}"]
                h0, h1 = P["h0"], P["h1"]
                m_Mt, m_ed, m_cd, m_Bw = P["m_Mt"], P["m_ed"], P["m_cd"], P["m_Bw"]
                wy = ybn.pre_write()
                r0, r1 = (pr * 2) % 4, (pr * 2 + 1) % 4
                bw0, bw1 = bk * 2, bk * 2 + 1
                m_y0a = s.emit("pe", lambda E_, c=c, h0=h0, bk=bk, r0=r0: E_.matmul(ps_Y[bk][:, 0:128], Mt_sb[r0][:], x_act[:, c * 2048 + h0 * 128:c * 2048 + (h0 + 1) * 128], start=True, stop=False), wait=[m_Mt[0], M["xact_m"][c]] + wy, sig="pe")
                m_y0b = s.emit("pe", lambda E_, bk=bk, pr=pr: E_.matmul(ps_Y[bk][:, 0:128], Cdec2[bk][0:64, :], H2b[pr][0:64, :], start=False, stop=True), wait=[m_cd], sig="pe")
                m_y1a = s.emit("pe", lambda E_, c=c, h1=h1, bk=bk, r1=r1: E_.matmul(ps_Y[bk][:, 128:256], Mt_sb[r1][:], x_act[:, c * 2048 + h1 * 128:c * 2048 + (h1 + 1) * 128], start=True, stop=False), wait=[m_Mt[1]], sig="pe")
                m_y1b = s.emit("pe", lambda E_, bk=bk, pr=pr: E_.matmul(ps_Y[bk][:, 128:256], Cdec2[bk][64:128, :], H2b[pr][64:128, :], start=False, stop=True), sig="pe")
                m_s0 = s.emit("pe", lambda E_, c=c, h0=h0, bk=bk, bw0=bw0: E_.matmul(ps_Y[bk][0:64, 256:384], Bw_sb[bw0][:], x_act[:, c * 2048 + h0 * 128:c * 2048 + (h0 + 1) * 128], start=True, stop=True), wait=[m_Bw[0]], sig="pe")
                m_s1 = s.emit("pe", lambda E_, c=c, h1=h1, bk=bk, bw1=bw1: E_.matmul(ps_Y[bk][64:128, 256:384], Bw_sb[bw1][:], x_act[:, c * 2048 + h1 * 128:c * 2048 + (h1 + 1) * 128], start=True, stop=True), wait=[m_Bw[1]], sig="pe")
                ybn.post_write(m_s1)
                M["Mt_free"][r0] = m_y0a
                M["Mt_free"][r1] = m_y1a
                M["Bw_free"][bw0] = m_s0
                M["Bw_free"][bw1] = m_s1
                M["Cdec2_free"][bk] = m_y1b
                m_yd = [None, None]
                for e_ix, (hh, m_yb) in enumerate([(h0, m_y0b), (h1, m_y1b)]):
                    m_yd[e_ix] = OP("dv", lambda E_, c=c, hh=hh, bk=bk, e_ix=e_ix: E_.scalar_tensor_tensor(
                        out=yd_sb[e_ix][:], in0=x_act[:, c * 2048 + hh * 128:c * 2048 + (hh + 1) * 128],
                        scalar=dvec[:, c * 16 + hh:c * 16 + hh + 1], in1=ps_Y[bk][:, e_ix * 128:(e_ix + 1) * 128],
                        op0=ALU.mult, op1=ALU.add), wait=[m_s1, m_dv, M["ydfree"][e_ix]])
                    ybn.read(m_yd[e_ix])
                m_h = OP("dv", lambda E_, pr=pr, bk=bk: E_.scalar_tensor_tensor(
                    out=H2f[pr][:], in0=H2f[pr][:], scalar=edec2[bk][:, 127:128],
                    in1=ps_Y[bk][:, 256:384], op0=ALU.mult, op1=ALU.add), wait=[m_s0, m_s1, m_ed])
                ybn.read(m_h)
                M["edec2_free"][bk] = m_h
                OP("po", lambda E_, pr=pr: E_.tensor_copy(out=H2b[pr][:], in_=H2f[pr][:]), wait=[m_h, m_y0b, m_y1b], sig=False)
                m_yg = [None, None]
                for e_ix, hh in enumerate([h0, h1]):
                    gx = (pr % 2) * 2 + e_ix
                    m_yg[e_ix] = OP("po", lambda E_, c=c, hh=hh, e_ix=e_ix, gx=gx: E_.tensor_tensor(
                        out=yg_pool[gx][:], in0=yd_sb[e_ix][:],
                        in1=siluz[:, c * 2048 + hh * 128:c * 2048 + (hh + 1) * 128], op=ALU.mult), wait=[m_yd[e_ix], M["ygfree"][gx], M["siluz_m"][c]])
                M["ydfree"] = m_yg[:]
                YGQ[pr] = m_yg
                if pr >= 2:
                    emit_ygtr(pr - 2)
                if pr + 2 < 8:
                    PIPE[pr + 2] = emit_burst(pr + 2)
                    emit_chain(pr + 2, PIPE[pr + 2])
                if TAIL_DEFER:
                    for _ in range(TAIL_POPS.get(pr, 0)):
                        if M["tailq"]:
                            M["tailq"].popleft()()
            emit_ygtr(6)
            emit_ygtr(7)
            M["scan_x"][c] = [s.mark("dv"), s.mark("pe")]
            M["scan_z"][c] = [s.mark("po")]
            M["Gmask_free"] = s.mark("po")
            M["C2_free"] = s.mark("po")
            if j == 0 and c == 0:
                SNAP("Gmask", Gmask[:], [s.mark("dv")])
                SNAP("subtok", subtok[:, 0:16], [s.mark("dv")])
                SNAP("sdec", sdec_sb[:, 0:16], [s.mark("ac")])
                SNAP("dvec", dvec[:, 0:16], [s.mark("dv")])

            def make_stages(j=j, c=c, marks_yg=list(marks_yg), q=q):
                tx = {}

                def st0():
                    tx["uF"] = DMA("dV0" if c % 2 == 0 else "dV1", uF[c % 2][:], d_u[j * T + c * 128:j * T + (c + 1) * 128, :], wait=[M["uFfree"][c % 2]])
                    bn = BK["mm0"]
                    wv = marks_yg + [m_wout] + bn.pre_write()
                    for pc in range(16):
                        m_mm = s.emit("pe", lambda E_, c=c, pc=pc: E_.matmul(
                            ps_mm[0][:, 0:512], ygT[:, pc * 128:(pc + 1) * 128],
                            wout_sb[:, pc * 1024 + 0 * 512:pc * 1024 + 1 * 512],
                            start=(pc == 0), stop=(pc == 15)),
                            wait=(wv if pc == 0 else ()), sig="pe" if pc == 15 else None)
                    bn.post_write(m_mm)
                    tx["op0"] = m_mm

                def st1():
                    bn = BK["mm1"]
                    wv = marks_yg + bn.pre_write()
                    for pc in range(16):
                        m_mm = s.emit("pe", lambda E_, c=c, pc=pc: E_.matmul(
                            ps_mm[1][:, 0:512], ygT[:, pc * 128:(pc + 1) * 128],
                            wout_sb[:, pc * 1024 + 1 * 512:pc * 1024 + 2 * 512],
                            start=(pc == 0), stop=(pc == 15)),
                            wait=(wv if pc == 0 else ()), sig="pe" if pc == 15 else None)
                    bn.post_write(m_mm)
                    tx["op1"] = m_mm
                    M["ygTfree"] = m_mm

                def st2():
                    usrc = uF[c % 2][:]
                    m_l1 = OP("dv", lambda E_, usrc=usrc: E_.tensor_scalar(out=c4_sb[:], in0=usrc, scalar1=scal[:, 48:49], scalar2=scal[:, 49:50], op0=ALU.subtract, op1=ALU.mult), wait=[tx["uF"]])
                    m_mx = [None, None]
                    for mj in range(2):
                        wv = [tx["op0" if mj == 0 else "op1"], m_l1] + ([M["mixfree"]] if mj == 0 else [])
                        m_mx[mj] = OP("dv", lambda E_, c=c, mj=mj: E_.scalar_tensor_tensor(
                            out=mix_sb[:, mj * 512:(mj + 1) * 512], in0=c4_sb[:, mj * 512:(mj + 1) * 512],
                            scalar=1.0, in1=ps_mm[mj][:, 0:512], op0=ALU.mult, op1=ALU.add), wait=wv)
                        BK[f"mm{mj}"].read(m_mx[mj])
                    tx["mx"] = m_mx
                    tx["usrc"] = usrc

                def st3():
                    m_mx = tx["mx"]
                    mixv = mix_sb[:]
                    OP("dv", lambda E_, mixv=mixv: E_.tensor_reduce(out=scal[:, 41:42], in_=mixv, axis=AX.X, op=ALU.add), wait=[m_mx[0], m_mx[1]], sig=False)
                    m_s3b = OP("ac", lambda E_, mixv=mixv: E_.activation(out=sq_scr[:], in_=mixv, func=AF.Square, accum_out=scal[:, 42:43]), wait=[m_mx[0], m_mx[1]])
                    OP("dv", lambda E_: E_.tensor_scalar(out=scal[:, 43:44], in0=scal[:, 41:42], scalar1=1.0 / DM, scalar2=None, op0=ALU.mult), wait=[m_s3b], sig=False)
                    OP("dv", lambda E_: E_.tensor_scalar(out=scal[:, 44:45], in0=scal[:, 42:43], scalar1=1.0 / DM, scalar2=None, op0=ALU.mult), sig=False)
                    OP("dv", lambda E_: E_.tensor_tensor(out=scal[:, 45:46], in0=scal[:, 43:44], in1=scal[:, 43:44], op=ALU.mult), sig=False)
                    OP("dv", lambda E_: E_.tensor_tensor(out=scal[:, 44:45], in0=scal[:, 44:45], in1=scal[:, 45:46], op=ALU.subtract), sig=False)
                    OP("dv", lambda E_: E_.tensor_scalar(out=scal[:, 44:45], in0=scal[:, 44:45], scalar1=1.0, scalar2=1e-5, op0=ALU.mult, op1=ALU.add), sig=False)
                    m_rc = OP("ac", lambda E_: E_.activation(out=scal[:, 45:46], in_=scal[:, 44:45], func=AF.Ln), wait=[s.mark("dv")])
                    m_r3 = OP("ac", lambda E_: E_.activation(out=scal[:, 46:47], in_=scal[:, 45:46], func=AF.Exp, scale=-0.5))
                    m_mr = OP("dv", lambda E_: E_.tensor_tensor(out=scal[:, 47:48], in0=scal[:, 43:44], in1=scal[:, 46:47], op=ALU.mult), wait=[m_r3])
                    m_x2a = OP("dv", lambda E_, mixv=mixv: E_.tensor_scalar(out=sq_scrF[:], in0=mixv, scalar1=scal[:, 46:47], scalar2=scal[:, 47:48], op0=ALU.mult, op1=ALU.subtract), wait=[m_mr, M["x2free"]])
                    M["mixfree"] = m_x2a
                    m_x2 = OP("dv", lambda E_, c=c: E_.tensor_tensor(out=x2_sb[:], in0=sq_scrF[:], in1=tx["usrc"], op=ALU.add))
                    M["uFfree"][c % 2] = m_x2
                    tx["x2"] = m_x2
                    tx["x2a"] = m_x2a

                def st4():
                    x2v = x2_sb[:]
                    m_x2 = tx["x2"]
                    OP("dv", lambda E_, x2v=x2v: E_.tensor_reduce(out=qst[:, 0:1], in_=x2v, axis=AX.X, op=ALU.add), wait=[m_x2], sig=False)
                    m_f2 = OP("ac", lambda E_, x2v=x2v: E_.activation(out=sq_scr[:], in_=x2v, func=AF.Square, accum_out=qst[:, 1:2]), wait=[m_x2, tx["x2a"]])
                    OP("dv", lambda E_: E_.tensor_scalar(out=qst[:, 2:3], in0=qst[:, 0:1], scalar1=1.0 / DM, scalar2=None, op0=ALU.mult), wait=[m_f2], sig=False)
                    OP("dv", lambda E_: E_.tensor_scalar(out=qst[:, 3:4], in0=qst[:, 1:2], scalar1=1.0 / DM, scalar2=None, op0=ALU.mult), sig=False)
                    OP("dv", lambda E_: E_.tensor_tensor(out=qst[:, 4:5], in0=qst[:, 2:3], in1=qst[:, 2:3], op=ALU.mult), sig=False)
                    OP("dv", lambda E_: E_.tensor_tensor(out=qst[:, 3:4], in0=qst[:, 3:4], in1=qst[:, 4:5], op=ALU.subtract), sig=False)
                    OP("dv", lambda E_: E_.tensor_scalar(out=qst[:, 3:4], in0=qst[:, 3:4], scalar1=1.0, scalar2=1e-5, op0=ALU.mult, op1=ALU.add), sig=False)
                    m_f3 = OP("ac", lambda E_: E_.activation(out=qst[:, 4:5], in_=qst[:, 3:4], func=AF.Ln), wait=[s.mark("dv")])
                    m_f4 = OP("ac", lambda E_: E_.activation(out=qst[:, 5:6], in_=qst[:, 4:5], func=AF.Exp, scale=-0.5))
                    m_c4 = OP("dv", lambda E_, x2v=x2v: E_.tensor_scalar(out=c4_sb[:], in0=x2v, scalar1=qst[:, 2:3], scalar2=None, op0=ALU.subtract), wait=[m_f2])
                    m_am = OP("dv", lambda E_: E_.tensor_reduce(out=qst[:, 6:7], in_=c4_sb[:], axis=AX.X, op=ALU.max, apply_absolute_value=True))
                    OP("dv", lambda E_: E_.tensor_tensor(out=qst[:, 7:8], in0=qst[:, 5:6], in1=qst[:, 6:7], op=ALU.mult), wait=[m_f4, m_am], sig=False)
                    OP("dv", lambda E_: E_.tensor_scalar(out=qst[:, 7:8], in0=qst[:, 7:8], scalar1=1e-8, scalar2=None, op0=ALU.max), sig=False)
                    OP("dv", lambda E_: E_.reciprocal(out=qst[:, 8:9], in_=qst[:, 7:8]), sig=False)
                    OP("dv", lambda E_: E_.tensor_scalar(out=qst[:, 9:10], in0=qst[:, 5:6], scalar1=127.0, scalar2=None, op0=ALU.mult), sig=False)
                    OP("dv", lambda E_: E_.tensor_tensor(out=qst[:, 10:11], in0=qst[:, 8:9], in1=qst[:, 9:10], op=ALU.mult), sig=False)
                    m_xs = OP("dv", lambda E_: E_.tensor_scalar(out=qst[:, 11:12], in0=qst[:, 7:8], scalar1=1.0 / 127.0, scalar2=None, op0=ALU.mult))
                    OP("dv", lambda E_: E_.tensor_scalar(out=sq_scrF[:], in0=c4_sb[:], scalar1=qst[:, 10:11], scalar2=MAGIC, op0=ALU.mult, op1=ALU.add), wait=[m_xs], sig=False)
                    m_q2 = OP("dv", lambda E_: E_.tensor_scalar(out=xq_tok[:], in0=sq_scrF[:], scalar1=MAGIC, scalar2=None, op0=ALU.subtract))
                    tx["q2"] = m_q2
                    tx["xs"] = m_xs

                def st5():
                    wv = [tx["q2"]] + BK["trb"].pre_write()
                    for fc in range(8):
                        m_tq = s.emit("pe", lambda E_, fc=fc: E_.transpose(ps_trb[:, fc * 128:(fc + 1) * 128], xq_tok[:, fc * 128:(fc + 1) * 128], id_bf[:]),
                                      wait=(wv if fc == 0 else ()), sig="pe" if fc == 7 else None)
                    BK["trb"].post_write(m_tq)
                    m_eq = OP("dv", lambda E_, c=c: E_.tensor_copy(out=xqT[:], in_=ps_trb[:]), wait=[m_tq, M["xqTfree"]])
                    BK["trb"].read(m_eq)
                    tx["eq"] = m_eq

                def st6():
                    m_fmm = [None, None]
                    for oj in range(2):
                        bn = BK[f"mm{oj}"]
                        wv = [tx["eq"]] + bn.pre_write()
                        for fc in range(8):
                            m_mm = s.emit("pe", lambda E_, c=c, oj=oj, fc=fc: E_.matmul(
                                ps_mm[oj][:, 0:512], xqT[:, fc * 128:(fc + 1) * 128],
                                wffn_sb[:, fc * 1024 + oj * 512:fc * 1024 + (oj + 1) * 512],
                                start=(fc == 0), stop=(fc == 7)),
                                wait=(wv if fc == 0 else ()), sig="pe" if fc == 7 else None)
                        bn.post_write(m_mm)
                        m_fmm[oj] = m_mm
                    M["xqTfree"] = m_fmm[1]
                    m_fe = [None, None]
                    for oj in range(2):
                        m_fe[oj] = OP("ac", lambda E_, oj=oj: E_.activation(out=sq_scrF[:, oj * 512:(oj + 1) * 512], in_=ps_mm[oj][:, 0:512], func=AF.Copy, scale=qst[:, 11:12]), wait=[m_fmm[oj], tx["xs"]])
                        BK[f"mm{oj}"].read(m_fe[oj])
                    tx["fe"] = m_fe

                def st7():
                    m_fe = tx["fe"]
                    m_p1 = OP("po", lambda E_, c=c: E_.tensor_tensor(out=ffn_sb[:], in0=sq_scrF[:], in1=br_ws[:], op=ALU.mult), wait=[m_fe[0], m_fe[1], M["ffnfree"]])
                    m_p2 = OP("po", lambda E_, c=c: E_.tensor_tensor(out=ffn_sb[:], in0=ffn_sb[:], in1=br_fb[:], op=ALU.add))
                    fv = ffn_sb[:]
                    OP("dv", lambda E_, fv=fv: E_.tensor_reduce(out=qst[:, 12:13], in_=fv, axis=AX.X, op=ALU.add), wait=[m_p2], sig=False)
                    m_s5b = OP("ac", lambda E_, fv=fv: E_.activation(out=sq_scr[:], in_=fv, func=AF.Square, accum_out=qst[:, 13:14]), wait=[m_p2])
                    OP("dv", lambda E_: E_.tensor_scalar(out=qst[:, 12:13], in0=qst[:, 12:13], scalar1=1.0 / DM, scalar2=None, op0=ALU.mult), wait=[m_s5b], sig=False)
                    OP("dv", lambda E_: E_.tensor_scalar(out=qst[:, 13:14], in0=qst[:, 13:14], scalar1=1.0 / DM, scalar2=None, op0=ALU.mult), sig=False)
                    OP("dv", lambda E_: E_.tensor_tensor(out=qst[:, 14:15], in0=qst[:, 12:13], in1=qst[:, 12:13], op=ALU.mult), sig=False)
                    OP("dv", lambda E_: E_.tensor_tensor(out=qst[:, 13:14], in0=qst[:, 13:14], in1=qst[:, 14:15], op=ALU.subtract), sig=False)
                    OP("dv", lambda E_: E_.tensor_scalar(out=qst[:, 13:14], in0=qst[:, 13:14], scalar1=1.0, scalar2=1e-5, op0=ALU.mult, op1=ALU.add), sig=False)
                    m_r5 = OP("ac", lambda E_: E_.activation(out=qst[:, 14:15], in_=qst[:, 13:14], func=AF.Ln), wait=[s.mark("dv")])
                    m_sq5 = OP("ac", lambda E_: E_.activation(out=qst[:, 15:16], in_=qst[:, 14:15], func=AF.Exp, scale=-0.5))
                    m_m5 = OP("dv", lambda E_: E_.tensor_tensor(out=qst[:, 14:15], in0=qst[:, 12:13], in1=qst[:, 15:16], op=ALU.mult), wait=[m_sq5])
                    m_o1 = OP("dv", lambda E_, fv=fv: E_.tensor_scalar(out=sq_scrF[:], in0=fv, scalar1=qst[:, 15:16], scalar2=qst[:, 14:15], op0=ALU.mult, op1=ALU.subtract), wait=[m_m5])
                    M["ffnfree"] = m_o1
                    m_o2 = OP("dv", lambda E_, c=c: E_.tensor_tensor(out=x2_sb[:], in0=sq_scrF[:], in1=x2_sb[:], op=ALU.add), wait=[M["outst"]])
                    m_st_ = DMA("dO", d_out[j * T + c * 128:j * T + (c + 1) * 128, :], x2_sb[:], wait=[m_o2])
                    M["x2free"] = m_st_
                    M["outst"] = m_st_

                return [st0, st1, st2, st3, st4, st5, st6, st7]

            cst = STATS[(j, c)][0]
            OP("dv", lambda E_, cst=cst: E_.tensor_copy(out=scal[:, 48:49], in_=cst["mu1"]), sig=False)
            OP("dv", lambda E_, cst=cst: E_.tensor_copy(out=scal[:, 49:50], in_=cst["rstd1"]), sig=False)
            for stg in make_stages():
                M["tailq"].append(stg)
            if not TAIL_DEFER:
                while M["tailq"]:
                    M["tailq"].popleft()()
        M["stats_free"][q] = s.mark("dv")
        M["dl_free"][q] = s.mark("dv")
        M["smT_free"][q] = s.mark("dv")
        M["toks_free"][q] = s.mark("dv")
        M["BC_free"][q] = s.mark("dv")
        M["BCT_free"][q] = [s.mark("pe"), s.mark("po")]
        M["lhl_free"][q] = s.mark("pe")

    NTv = globals().get('NT_OVERRIDE', NT)
    for i in range(NTv + 1):
        if i < NT:
            emit_A1(i)
        if i >= 1:
            if i == NT:
                M["patch_zero"] = OP("dv", lambda E_: E_.memset(patchcol[:], 0.0), wait=[M["patch"]])
            emit_F(i - 1, is_last=(i == NT))
        if i < NT:
            emit_A2(i)

    while M["tailq"]:
        M["tailq"].popleft()()
    final_waits = [(ch, s.cnt[ch]) for ch in DMA_CH]
    s.emit("sy", lambda E_: None, wait=final_waits, sig=None)

    with nc.Block("init") as b0:
        b0.sync(lambda eng: s.run(0, "sy", eng))
        b0.tensor(lambda eng: s.run(0, "pe", eng))
        b0.vector(lambda eng: s.run(0, "dv", eng))
        b0.scalar(lambda eng: s.run(0, "ac", eng))
        b0.gpsimd(lambda eng: s.run(0, "po", eng))
    with nc.Block("main") as b1:
        b1.sync(lambda eng: s.run(1, "sy", eng))
        b1.tensor(lambda eng: s.run(1, "pe", eng))
        b1.vector(lambda eng: s.run(1, "dv", eng))
        b1.scalar(lambda eng: s.run(1, "ac", eng))
        b1.gpsimd(lambda eng: s.run(1, "po", eng))

    st.close()
    return nc


def make_consts():
    bf = ml_dtypes.bfloat16
    idbf = np.eye(128, dtype=bf)
    idf = np.eye(128, dtype=np.float32)
    tri = (np.arange(128)[:, None] < np.arange(128)[None, :]).astype(np.float32)
    oh48 = np.zeros((48, NH * 128), dtype=bf)
    for h in range(NH):
        oh48[h, h * 128:(h + 1) * 128] = 1
        oh48[32 + h, h * 128:(h + 1) * 128] = 1
    pp48 = np.zeros((48, 8 * 128), dtype=bf)
    for pr in range(8):
        h0, h1 = 2 * pr, 2 * pr + 1
        pp48[h0, pr * 128:pr * 128 + 64] = 1
        pp48[32 + h0, pr * 128:pr * 128 + 64] = 1
        pp48[h1, pr * 128 + 64:pr * 128 + 128] = 1
        pp48[32 + h1, pr * 128 + 64:pr * 128 + 128] = 1
    oh8 = np.zeros((8, 8 * 128), dtype=np.float32)
    for ocb in range(8):
        oh8[ocb, ocb * 128:(ocb + 1) * 128] = 1
    return dict(idbf=idbf, idf=idf, tri=tri, oh48=oh48, pp48=pp48, oh8=oh8)


def make_inputs(inputs_dict):
    g = lambda k: np.asarray(inputs_dict[k], np.float32)
    consts = make_consts()
    scale = g("L0_attn_norm_scale")
    biasv = g("L0_attn_norm_bias")
    scale_big = np.ascontiguousarray(np.broadcast_to(scale.reshape(8, 128).T[:, :, None], (128, 8, 128)).reshape(128, DM)).astype(ml_dtypes.bfloat16)
    bcol = np.ascontiguousarray(biasv.reshape(8, 128).T).astype(ml_dtypes.bfloat16)
    common = {
        "w_in": g("L0_attn_in_proj_w"), "w_th": g("L0_attn_theta_w"),
        "w_out": g("L0_attn_out_proj_w"), "w_ffn": g("L0_ffn_weight"),
        "idbf": consts["idbf"], "idf": consts["idf"], "tri": consts["tri"],
        "oh48": consts["oh48"], "pp48": consts["pp48"], "oh8": consts["oh8"],
        "scale_big": scale_big, "bcol": bcol,
        "br_dtb": np.ascontiguousarray(np.tile(g("L0_attn_dt_bias")[None, :], (128, 1))),
        "br_D": np.ascontiguousarray(np.tile(g("L0_attn_D")[None, :], (128, 1))),
        "br_bc": np.ascontiguousarray(np.tile(np.concatenate([g("L0_attn_bcB_scale"), g("L0_attn_bcC_scale")])[None, :], (128, 1))),
        "br_bcb": np.ascontiguousarray(np.tile(np.concatenate([g("L0_attn_B_bias"), g("L0_attn_C_bias")])[None, :], (128, 1))),
        "br_fb": np.ascontiguousarray(np.tile(g("L0_ffn_bias")[None, :], (128, 1))),
        "alog": g("L0_attn_A_log").reshape(NH, 1),
    }
    u = g("L0_attn_u")
    return [dict(common, u=np.ascontiguousarray(u[b])) for b in range(8)]


def kernel(**inputs):
    nc = build_program()
    in_maps = make_inputs(inputs)
    res = run_bass_kernel_spmd(nc, in_maps, core_ids=list(range(8)))
    return np.stack([res.results[b]["out"] for b in range(8)], axis=0)


# version 10
# speedup vs baseline: 1.0150x; 1.0150x over previous
import collections
import contextlib
import math
import numpy as np
import ml_dtypes

import concourse.bass as bass
import concourse.mybir as mybir
from concourse.bass_utils import run_bass_kernel_spmd

F32 = mybir.dt.float32
BF16 = mybir.dt.bfloat16
AF = mybir.ActivationFunctionType
ALU = mybir.AluOpType
AX = mybir.AxisListType

L, DM, DI, NH, HP, NS, PROJ = 4096, 1024, 2048, 16, 128, 64, 4256
T, Q = 256, 128
NCH = T // 128
NT = L // T
TAIL_DEFER = True
TAIL_POPS = {0: 1, 1: 1, 2: 1, 3: 1, 4: 1, 5: 1, 6: 1, 7: 1}
MAGIC = 1.5 * 2.0 ** 23
PI = math.pi
INV2PI = 1.0 / (2.0 * math.pi)

ENGS = ("sy", "pe", "dv", "ac", "po")
DMA_CH = ("dC", "dW8", "dW0", "dW1", "dU0", "dU1", "dV0", "dV1", "dO",
          "dF0", "dF1", "dT", "dWo", "dG0", "dG1", "dDB")
PO_CH = {"dT", "dWo", "dG0", "dG1"}


def _maxm(marks):
    best = {}
    stack = list(marks)
    while stack:
        m = stack.pop()
        if m is None:
            continue
        if isinstance(m, list):
            stack.extend(m)
            continue
        s_, v = m
        if v and (s_ not in best or v > best[s_]):
            best[s_] = v
    return [(k, v) for k, v in best.items()]


class Sched:
    def __init__(self, nc):
        self.nc = nc
        self.q = {0: {e: [] for e in ENGS}, 1: {e: [] for e in ENGS}}
        self.cnt = {}
        self.sems = {}
        self.phase = 0

    def new_sem(self, name, st):
        self.sems[name] = st.enter_context(self.nc.semaphore(name))
        self.cnt[name] = 0

    SERIAL = ("dv", "ac", "po")

    def emit(self, eng, fn, wait=(), sig=None):
        wait = list(wait)
        if eng in self.SERIAL:
            wait.append((eng, self.cnt[eng]))
            if sig is None or sig == eng:
                sig = eng
        markv = None
        inc = 0
        if sig is not None:
            inc = 16 if sig in DMA_CH else 1
            self.cnt[sig] += inc
            markv = self.cnt[sig]
        self.q[self.phase][eng].append((_maxm(wait), fn, sig, inc))
        return (sig, markv) if sig is not None else None

    def mark(self, sig):
        return (sig, self.cnt[sig])

    def run(self, phase, eng_name, eng):
        for waits, fn, sig, inc in self.q[phase][eng_name]:
            for s_, v in waits:
                eng.wait_ge(self.sems[s_], v)
            ins = fn(eng)
            if sig is not None:
                ins.then_inc(self.sems[sig], inc)


def build_program(dbg=()):
    nc = bass.Bass("TRN2", target_bir_lowering=False, detect_race_conditions=False)
    st = contextlib.ExitStack()

    d_u = nc.dram_tensor("u", [L, DM], F32, kind="ExternalInput").ap()
    d_out = nc.dram_tensor("out", [L, DM], F32, kind="ExternalOutput").ap()
    d_win = nc.dram_tensor("w_in", [DM, PROJ], F32, kind="ExternalInput").ap()
    d_thw = nc.dram_tensor("w_th", [DM, 32], F32, kind="ExternalInput").ap()
    d_wout = nc.dram_tensor("w_out", [DI, DM], F32, kind="ExternalInput").ap()
    d_wffn = nc.dram_tensor("w_ffn", [DM, DM], F32, kind="ExternalInput").ap()
    d_idbf = nc.dram_tensor("idbf", [128, 128], BF16, kind="ExternalInput").ap()
    d_idf = nc.dram_tensor("idf", [128, 128], F32, kind="ExternalInput").ap()
    d_tri = nc.dram_tensor("tri", [128, 128], F32, kind="ExternalInput").ap()
    d_oh48 = nc.dram_tensor("oh48", [48, NH * 128], BF16, kind="ExternalInput").ap()
    d_pp48 = nc.dram_tensor("pp48", [48, (NH // 2) * 128], BF16, kind="ExternalInput").ap()
    d_scale_big = nc.dram_tensor("scale_big", [128, DM], BF16, kind="ExternalInput").ap()
    d_bcol = nc.dram_tensor("bcol", [128, 8], BF16, kind="ExternalInput").ap()
    d_brdtb = nc.dram_tensor("br_dtb", [128, NH], F32, kind="ExternalInput").ap()
    d_brD = nc.dram_tensor("br_D", [128, NH], F32, kind="ExternalInput").ap()
    d_brbc = nc.dram_tensor("br_bc", [128, 128], F32, kind="ExternalInput").ap()
    d_brbcb = nc.dram_tensor("br_bcb", [128, 128], F32, kind="ExternalInput").ap()
    d_brfb = nc.dram_tensor("br_fb", [128, DM], F32, kind="ExternalInput").ap()
    d_alog = nc.dram_tensor("alog", [NH, 1], F32, kind="ExternalInput").ap()
    d_oh8 = nc.dram_tensor("oh8", [8, 8 * 128], F32, kind="ExternalInput").ap()
    d_winb = nc.dram_tensor("w_in_bf", [DM, PROJ], BF16, kind="Internal").ap()
    d_bw = nc.dram_tensor("bw_row", [1, PROJ], BF16, kind="Internal").ap()

    dbg_set = {d[0] for d in dbg}
    dbg_d = {}
    for d in dbg:
        name, shape = d[0], d[1]
        dt = BF16 if (len(d) > 2 and d[2] == "bf16") else F32
        dbg_d[name] = nc.dram_tensor("dbg_" + name, list(shape), dt, kind="ExternalOutput").ap()

    SB = {}

    def sb(name, shape, dt):
        h = st.enter_context(nc.sbuf_tensor(name, shape, dt))
        SB[name] = h
        return h

    wout_sb = sb("wout_sb", [128, 16 * DM], BF16)
    wffn_sb = sb("wffn_sb", [128, 8 * DM], BF16)
    thw_sb = sb("thw_sb", [128, 8 * 32], BF16)
    win_buf = [sb(f"win{b}", [128, 8 * 512 + 512], BF16) for b in range(2)]
    win8 = [sb("win8a", [128, 8 * 160], BF16)]
    id_bf = sb("id_bf", [128, 128], BF16)
    id_f = sb("id_f", [128, 128], F32)
    tri_sb = sb("tri_sb", [128, 128], F32)
    oh48_sb = sb("oh48_sb", [48, NH * 128], BF16)
    pp48_sb = sb("pp48_sb", [48, (NH // 2) * 128], BF16)
    scale_big = sb("scale_big_sb", [128, DM], BF16)
    bcol_sb = sb("bcol_sb", [128, 8], BF16)
    ones1b = sb("ones1b", [1, 128], BF16)
    bwT_sb = sb("bwT_sb", [128, 34], BF16)
    bwT_acc = sb("bwT_acc", [128, 34], F32)
    bw8_st = sb("bw8_st", [1, 160], BF16)
    br_dtb = sb("br_dtb_sb", [128, NH], F32)
    br_D = sb("br_D_sb", [128, NH], F32)
    br_bc = sb("br_bc_sb", [128, 128], F32)
    br_bcb = sb("br_bcb_sb", [128, 128], F32)
    br_fb = sb("br_fb_sb", [128, DM], F32)
    br_ws = sb("br_ws_sb", [128, DM], F32)
    acol = sb("acol", [NH, 1], F32)
    ones32 = sb("ones32", [32, Q], F32)

    uA = [sb(f"uA{b}", [128, DM], F32) for b in range(2)]
    uF = [sb(f"uF{b}", [128, DM], F32) for b in range(2)]
    x_act = sb("x_act", [128, NCH * DI], BF16)
    siluz = sb("siluz", [128, NCH * DI], BF16)
    t2_tok = [sb(f"t2tok{b}", [128, DM], BF16) for b in range(2)]
    t2T = [sb(f"t2T{k}", [128, DM], BF16) for k in range(2)]
    xinT = [sb(f"xinT{k}", [128, DM], BF16) for k in range(NCH)]
    bcdl = [sb(f"bcdl{b}", [128, NCH * 160], F32) for b in range(2)]
    dl_sb = [sb(f"dl{b}", [128, NCH * 32], F32) for b in range(2)]
    theta_tok = sb("theta_tok", [128, NCH * 32], F32)
    stats = [sb(f"stats{b}", [128, 64], F32) for b in range(2)]
    sq_scr = sb("sq_scr", [128, DM], BF16)
    sq_scrF = sb("sq_scrF", [128, DM], F32)
    BCbf = [sb(f"BCbf{b}", [128, NCH * 128], BF16) for b in range(2)]
    BCT = [sb(f"BCT{b}", [128, NCH * 128], BF16) for b in range(2)]
    dtT_sb = [sb(f"dtT{b}", [16, NCH * 128], F32) for b in range(2)]
    lamT_sb = [sb(f"lamT{b}", [16, NCH * 128], F32) for b in range(2)]
    termsT_sb = sb("termsT", [32, NCH * 128], F32)
    lamt_sb = sb("lamt", [16, NCH * 128], F32)
    lamrev_sb = sb("lamrev", [16, NCH * 128], F32)
    lhl = [sb(f"lhl{b}", [48, NCH * 128], BF16) for b in range(2)]
    toks = [sb(f"toks{b}", [128, NCH * 48], F32) for b in range(2)]
    phiT = sb("phiT", [32, T], F32)
    phicarry = sb("phicarry", [32, 1], F32)
    phitok = sb("phitok", [128, NCH * 32], F32)
    cs_tile = sb("cs_tile", [128, NCH * 96], F32)
    bctmp = sb("bctmp", [128, 128], F32)
    ropeA = sb("ropeA", [128, 128], F32)
    ropeB = sb("ropeB", [128, 128], F32)
    wT_sb = sb("wT_sb", [16, T], F32)
    dtlT_sb = sb("dtlT_sb", [16, T], F32)
    rT_sb = sb("rT_sb", [16, T], F32)
    lnrT_sb = sb("lnrT_sb", [16, T], F32)
    patchcol = sb("patchcol", [16, 1], F32)
    wtok = sb("wtok", [128, NCH * NH], F32)
    dvec = sb("dvec", [128, NCH * NH], F32)
    sdec_sb = sb("sdec_sb", [128, NCH * NH], F32)
    subtok = sb("subtok", [128, NCH * NH], F32)
    dgtmp = sb("dgtmp", [128, NS], F32)
    Gmask = sb("GmaskSB", [128, 128], BF16)
    nE_sb = [sb(f"nE{b}", [128, 128], F32) for b in range(4)]
    expE = [sb(f"expE{b}", [128, 128], BF16) for b in range(4)]
    Mt_sb = [sb(f"Mt{b}", [128, 128], BF16) for b in range(4)]
    edec2 = [sb(f"edec2{b}", [128, 128], F32) for b in range(2)]
    C2_sb = sb("C2_sb", [128, 128], BF16)
    Cdec2 = [sb(f"Cdec2{b}", [128, 128], BF16) for b in range(2)]
    Bw_sb = [sb(f"Bw{b}", [128, NS], BF16) for b in range(4)]
    H2f = [sb(f"H2f_{p}", [128, 128], F32) for p in range(8)]
    H2b = [sb(f"H2b_{p}", [128, 128], BF16) for p in range(8)]
    yd_sb = [sb(f"yd{b}", [128, 128], F32) for b in range(2)]
    yg_pool = [sb(f"ygp{b}", [128, 128], BF16) for b in range(4)]
    ygT = sb("ygT", [128, 16 * 128], BF16)
    mix_sb = sb("mix_sb", [128, DM], F32)
    x2_sb = sb("x2_sb", [128, DM], F32)
    c4_sb = sb("c4_sb", [128, DM], F32)
    xq_tok = sb("xq_tok", [128, DM], BF16)
    xqT = sb("xqT", [128, DM], BF16)
    ffn_sb = sb("ffn_sb", [128, DM], F32)
    scal = sb("scal", [128, 64], F32)
    sm16 = sb("sm16", [128, 96], F32)
    qst = sb("qst", [128, 16], F32)
    fsm = sb("fsm", [128, 16], F32)

    stg_w = [uA[0], uA[1]]
    stg_q = xq_tok
    stg_s = sb("stg_s", [128, 16], F32)
    stg_wt = sb("stg_wt", [8, 128], F32)

    ps_mm = [st.enter_context(nc.psum_tensor(f"ps_mm{b}", [128, 512], F32)) for b in range(2)]
    ps_trb = st.enter_context(nc.psum_tensor("ps_trb", [128, 1024], BF16))
    ps_trf = st.enter_context(nc.psum_tensor("ps_trf", [128, 512], F32))
    ps_br = [st.enter_context(nc.psum_tensor(f"ps_br{b}", [128, 512], F32)) for b in range(2)]
    ps_Y = [st.enter_context(nc.psum_tensor(f"ps_Y{b}", [128, 512], F32)) for b in range(2)]

    s = Sched(nc)
    for sem in ("pe", "dv", "ac", "po") + DMA_CH:
        s.new_sem(sem, st)

    def DMA(qname, out, in_, wait=(), slow=False):
        e = "po" if qname in PO_CH else "sy"
        if slow:
            return s.emit(e, lambda E, o=out, i=in_: E.dma_start(out=o, in_=i, allow_slow_non_contiguous=True), wait=wait, sig=qname)
        return s.emit(e, lambda E, o=out, i=in_: E.dma_start(out=o, in_=i), wait=wait, sig=qname)

    def OP(eng, fn, wait=(), sig=True):
        return s.emit(eng, fn, wait=wait, sig=(eng if sig else None))

    def SNAP(tag, src, wait):
        if tag in dbg_set:
            M["dbg_last"] = DMA("dDB", dbg_d[tag], src, wait=list(wait) + [M.get("dbg_last")])

    class Bank:
        def __init__(self):
            self.readers = []
            self.writer = None

        def pre_write(self):
            w = list(self.readers)
            self.readers = []
            return w

        def post_write(self, mark):
            self.writer = mark

        def read(self, mark):
            self.readers.append(mark)

    BK = {"trb": Bank(), "trf": Bank(), "mm0": Bank(), "mm1": Bank(),
          "br0": Bank(), "br1": Bank(), "Y0": Bank(), "Y1": Bank()}

    s.phase = 0
    for o, i in [
        (id_bf[:], d_idbf), (id_f[:], d_idf), (tri_sb[:], d_tri),
        (oh48_sb[:], d_oh48), (pp48_sb[:], d_pp48),
        (scale_big[:], d_scale_big), (bcol_sb[:], d_bcol),
        (br_dtb[:], d_brdtb), (br_D[:], d_brD), (br_bc[:], d_brbc),
        (br_bcb[:], d_brbcb), (br_fb[:], d_brfb), (acol[:], d_alog),
    ]:
        DMA("dC", o, i)
    m_consts = s.mark("dC")

    m_oh8 = DMA("dC", uF[0][0:8, 0:1024], d_oh8, wait=[m_consts])
    m_thw = DMA("dT", thw_sb[:].rearrange("p (c m) -> p c m", c=8), d_thw.rearrange("(c p) m -> p c m", p=128))
    m_wout = DMA("dWo", wout_sb[:].rearrange("p (c m) -> p c m", c=16), d_wout.rearrange("(c p) m -> p c m", p=128))

    m_bz = OP("dv", lambda E: E.memset(bwT_acc[:], 0.0))
    pst1 = pst2 = None
    m_bwadd = None
    for fc in range(8):
        m_l1_ = DMA("dG0", win_buf[0][:, 0:4096], d_win[fc * 128:(fc + 1) * 128, 0:4096], wait=[pst1, m_bwadd])
        m_l2_ = DMA("dG1", win8[0][:, 0:160], d_win[fc * 128:(fc + 1) * 128, 4096:4256], wait=[pst2])
        wv0 = [m_l1_, m_consts, m_bz] + BK["trf"].pre_write()
        for sub in range(32):
            m_bwmm = s.emit("pe", lambda E, fc=fc, sub=sub: E.matmul(
                ps_trf[0:128, 440 + sub:441 + sub], win_buf[0][:, sub * 128:(sub + 1) * 128],
                bcol_sb[:, fc:fc + 1], start=True, stop=True),
                wait=(wv0 if sub == 0 else ()), sig="pe" if sub == 31 else None)
        BK["trf"].post_write(m_bwmm)
        m_bwadd = OP("dv", lambda E: E.tensor_tensor(out=bwT_acc[:, 0:32], in0=bwT_acc[:, 0:32], in1=ps_trf[:, 440:472], op=ALU.add), wait=[m_bwmm])
        BK["trf"].read(m_bwadd)
        pst1 = DMA("dW0", d_winb[fc * 128:(fc + 1) * 128, 0:4096], win_buf[0][:, 0:4096], wait=[m_l1_])
        pst2 = DMA("dW1", d_winb[fc * 128:(fc + 1) * 128, 4096:4256], win8[0][:, 0:160], wait=[m_l2_])
    m_winb = [pst1, pst2]
    m_win8 = DMA("dW8", win8[0][:].rearrange("p (c m) -> p c m", c=8), d_winb[:, 4096:4256].rearrange("(c p) m -> p c m", p=128), wait=[pst2])
    m_bw8add = None
    for fc in range(8):
        wv8 = [m_win8] + BK["trf"].pre_write()
        m_bw8a = s.emit("pe", lambda E, fc=fc: E.matmul(
            ps_trf[0:128, 472:473], win8[0][:, fc * 160:fc * 160 + 128],
            bcol_sb[:, fc:fc + 1], start=True, stop=True), wait=wv8, sig=None)
        m_bw8b = s.emit("pe", lambda E, fc=fc: E.matmul(
            ps_trf[0:32, 473:474], win8[0][:, fc * 160 + 128:fc * 160 + 160],
            bcol_sb[:, fc:fc + 1], start=True, stop=True), sig="pe")
        BK["trf"].post_write(m_bw8b)
        OP("dv", lambda E: E.tensor_tensor(out=bwT_acc[:, 32:33], in0=bwT_acc[:, 32:33], in1=ps_trf[:, 472:473], op=ALU.add), wait=[m_bw8b], sig=False)
        m_bw8add = OP("dv", lambda E: E.tensor_tensor(out=bwT_acc[0:32, 33:34], in0=bwT_acc[0:32, 33:34], in1=ps_trf[0:32, 473:474], op=ALU.add))
        BK["trf"].read(m_bw8add)
    m_bwe = OP("dv", lambda E: E.tensor_copy(out=bwT_sb[:], in_=bwT_acc[:]), wait=[m_bw8add])
    m_bws1 = DMA("dG0", d_bw[0:1, 0:4224].rearrange("r (j p) -> p (r j)", p=128), bwT_sb[:, 0:33], wait=[m_bwe], slow=True)
    m_bws2 = DMA("dG1", d_bw[0:1, 4224:4256].rearrange("r (j p) -> p (r j)", p=32), bwT_sb[0:32, 33:34], wait=[m_bwe], slow=True)
    m_bw8st = DMA("dW8", bw8_st[:], d_bw[0:1, 4096:4256], wait=[m_bws1, m_bws2])

    m_ae = OP("ac", lambda E: E.activation(out=acol[:], in_=acol[:], func=AF.Exp), wait=[m_consts])
    m_acol = OP("dv", lambda E: E.tensor_scalar(out=acol[:], in0=acol[:], scalar1=-1.0, scalar2=None, op0=ALU.mult), wait=[m_ae])

    OP("po", lambda E: E.memset(ones32[:], 1.0), wait=[], sig=False)
    OP("po", lambda E: E.memset(phicarry[:], 0.0), sig=False)
    for p in range(8):
        OP("po", lambda E, p=p: E.memset(H2f[p][:], 0.0), sig=False)
        OP("po", lambda E, p=p: E.memset(H2b[p][:], 0.0), sig=False)
    OP("po", lambda E: E.memset(toks[0][:], 0.0), sig=False)
    OP("po", lambda E: E.memset(toks[1][:], 0.0), sig=False)
    OP("po", lambda E: E.memset(lhl[0][:], 0.0), sig=False)
    OP("po", lambda E: E.memset(lhl[1][:], 0.0), sig=False)
    OP("po", lambda E: E.memset(ones1b[:], 1.0), sig=False)
    m_initpo = OP("po", lambda E: E.memset(patchcol[:], 0.0))

    m_qread = [None, None]
    m_wffn_ev = None
    for oc in range(8):
        b = oc % 2
        m_lw = DMA("dF0" if b == 0 else "dF1", stg_w[b][:], d_wffn[oc * 128:(oc + 1) * 128, :], wait=[m_qread[b]])
        OP("dv", lambda E, b=b: E.tensor_reduce(out=scal[:, 0:1], in_=stg_w[b][:], axis=AX.X, op=ALU.max, apply_absolute_value=True), wait=[m_lw], sig=False)
        OP("dv", lambda E: E.tensor_scalar(out=scal[:, 1:2], in0=scal[:, 0:1], scalar1=1e-8, scalar2=None, op0=ALU.max), sig=False)
        OP("dv", lambda E: E.reciprocal(out=scal[:, 2:3], in_=scal[:, 1:2]), sig=False)
        m_qs = OP("dv", lambda E: E.tensor_scalar(out=scal[:, 3:4], in0=scal[:, 2:3], scalar1=127.0, scalar2=None, op0=ALU.mult))
        OP("dv", lambda E, oc=oc: E.tensor_scalar(out=stg_s[:, oc:oc + 1], in0=scal[:, 1:2], scalar1=1.0 / 127.0, scalar2=None, op0=ALU.mult), sig=False)
        OP("dv", lambda E, b=b: E.tensor_scalar(out=sq_scrF[:], in0=stg_w[b][:], scalar1=scal[:, 3:4], scalar2=MAGIC, op0=ALU.mult, op1=ALU.add), wait=[m_qs], sig=False)
        m_qq = OP("dv", lambda E: E.tensor_scalar(out=stg_q[:], in0=sq_scrF[:], scalar1=MAGIC, scalar2=None, op0=ALU.subtract))
        m_qread[b] = m_qq
        wv = [m_qq, m_consts] + BK["trb"].pre_write()
        for fc in range(8):
            m_tr = s.emit("pe", lambda E, fc=fc: E.transpose(ps_trb[:, fc * 128:(fc + 1) * 128], stg_q[:, fc * 128:(fc + 1) * 128], id_bf[:]),
                          wait=(wv if fc == 0 else ()), sig="pe" if fc == 7 else None)
        BK["trb"].post_write(m_tr)
        dst = wffn_sb[:].rearrange("p (f o) -> p f o", f=8)[:, :, oc * 128:(oc + 1) * 128]
        m_wffn_ev = OP("dv", lambda E, dst=dst: E.tensor_copy(out=dst, in_=ps_trb[:].rearrange("p (f t) -> p f t", f=8)), wait=[m_tr])
        BK["trb"].read(m_wffn_ev)
    m_wst = s.emit("pe", lambda E: E.transpose(ps_trf[0:8, 0:128], stg_s[:, 0:8], id_f[:]), wait=[m_wffn_ev] + BK["trf"].pre_write(), sig="pe")
    BK["trf"].post_write(m_wst)
    m_wse = OP("dv", lambda E: E.tensor_copy(out=stg_wt[:], in_=ps_trf[0:8, 0:128]), wait=[m_wst])
    BK["trf"].read(m_wse)
    m_brws = None
    for ocb in range(8):
        bn = BK[f"mm{ocb % 2}"]
        mmm = s.emit("pe", lambda E, ocb=ocb: E.matmul(ps_mm[ocb % 2][:, 0:128], uF[0][0:8, ocb * 128:(ocb + 1) * 128], stg_wt[:], start=True, stop=True),
                     wait=[m_wse, m_oh8] + bn.pre_write(), sig="pe")
        bn.post_write(mmm)
        m_brws = OP("dv", lambda E, ocb=ocb: E.tensor_copy(out=br_ws[:, ocb * 128:(ocb + 1) * 128], in_=ps_mm[ocb % 2][:, 0:128]), wait=[mmm])
        bn.read(m_brws)

    s.phase = 1

    M = {
        "uAfree": [None, None], "uFfree": [None, None],
        "win8read": [None, None], "winread": [None, None],
        "t2b": [None, None],
        "win8load": [None, None],
        "scan_x": [None] * 4, "scan_z": [None] * 4,
        "t2tok": [None, None],
        "patch": None, "patch_zero": None,
        "outst": None, "pcarry": None,
        "lhl_free": [None, None], "smT_free": [None, None], "toks_free": [None, None],
        "BC_free": [None, None], "BCT_free": [None, None],
        "bcdl_free": [None, None], "dl_free": [None, None], "stats_free": [None, None],
        "mixfree": None, "x2free": None, "ffnfree": None,
        "ygTfree": None, "xqTfree": None,
        "ydfree": [None, None], "ygfree": [None] * 4,
        "tailq": collections.deque(),
        "E_free": [None] * 4, "expE_free": [None] * 4, "Mt_free": [None] * 4,
        "edec2_free": [None, None], "Cdec2_free": [None, None], "Bw_free": [None] * 4,
        "Gmask_free": None, "C2_free": None,
        "sinlast": None, "coslast": None,
        "BCT_m": [[None] * 4, [None] * 4],
        "lhl_m": [[None] * 4, [None] * 4],
        "xact_m": [None] * 4, "siluz_m": [None] * 4,
        "phiscan": None,
        "xinT_m": [None] * 4,
        "theta_m": [None] * 4,
        "t2T_m": [None] * 4,
        "dlm": [None] * 4,
        "bcdl_m": [None] * 4,
        "t2T_free": [None] * 2,
        "xinT_free": [None] * 4,
    }
    STATS = {}

    def stat_cols(p, k):
        b = stats[p]
        return {n: b[:, k * 16 + j: k * 16 + j + 1] for j, n in enumerate(
            ["mu1", "rstd1", "sa", "sbv", "irstd2", "t0", "t1", "t2", "t3", "t4", "t5"])}

    def emit_stats(i, k, usrc, m_uload):
        p = i % 2
        c = stat_cols(p, k)
        wfree = [M["stats_free"][p]] if k == 0 else []
        m1 = OP("dv", lambda E: E.tensor_reduce(out=c["t0"], in_=usrc, axis=AX.X, op=ALU.add), wait=[m_uload] + wfree)
        m2 = OP("ac", lambda E: E.activation(out=sq_scr[:], in_=usrc, func=AF.Square, accum_out=c["t1"]), wait=[m_uload] + wfree)
        m_mu = OP("dv", lambda E: E.tensor_scalar(out=c["mu1"], in0=c["t0"], scalar1=1.0 / DM, scalar2=None, op0=ALU.mult), wait=[m1])
        OP("dv", lambda E: E.tensor_scalar(out=c["t2"], in0=c["t1"], scalar1=1.0 / DM, scalar2=None, op0=ALU.mult), wait=[m2], sig=False)
        OP("dv", lambda E: E.tensor_tensor(out=c["t3"], in0=c["mu1"], in1=c["mu1"], op=ALU.mult), sig=False)
        OP("dv", lambda E: E.tensor_tensor(out=c["t4"], in0=c["t2"], in1=c["t3"], op=ALU.subtract), sig=False)
        m2b = OP("dv", lambda E: E.tensor_scalar(out=c["t5"], in0=c["t4"], scalar1=1.0, scalar2=1e-5, op0=ALU.mult, op1=ALU.add))
        m3l = OP("ac", lambda E: E.activation(out=c["t2"], in_=c["t5"], func=AF.Ln), wait=[m2b])
        m4 = OP("ac", lambda E: E.activation(out=c["rstd1"], in_=c["t2"], func=AF.Exp, scale=-0.5), wait=[m3l])
        OP("dv", lambda E: E.tensor_tensor(out=c["t3"], in0=c["rstd1"], in1=c["rstd1"], op=ALU.mult), wait=[m4], sig=False)
        OP("dv", lambda E: E.tensor_tensor(out=c["t5"], in0=c["t4"], in1=c["t3"], op=ALU.mult), sig=False)
        m5 = OP("dv", lambda E: E.tensor_scalar(out=c["t5"], in0=c["t5"], scalar1=1.0, scalar2=1e-6, op0=ALU.mult, op1=ALU.add))
        m5l = OP("ac", lambda E: E.activation(out=c["t2"], in_=c["t5"], func=AF.Ln), wait=[m5])
        m6 = OP("ac", lambda E: E.activation(out=c["irstd2"], in_=c["t2"], func=AF.Exp, scale=0.5))
        m8 = OP("ac", lambda E: E.activation(out=c["t3"], in_=c["t2"], func=AF.Exp, scale=-0.5))
        OP("dv", lambda E: E.tensor_tensor(out=c["sa"], in0=c["rstd1"], in1=c["t3"], op=ALU.mult), wait=[m8], sig=False)
        m9 = OP("dv", lambda E: E.tensor_tensor(out=c["sbv"], in0=c["sa"], in1=c["irstd2"], op=ALU.mult))
        STATS[(i, k)] = (c, m9, m6, m_mu)

    def emit_t2build(i, k, usrc):
        c, m_sa, _, m_mu = STATS[(i, k)]
        tb = k % 2
        m_t2 = OP("dv", lambda E, tb=tb: E.tensor_scalar(out=t2_tok[tb][:], in0=usrc, scalar1=c["mu1"], scalar2=None, op0=ALU.subtract),
                  wait=[m_mu, M["t2tok"][tb]])
        M["t2b"][tb] = m_t2

    def emit_t2T(i, k):
        tb = k % 2
        wv = [M["t2b"][tb]] + BK["trb"].pre_write()
        for fc in range(8):
            m_tr = s.emit("pe", lambda E, fc=fc, tb=tb: E.transpose(ps_trb[:, fc * 128:(fc + 1) * 128], t2_tok[tb][:, fc * 128:(fc + 1) * 128], id_bf[:]),
                          wait=(wv if fc == 0 else ()), sig="pe" if fc == 7 else None)
        BK["trb"].post_write(m_tr)
        M["t2tok"][tb] = m_tr
        m_ev1 = OP("ac", lambda E, k=k: E.activation(out=t2T[k % 2][:], in_=ps_trb[:], func=AF.Copy), wait=[m_tr, M["t2T_free"][k % 2]])
        m_ev2 = OP("dv", lambda E, k=k: E.tensor_tensor(out=xinT[k][:], in0=ps_trb[:], in1=scale_big[:], op=ALU.mult), wait=[m_tr, m_ev1, M["xinT_free"][k]])
        BK["trb"].read(m_ev1)
        BK["trb"].read(m_ev2)
        M["xinT_m"][k] = m_ev2
        M["t2T_m"][k] = m_ev1

    def emit_m8(i, k):
        p = i % 2
        c, m_sa = STATS[(i, k)][0], STATS[(i, k)][1]
        bn = BK["mm0"]
        wv = [M["xinT_m"][k], m_win8] + bn.pre_write()
        for fc in range(8):
            m_mm = s.emit("pe", lambda E, fc=fc, k=k, p=p: E.matmul(
                ps_mm[0][:, 0:160], xinT[k][:, fc * 128:(fc + 1) * 128],
                win8[0][:, fc * 160:(fc + 1) * 160], start=(fc == 0), stop=False),
                wait=(wv if fc == 0 else ()), sig="pe" if fc == 7 else None)
        m_mm = s.emit("pe", lambda E: E.matmul(ps_mm[0][:, 0:160], ones1b[0:1, 0:128], bw8_st[0:1, 0:160], start=False, stop=True),
                      wait=[m_bw8st], sig="pe")
        bn.post_write(m_mm)
        wdst = [m_mm, m_sa] + ([M["bcdl_free"][p]] if k == 0 else [])
        m_ev = OP("dv", lambda E, k=k, p=p, c=c: E.tensor_scalar(out=bcdl[p][:, k * 160:(k + 1) * 160], in0=ps_mm[0][:, 0:160], scalar1=c["sa"], scalar2=None, op0=ALU.mult), wait=wdst)
        bn.read(m_ev)
        M["bcdl_m"][k] = m_ev
        dtr = bcdl[p][:, k * 160 + 128: k * 160 + 144]
        lmr = bcdl[p][:, k * 160 + 144: k * 160 + 160]
        wdl = [M["dl_free"][p]] if k == 0 else []
        m_a = OP("dv", lambda E, dtr=dtr: E.tensor_tensor(out=sm16[:, 0:16], in0=dtr, in1=br_dtb[:], op=ALU.add), wait=[m_ev, M["sinlast"]] + wdl)
        m_b = OP("ac", lambda E: E.activation(out=sm16[:, 16:32], in_=sm16[:, 0:16], func=AF.Abs), wait=[m_a])
        m_c2 = OP("ac", lambda E: E.activation(out=sm16[:, 16:32], in_=sm16[:, 16:32], func=AF.Exp, scale=-1.0))
        m_d = OP("ac", lambda E: E.activation(out=sm16[:, 16:32], in_=sm16[:, 16:32], func=AF.Ln, bias=1.0))
        m_e = OP("ac", lambda E: E.activation(out=sm16[:, 0:16], in_=sm16[:, 0:16], func=AF.Relu))
        m_dt = OP("dv", lambda E, k=k, p=p: E.tensor_tensor(out=dl_sb[p][:, k * 32:k * 32 + 16], in0=sm16[:, 0:16], in1=sm16[:, 16:32], op=ALU.add), wait=[m_d, m_e])
        m_l1_ = OP("ac", lambda E, lmr=lmr: E.activation(out=sm16[:, 32:48], in_=lmr, func=AF.Exp, scale=-1.0), wait=[m_ev, M["coslast"]])
        m_l2_ = OP("dv", lambda E: E.tensor_scalar(out=sm16[:, 32:48], in0=sm16[:, 32:48], scalar1=1.0, scalar2=None, op0=ALU.add), wait=[m_l1_])
        m_lm = OP("dv", lambda E, k=k, p=p: E.reciprocal(out=dl_sb[p][:, k * 32 + 16:k * 32 + 32], in_=sm16[:, 32:48]))
        M["dlm"][k] = (m_dt, m_lm)

    def emit_patch(i):
        p = i % 2
        m_dt, m_lm = M["dlm"][0]
        dtv = dl_sb[p][:, 0:16]
        lmv = dl_sb[p][:, 16:32]
        m1 = OP("dv", lambda E: E.tensor_tensor(out=sm16[:, 32:48], in0=dtv, in1=lmv, op=ALU.mult), wait=[m_dt, m_lm, M["coslast"]])
        m2 = OP("dv", lambda E: E.tensor_tensor(out=sm16[:, 48:64], in0=dtv, in1=sm16[:, 32:48], op=ALU.subtract))
        wv = [m2] + BK["trf"].pre_write()
        m3 = s.emit("pe", lambda E: E.transpose(ps_trf[0:16, 0:128], sm16[:, 48:64], id_f[:]), wait=wv, sig="pe")
        BK["trf"].post_write(m3)
        m4 = OP("dv", lambda E: E.tensor_copy(out=patchcol[:], in_=ps_trf[0:16, 0:1]), wait=[m3])
        BK["trf"].read(m4)
        M["patch"] = m4

    def emit_uload_stats_t2b(i, k, extra=()):
        m_u = DMA("dU0" if k == 0 else "dU1", uA[k % 2][:], d_u[i * T + k * 128:i * T + (k + 1) * 128, :],
                  wait=[M["uAfree"][k % 2]] + list(extra))
        emit_stats(i, k, uA[k % 2][:], m_u)
        emit_t2build(i, k, uA[k % 2][:])
        M["uAfree"][k % 2] = [s.mark("dv"), s.mark("ac")]

    def emit_A1(i):
        emit_t2T(i, 0)
        emit_m8(i, 0)
        emit_patch(i)

    def emit_theta(i, k):
        bn = BK["mm1"]
        c, m_sa = STATS[(i, k)][0], STATS[(i, k)][1]
        wv = [M["t2T_m"][k]] + bn.pre_write()
        for fc in range(8):
            m_mm = s.emit("pe", lambda E, fc=fc, k=k: E.matmul(
                ps_mm[1][:, 0:32], t2T[k % 2][:, fc * 128:(fc + 1) * 128],
                thw_sb[:, fc * 32:(fc + 1) * 32], start=(fc == 0), stop=(fc == 7)),
                wait=(wv if fc == 0 else ()), sig="pe" if fc == 7 else None)
        bn.post_write(m_mm)
        M["t2T_free"][k % 2] = m_mm
        m_ev = OP("dv", lambda E, k=k, c=c: E.tensor_scalar(
            out=theta_tok[:, k * 32:(k + 1) * 32], in0=ps_mm[1][:, 0:32],
            scalar1=c["sbv"], scalar2=None, op0=ALU.mult), wait=[m_mm, m_sa])
        bn.read(m_ev)
        M["theta_m"][k] = m_ev

    def trf_transpose(out_ap, src_ap, kdim, wait):
        wv = list(wait) + BK["trf"].pre_write()
        b0 = src_ap.base_partition() if kdim != 128 else 0
        ident = id_f[:] if kdim == 128 else id_f[b0:b0 + kdim, b0:b0 + kdim]
        m = s.emit("pe", lambda E: E.transpose(out_ap, src_ap, ident), wait=wv, sig="pe")
        BK["trf"].post_write(m)
        return m

    def emit_smalls_T(i, k):
        p = i % 2
        m_dt, m_lm = M["dlm"][k]
        m_theta = M["theta_m"][k]
        m_s1 = OP("dv", lambda E, p=p, k=k: E.tensor_reduce(out=scal[:, 32:33], in_=dl_sb[p][:, k * 32:k * 32 + 16], axis=AX.X, op=ALU.add), wait=[m_dt])
        m_s2 = OP("dv", lambda E: E.tensor_scalar(out=scal[:, 33:34], in0=scal[:, 32:33], scalar1=1.0 / NH, scalar2=None, op0=ALU.mult))
        m_tm = OP("dv", lambda E, k=k: E.tensor_scalar(out=sm16[:, 0:32], in0=theta_tok[:, k * 32:(k + 1) * 32], scalar1=scal[:, 33:34], scalar2=None, op0=ALU.mult), wait=[m_s2, m_theta, M["sinlast"]])
        wfree = [M["smT_free"][p]] if k == 0 else []
        m_t1 = trf_transpose(ps_trf[0:16, 0:128], dl_sb[p][:, k * 32:k * 32 + 16], 128, [m_dt])
        m_e1 = OP("dv", lambda E, p=p, k=k: E.tensor_copy(out=dtT_sb[p][:, k * 128:(k + 1) * 128], in_=ps_trf[0:16, 0:128]), wait=[m_t1] + wfree)
        BK["trf"].read(m_e1)
        m_t2_ = trf_transpose(ps_trf[0:16, 128:256], dl_sb[p][:, k * 32 + 16:k * 32 + 32], 128, [m_lm])
        m_e2 = OP("dv", lambda E, p=p, k=k: E.tensor_copy(out=lamT_sb[p][:, k * 128:(k + 1) * 128], in_=ps_trf[0:16, 128:256]), wait=[m_t2_])
        BK["trf"].read(m_e2)
        m_t3 = trf_transpose(ps_trf[0:32, 256:384], sm16[:, 0:32], 128, [m_tm])
        m_e3 = OP("dv", lambda E, p=p, k=k: E.tensor_copy(out=termsT_sb[:, k * 128:(k + 1) * 128], in_=ps_trf[0:32, 256:384]), wait=[m_t3])
        BK["trf"].read(m_e3)
        m_sc = OP("dv", lambda E, p=p, k=k: E.tensor_tensor_scan(
            out=lamt_sb[:, k * 128:(k + 1) * 128], data0=ones32[0:16, :], data1=dtT_sb[p][:, k * 128:(k + 1) * 128],
            initial=0.0, op0=ALU.mult, op1=ALU.add), wait=[m_e1])
        m_lam = OP("dv", lambda E, k=k: E.tensor_scalar(out=lamt_sb[:, k * 128:(k + 1) * 128], in0=lamt_sb[:, k * 128:(k + 1) * 128], scalar1=acol[:], scalar2=None, op0=ALU.mult), wait=[m_sc])
        m_lr = OP("dv", lambda E, k=k: E.tensor_scalar(
            out=lamrev_sb[:, k * 128:(k + 1) * 128], in0=lamt_sb[:, k * 128:(k + 1) * 128],
            scalar1=lamt_sb[:, k * 128 + 127:k * 128 + 128], scalar2=-1.0, op0=ALU.subtract, op1=ALU.mult), wait=[m_lam])
        wl = [M["lhl_free"][p]] if k == 0 else []
        m_hi = OP("dv", lambda E, p=p, k=k: E.tensor_copy(out=lhl[p][0:16, k * 128:(k + 1) * 128], in_=lamt_sb[:, k * 128:(k + 1) * 128]), wait=wl + [m_lam])
        m_lo = OP("dv", lambda E, p=p, k=k: E.tensor_tensor(out=lhl[p][32:48, k * 128:(k + 1) * 128], in0=lamt_sb[:, k * 128:(k + 1) * 128], in1=lhl[p][0:16, k * 128:(k + 1) * 128], op=ALU.subtract))
        M["lhl_m"][p][k] = m_lo
        wt = [M["toks_free"][p]] if k == 0 else []
        m_t4 = trf_transpose(ps_trf[:, 0:16], lamt_sb[:, k * 128:(k + 1) * 128], 16, [m_lam])
        m_e4 = OP("dv", lambda E, p=p, k=k: E.tensor_copy(out=toks[p][:, k * 48:k * 48 + 16], in_=ps_trf[:, 0:16]), wait=[m_t4] + wt)
        BK["trf"].read(m_e4)
        m_t5 = trf_transpose(ps_trf[:, 16:32], lamrev_sb[:, k * 128:(k + 1) * 128], 16, [m_lr])
        m_e5 = OP("dv", lambda E, p=p, k=k: E.tensor_copy(out=toks[p][:, k * 48 + 16:k * 48 + 32], in_=ps_trf[:, 16:32]), wait=[m_t5])
        BK["trf"].read(m_e5)

    def emit_phi_cs(i, k):
        p = i % 2
        init = phicarry[:] if k == 0 else phiT[:, k * 128 - 1:k * 128]
        wv = [s.mark("dv")]
        if k == 0 and M["pcarry"]:
            wv.append(M["pcarry"])
        if k > 0:
            wv.append(M["phiscan"])
        m_sc = OP("dv", lambda E, k=k, init=init: E.tensor_tensor_scan(
            out=phiT[:, k * 128:(k + 1) * 128], data0=ones32[:], data1=termsT_sb[:, k * 128:(k + 1) * 128],
            initial=init, op0=ALU.mult, op1=ALU.add), wait=wv)
        M["phiscan"] = m_sc
        if k == NCH - 1:
            M["pcarry"] = OP("dv", lambda E: E.tensor_copy(out=phicarry[:], in_=phiT[:, T - 1:T]), wait=[m_sc])
        m_tr = trf_transpose(ps_trf[:, 0:32], phiT[0:32, k * 128:(k + 1) * 128], 32, [m_sc])
        m_ev = OP("dv", lambda E, k=k: E.tensor_copy(out=phitok[:, k * 32:(k + 1) * 32], in_=ps_trf[:, 0:32]), wait=[m_tr])
        BK["trf"].read(m_ev)
        ph = phitok[:, k * 32:(k + 1) * 32]
        cb = cs_tile[:, k * 96:k * 96 + 96]
        OP("dv", lambda E, ph=ph: E.tensor_scalar(out=sm16[:, 0:32], in0=ph, scalar1=INV2PI, scalar2=None, op0=ALU.mult), wait=[m_ev, M["sinlast"]], sig=False)
        OP("dv", lambda E: E.tensor_scalar(out=sm16[:, 0:32], in0=sm16[:, 0:32], scalar1=MAGIC, scalar2=MAGIC, op0=ALU.add, op1=ALU.subtract), sig=False)
        OP("dv", lambda E, ph=ph: E.scalar_tensor_tensor(out=sm16[:, 0:32], in0=sm16[:, 0:32], scalar=-2.0 * PI, in1=ph, op0=ALU.mult, op1=ALU.add), sig=False)
        m_sr = OP("dv", lambda E: E.tensor_scalar(out=sm16[:, 0:32], in0=sm16[:, 0:32], scalar1=PI, scalar2=-PI, op0=ALU.min, op1=ALU.max))
        m_sin = OP("ac", lambda E, cb=cb: E.activation(out=cb[:, 32:64], in_=sm16[:, 0:32], func=AF.Sin), wait=[m_sr])
        M["sinlast"] = m_sin
        OP("dv", lambda E, ph=ph: E.tensor_scalar(out=sm16[:, 32:64], in0=ph, scalar1=INV2PI, scalar2=0.25, op0=ALU.mult, op1=ALU.add), wait=[M["coslast"]], sig=False)
        OP("dv", lambda E: E.tensor_scalar(out=sm16[:, 64:96], in0=sm16[:, 32:64], scalar1=MAGIC, scalar2=MAGIC, op0=ALU.add, op1=ALU.subtract), sig=False)
        OP("dv", lambda E: E.tensor_tensor(out=sm16[:, 64:96], in0=sm16[:, 32:64], in1=sm16[:, 64:96], op=ALU.subtract), sig=False)
        OP("dv", lambda E: E.tensor_scalar(out=sm16[:, 64:96], in0=sm16[:, 64:96], scalar1=2.0 * PI, scalar2=PI, op0=ALU.mult, op1=ALU.min), sig=False)
        m_cr = OP("dv", lambda E: E.tensor_scalar(out=sm16[:, 64:96], in0=sm16[:, 64:96], scalar1=-PI, scalar2=None, op0=ALU.max))
        m_cos = OP("ac", lambda E, cb=cb: E.activation(out=cb[:, 0:32], in_=sm16[:, 64:96], func=AF.Sin), wait=[m_cr])
        M["coslast"] = m_cos
        m_ns = OP("dv", lambda E, cb=cb: E.tensor_scalar(out=cb[:, 64:96], in0=cb[:, 32:64], scalar1=-1.0, scalar2=None, op0=ALU.mult), wait=[m_sin])
        return (m_sin, m_cos, m_ns)

    def _rep4(off, w):
        ap = cs_tile[:, off:off + w]
        return bass.AP(ap.tensor, ap.offset, [list(ap.ap[0]), [0, 4], [1, w]])

    def _swapview():
        ap = bctmp[:, 0:128]
        return bass.AP(ap.tensor, ap.offset + 32, [list(ap.ap[0]), [64, 2], [-32, 2], [1, 32]])

    def _negssview(off):
        ap = cs_tile[:, 0:128]
        return bass.AP(ap.tensor, ap.offset + off + 64, [list(ap.ap[0]), [0, 2], [-32, 2], [1, 32]])

    def emit_BC(i, k, m_cs):
        p = i % 2
        m_bcdl = M["bcdl_m"][k]
        Braw = bcdl[p][:, k * 160:k * 160 + 64]
        Craw = bcdl[p][:, k * 160 + 64:k * 160 + 128]
        OP("dv", lambda E, Braw=Braw: E.tensor_tensor(out=dgtmp[:], in0=Braw, in1=Braw, op=ALU.mult), wait=[m_bcdl], sig=False)
        OP("dv", lambda E: E.tensor_reduce(out=scal[:, 34:35], in_=dgtmp[:], axis=AX.X, op=ALU.add), sig=False)
        OP("dv", lambda E, Craw=Craw: E.tensor_tensor(out=dgtmp[:], in0=Craw, in1=Craw, op=ALU.mult), sig=False)
        OP("dv", lambda E: E.tensor_reduce(out=scal[:, 35:36], in_=dgtmp[:], axis=AX.X, op=ALU.add), sig=False)
        OP("dv", lambda E: E.tensor_scalar(out=scal[:, 36:37], in0=scal[:, 34:35], scalar1=1.0 / NS, scalar2=1e-6, op0=ALU.mult, op1=ALU.add), sig=False)
        m_r1a = OP("ac", lambda E: E.activation(out=scal[:, 36:37], in_=scal[:, 36:37], func=AF.Ln), wait=[m_r1w := s.mark("dv")])
        m_r2 = OP("ac", lambda E: E.activation(out=scal[:, 38:39], in_=scal[:, 36:37], func=AF.Exp, scale=-0.5))
        m_r3w = OP("dv", lambda E: E.tensor_scalar(out=scal[:, 37:38], in0=scal[:, 35:36], scalar1=1.0 / NS, scalar2=1e-6, op0=ALU.mult, op1=ALU.add))
        m_r3a = OP("ac", lambda E: E.activation(out=scal[:, 37:38], in_=scal[:, 37:38], func=AF.Ln), wait=[m_r3w])
        m_r4 = OP("ac", lambda E: E.activation(out=scal[:, 39:40], in_=scal[:, 37:38], func=AF.Exp, scale=-0.5))
        OP("dv", lambda E, Braw=Braw: E.tensor_scalar(out=bctmp[:, 0:64], in0=Braw, scalar1=scal[:, 38:39], scalar2=None, op0=ALU.mult), wait=[m_r2], sig=False)
        OP("dv", lambda E, Craw=Craw: E.tensor_scalar(out=bctmp[:, 64:128], in0=Craw, scalar1=scal[:, 39:40], scalar2=None, op0=ALU.mult), wait=[m_r4], sig=False)
        OP("dv", lambda E: E.tensor_tensor(out=ropeA[:], in0=bctmp[:], in1=br_bc[:], op=ALU.mult), sig=False)
        m_n4 = OP("dv", lambda E: E.tensor_tensor(out=bctmp[:], in0=ropeA[:], in1=br_bcb[:], op=ALU.add))
        m_T1 = OP("dv", lambda E, k=k: E.tensor_tensor(out=ropeA[:], in0=bctmp[:], in1=_rep4(k * 96, 32), op=ALU.mult), wait=[m_n4] + list(m_cs))
        m_T2 = OP("dv", lambda E, k=k: E.tensor_tensor(out=ropeB[:], in0=_swapview(), in1=_negssview(k * 96), op=ALU.mult))
        wbc = [M["BC_free"][p]] if k == 0 else []
        m_bc = OP("dv", lambda E, p=p, k=k: E.tensor_tensor(out=BCbf[p][:, k * 128:(k + 1) * 128], in0=ropeA[:], in1=ropeB[:], op=ALU.add), wait=wbc)
        wv = [m_bc] + BK["trb"].pre_write()
        m_tr = s.emit("pe", lambda E, p=p, k=k: E.transpose(ps_trb[:, 0:128], BCbf[p][:, k * 128:(k + 1) * 128], id_bf[:]), wait=wv, sig="pe")
        BK["trb"].post_write(m_tr)
        wct = [M["BCT_free"][p]] if k == 0 else []
        m_ev = OP("dv", lambda E, p=p, k=k: E.tensor_copy(out=BCT[p][:, k * 128:(k + 1) * 128], in_=ps_trb[:, 0:128]), wait=[m_tr] + wct)
        BK["trb"].read(m_ev)
        M["BCT_m"][p][k] = m_ev

    def emit_inproj(i):
        p = i % 2
        for mj in range(8):
            b = mj % 2
            m_wl = DMA("dW0" if b == 0 else "dW1", win_buf[b][:, 0:4096].rearrange("p (c m) -> p c m", c=8), d_winb[:, mj * 512:(mj + 1) * 512].rearrange("(c p) m -> p c m", p=128),
                       wait=[M["winread"][b]] + (m_winb if i == 0 else []))
            m_wb = DMA("dG0" if b == 0 else "dG1", win_buf[b][0:1, 4096:4608], d_bw[0:1, mj * 512:(mj + 1) * 512],
                       wait=[M["winread"][b]] + ([m_bw8st] if i == 0 else []))
            last_mm = None
            for k in range(NCH):
                cst, m_sa = STATS[(i, k)][0], STATS[(i, k)][1]
                bn = BK[f"mm{k % 2}"]
                wv = [m_wl, M["xinT_m"][k]] + bn.pre_write()
                for fc in range(8):
                    m_mmf = s.emit("pe", lambda E, k=k, fc=fc, b=b: E.matmul(
                        ps_mm[k % 2][:, 0:512], xinT[k][:, fc * 128:(fc + 1) * 128],
                        win_buf[b][:, fc * 512:(fc + 1) * 512], start=(fc == 0), stop=False),
                        wait=(wv if fc == 0 else ()), sig="pe" if fc == 7 else None)
                m_mm = s.emit("pe", lambda E, k=k, b=b: E.matmul(
                    ps_mm[k % 2][:, 0:512], ones1b[0:1, 0:128], win_buf[b][0:1, 4096:4608], start=False, stop=True),
                    wait=[m_wb] + ([m_initpo] if i == 0 and mj == 0 and k == 0 else []), sig="pe")
                bn.post_write(m_mm)
                last_mm = m_mm
                if mj == 7:
                    M["xinT_free"][k] = m_mmf
                if mj < 4:
                    dst = x_act[:, k * 2048 + mj * 512: k * 2048 + (mj + 1) * 512]
                    gate = M["scan_x"][k] or []
                else:
                    dst = siluz[:, k * 2048 + (mj - 4) * 512: k * 2048 + (mj - 3) * 512]
                    gate = M["scan_z"][k] or []
                m_ev = OP("ac", lambda E, dst=dst, k=k, cst=cst: E.activation(out=dst, in_=ps_mm[k % 2][:, 0:512], func=AF.Silu, scale=cst["sa"]),
                          wait=[m_mm, m_sa] + list(gate))
                bn.read(m_ev)
                if mj == 3:
                    M["xact_m"][k] = m_ev
                if mj == 7:
                    M["siluz_m"][k] = m_ev
            M["winread"][b] = last_mm

    def emit_A2(i):
        emit_t2T(i, 1)
        emit_m8(i, 1)
        if i + 1 < NTv:
            for k in range(NCH):
                emit_uload_stats_t2b(i + 1, k)
        emit_inproj(i)
        for k in range(NCH):
            emit_theta(i, k)
            emit_smalls_T(i, k)
            m_cs = emit_phi_cs(i, k)
            emit_BC(i, k, m_cs)
        p = i % 2
        M["bcdl_free"][p] = s.mark("dv")
        if i != 0:
            return
        SNAP("x_act", x_act[:], [s.mark("ac")])
        SNAP("siluz", siluz[:], [s.mark("ac")])
        SNAP("bcdl", bcdl[p][:], [s.mark("dv")])
        SNAP("dl", dl_sb[p][:], [s.mark("dv"), s.mark("ac")])
        SNAP("theta", theta_tok[:], [s.mark("dv")])
        SNAP("phitok", phitok[:], [s.mark("dv")])
        SNAP("cs", cs_tile[:], [s.mark("dv"), s.mark("ac")])
        SNAP("BCbf", BCbf[p][:], [s.mark("dv")])
        SNAP("lampk", lamt_sb[:], [s.mark("dv")])
        SNAP("toks_a", toks[p][:], [s.mark("dv")])
        SNAP("xinT0", xinT[0][:], [s.mark("dv")])
        SNAP("uA0", uA[0][:], [s.mark("dv")])
        SNAP("t2T0", t2T[0][:], [s.mark("ac")])
        SNAP("stats0", stats[0][:], [s.mark("dv"), s.mark("ac")])
        SNAP("scaleb", scale_big[:], [s.mark("dv")])
        SNAP("idbf_t", id_bf[:], [s.mark("dv")])
        SNAP("winb0", win_buf[0][:], [s.mark("pe")])

    def emit_F(j, is_last):
        q = j % 2
        m_w1 = OP("dv", lambda E, q=q: E.tensor_tensor(out=wT_sb[:], in0=dtT_sb[q][:], in1=lamT_sb[q][:], op=ALU.mult))
        OP("dv", lambda E, q=q: E.tensor_tensor(out=dtlT_sb[:], in0=dtT_sb[q][:], in1=wT_sb[:], op=ALU.subtract), sig=False)
        m_r1 = OP("dv", lambda E: E.tensor_tensor(out=rT_sb[:, 0:T - 1], in0=wT_sb[:, 0:T - 1], in1=dtlT_sb[:, 1:T], op=ALU.add))
        wpatch = [M["patch_zero"] if is_last else M["patch"]]
        m_r2 = OP("dv", lambda E: E.tensor_tensor(out=rT_sb[:, T - 1:T], in0=wT_sb[:, T - 1:T], in1=patchcol[:], op=ALU.add), wait=wpatch)
        m_rcl = OP("dv", lambda E: E.tensor_scalar(out=rT_sb[:], in0=rT_sb[:], scalar1=1e-38, scalar2=None, op0=ALU.max), wait=[m_r1, m_r2])
        m_ln = OP("ac", lambda E: E.activation(out=lnrT_sb[:], in_=rT_sb[:], func=AF.Ln), wait=[m_rcl])
        if j == 0:
            SNAP("fin1", wT_sb[:], [s.mark("dv")])
            SNAP("fin2", lnrT_sb[:], [s.mark("ac"), s.mark("dv")])

        for c in range(NCH):
            m_t = trf_transpose(ps_trf[:, 0:16], lnrT_sb[:, c * 128:(c + 1) * 128], 16, [m_ln])
            m_lt = OP("dv", lambda E, q=q, c=c: E.tensor_copy(out=toks[q][:, c * 48 + 32:c * 48 + 48], in_=ps_trf[:, 0:16]), wait=[m_t])
            BK["trf"].read(m_lt)
            m_st = OP("dv", lambda E, q=q, c=c: E.tensor_tensor(out=subtok[:, c * 16:(c + 1) * 16], in0=toks[q][:, c * 48:c * 48 + 16], in1=toks[q][:, c * 48 + 32:c * 48 + 48], op=ALU.subtract), wait=[m_lt])
            m_sd1 = OP("dv", lambda E, q=q, c=c: E.tensor_tensor(out=fsm[:], in0=toks[q][:, c * 48 + 16:c * 48 + 32], in1=toks[q][:, c * 48 + 32:c * 48 + 48], op=ALU.add))
            m_sd = OP("ac", lambda E, c=c: E.activation(out=sdec_sb[:, c * 16:(c + 1) * 16], in_=fsm[:], func=AF.Exp), wait=[m_sd1])
            m_wt = OP("dv", lambda E, q=q, c=c: E.tensor_tensor(out=wtok[:, c * 16:(c + 1) * 16], in0=dl_sb[q][:, c * 32:c * 32 + 16], in1=dl_sb[q][:, c * 32 + 16:c * 32 + 32], op=ALU.mult))
            OP("dv", lambda E, q=q, c=c: E.tensor_tensor(out=dgtmp[:], in0=BCbf[q][:, c * 128:c * 128 + 64], in1=BCbf[q][:, c * 128 + 64:c * 128 + 128], op=ALU.mult), sig=False)
            m_dg2 = OP("dv", lambda E: E.tensor_reduce(out=scal[:, 40:41], in_=dgtmp[:], axis=AX.X, op=ALU.add))
            m_dv = OP("dv", lambda E, c=c: E.scalar_tensor_tensor(out=dvec[:, c * 16:(c + 1) * 16], in0=wtok[:, c * 16:(c + 1) * 16], scalar=scal[:, 40:41], in1=br_D[:], op0=ALU.mult, op1=ALU.add), wait=[m_dg2, m_wt])
            m_c2a = OP("po", lambda E, q=q, c=c: E.tensor_copy(out=C2_sb[0:64, :], in_=BCT[q][64:128, c * 128:(c + 1) * 128]), wait=[M["C2_free"], M["BCT_m"][q][c]])
            m_c2b = OP("po", lambda E, q=q, c=c: E.tensor_copy(out=C2_sb[64:128, :], in_=BCT[q][64:128, c * 128:(c + 1) * 128]))
            wv = BK["trf"].pre_write()
            m_G = s.emit("pe", lambda E, q=q, c=c: E.matmul(ps_trf[:, 0:128], BCT[q][0:64, c * 128:(c + 1) * 128], C2_sb[0:64, :], start=True, stop=True), wait=wv + [m_c2a, M["BCT_m"][q][c]], sig="pe")
            BK["trf"].post_write(m_G)
            m_gm = OP("dv", lambda E: E.tensor_tensor(out=Gmask[:], in0=ps_trf[:, 0:128], in1=tri_sb[:], op=ALU.mult), wait=[m_G, M["Gmask_free"]])
            BK["trf"].read(m_gm)

            marks_yg = []
            PIPE = [None] * 8
            YGQ = [None] * 8

            def emit_ygtr(tp):
                g0, g1 = YGQ[tp]
                gx0, gx1 = (tp % 2) * 2, (tp % 2) * 2 + 1
                wv = ([] if tp % 4 else BK["trb"].pre_write())
                m_t0 = s.emit("pe", lambda E_, tp=tp, gx0=gx0: E_.transpose(ps_trb[:, (tp % 4) * 256:(tp % 4) * 256 + 128], yg_pool[gx0][:], id_bf[:]), wait=[g0] + wv, sig="pe")
                m_t1 = s.emit("pe", lambda E_, tp=tp, gx1=gx1: E_.transpose(ps_trb[:, (tp % 4) * 256 + 128:(tp % 4) * 256 + 256], yg_pool[gx1][:], id_bf[:]), wait=[g1], sig="pe")
                BK["trb"].post_write(m_t1)
                M["ygfree"][gx0] = m_t0
                M["ygfree"][gx1] = m_t1
                if tp in (3, 7):
                    half = 0 if tp == 3 else 1
                    wyt = [M["ygTfree"]] if tp == 3 else []
                    m_evy = OP("dv", lambda E_, c=c, half=half: E_.tensor_copy(out=ygT[:, half * 1024:(half + 1) * 1024], in_=ps_trb[:]), wait=[m_t1] + wyt)
                    BK["trb"].read(m_evy)
                    marks_yg.append(m_evy)

            def emit_burst(pr):
                bk = pr % 2
                bbn = BK[f"br{bk}"]
                h0, h1 = 2 * pr, 2 * pr + 1
                wb = bbn.pre_write()
                m_b0 = s.emit("pe", lambda E, q=q, c=c, h0=h0, bk=bk: E.matmul(ps_br[bk][:, 0:128], oh48_sb[:, h0 * 128:(h0 + 1) * 128], lhl[q][:, c * 128:(c + 1) * 128], start=True, stop=True), wait=wb + [M["lhl_m"][q][c]], sig="pe")
                m_b1 = s.emit("pe", lambda E, q=q, c=c, h1=h1, bk=bk: E.matmul(ps_br[bk][:, 128:256], oh48_sb[:, h1 * 128:(h1 + 1) * 128], lhl[q][:, c * 128:(c + 1) * 128], start=True, stop=True), sig="pe")
                m_b2 = s.emit("pe", lambda E, q=q, c=c, pr=pr, bk=bk: E.matmul(ps_br[bk][:, 256:384], pp48_sb[:, pr * 128:(pr + 1) * 128], lhl[q][:, c * 128:(c + 1) * 128], start=True, stop=True), sig="pe")
                bbn.post_write(m_b2)
                return dict(bk=bk, bbn=bbn, h0=h0, h1=h1, m_b0=m_b0, m_b1=m_b1, m_b2=m_b2)

            def emit_chain(pr, P):
                bk, bbn, h0, h1 = P["bk"], P["bbn"], P["h0"], P["h1"]
                m_Mt = [None, None]
                m_E = [None, None]
                for e_ix, (hh, m_bb) in enumerate([(h0, P["m_b0"]), (h1, P["m_b1"])]):
                    r_ix = (pr * 2 + e_ix) % 4
                    m_E[e_ix] = OP("ac", lambda E_, c=c, hh=hh, bk=bk, e_ix=e_ix, r_ix=r_ix: E_.activation(
                        out=nE_sb[r_ix][:], in_=ps_br[bk][:, e_ix * 128:(e_ix + 1) * 128], func=AF.Relu,
                        scale=-1.0, bias=subtok[:, c * 16 + hh:c * 16 + hh + 1]),
                        wait=[P["m_b2"], m_st, M["E_free"][r_ix]])
                    bbn.read(m_E[e_ix])
                    m_x = OP("ac", lambda E_, r_ix=r_ix: E_.activation(out=expE[r_ix][:], in_=nE_sb[r_ix][:], func=AF.Exp, scale=-1.0), wait=[M["expE_free"][r_ix]])
                    M["E_free"][r_ix] = m_x
                    m_Mt[e_ix] = OP("dv", lambda E_, e_ix=e_ix, r_ix=r_ix: E_.tensor_tensor(out=Mt_sb[r_ix][:], in0=expE[r_ix][:], in1=Gmask[:], op=ALU.mult), wait=[m_x, m_gm, M["Mt_free"][r_ix]])
                    M["expE_free"][r_ix] = m_Mt[e_ix]
                m_ed = OP("ac", lambda E_, bk=bk: E_.activation(out=edec2[bk][:], in_=ps_br[bk][:, 256:384], func=AF.Exp), wait=[P["m_b2"], m_E[0], m_E[1], M["edec2_free"][bk]])
                bbn.read(m_ed)
                m_cd = OP("po", lambda E_, bk=bk: E_.tensor_tensor(out=Cdec2[bk][:], in0=C2_sb[:], in1=edec2[bk][:], op=ALU.mult), wait=[m_ed, m_c2b, M["Cdec2_free"][bk]])
                m_Bw = [None, None]
                for e_ix, hh in enumerate([h0, h1]):
                    bw = bk * 2 + e_ix
                    m_Bw[e_ix] = OP("dv", lambda E_, q=q, c=c, hh=hh, bw=bw: E_.tensor_scalar(
                        out=Bw_sb[bw][:], in0=BCbf[q][:, c * 128:c * 128 + 64],
                        scalar1=sdec_sb[:, c * 16 + hh:c * 16 + hh + 1], scalar2=None, op0=ALU.mult), wait=[m_sd, M["Bw_free"][bw]])
                P.update(m_Mt=m_Mt, m_ed=m_ed, m_cd=m_cd, m_Bw=m_Bw)

            for pp in (0, 1):
                PIPE[pp] = emit_burst(pp)
                emit_chain(pp, PIPE[pp])

            for pr in range(8):
                P = PIPE[pr]
                bk = P["bk"]
                ybn = BK[f"Y{bk}"]
                h0, h1 = P["h0"], P["h1"]
                m_Mt, m_ed, m_cd, m_Bw = P["m_Mt"], P["m_ed"], P["m_cd"], P["m_Bw"]
                wy = ybn.pre_write()
                r0, r1 = (pr * 2) % 4, (pr * 2 + 1) % 4
                bw0, bw1 = bk * 2, bk * 2 + 1
                m_y0a = s.emit("pe", lambda E_, c=c, h0=h0, bk=bk, r0=r0: E_.matmul(ps_Y[bk][:, 0:128], Mt_sb[r0][:], x_act[:, c * 2048 + h0 * 128:c * 2048 + (h0 + 1) * 128], start=True, stop=False), wait=[m_Mt[0], M["xact_m"][c]] + wy, sig="pe")
                m_y0b = s.emit("pe", lambda E_, bk=bk, pr=pr: E_.matmul(ps_Y[bk][:, 0:128], Cdec2[bk][0:64, :], H2b[pr][0:64, :], start=False, stop=True), wait=[m_cd], sig="pe")
                m_y1a = s.emit("pe", lambda E_, c=c, h1=h1, bk=bk, r1=r1: E_.matmul(ps_Y[bk][:, 128:256], Mt_sb[r1][:], x_act[:, c * 2048 + h1 * 128:c * 2048 + (h1 + 1) * 128], start=True, stop=False), wait=[m_Mt[1]], sig="pe")
                m_y1b = s.emit("pe", lambda E_, bk=bk, pr=pr: E_.matmul(ps_Y[bk][:, 128:256], Cdec2[bk][64:128, :], H2b[pr][64:128, :], start=False, stop=True), sig="pe")
                m_s0 = s.emit("pe", lambda E_, c=c, h0=h0, bk=bk, bw0=bw0: E_.matmul(ps_Y[bk][0:64, 256:384], Bw_sb[bw0][:], x_act[:, c * 2048 + h0 * 128:c * 2048 + (h0 + 1) * 128], start=True, stop=True), wait=[m_Bw[0]], sig="pe")
                m_s1 = s.emit("pe", lambda E_, c=c, h1=h1, bk=bk, bw1=bw1: E_.matmul(ps_Y[bk][64:128, 256:384], Bw_sb[bw1][:], x_act[:, c * 2048 + h1 * 128:c * 2048 + (h1 + 1) * 128], start=True, stop=True), wait=[m_Bw[1]], sig="pe")
                ybn.post_write(m_s1)
                M["Mt_free"][r0] = m_y0a
                M["Mt_free"][r1] = m_y1a
                M["Bw_free"][bw0] = m_s0
                M["Bw_free"][bw1] = m_s1
                M["Cdec2_free"][bk] = m_y1b
                m_yd = [None, None]
                for e_ix, (hh, m_yb) in enumerate([(h0, m_y0b), (h1, m_y1b)]):
                    m_yd[e_ix] = OP("dv", lambda E_, c=c, hh=hh, bk=bk, e_ix=e_ix: E_.scalar_tensor_tensor(
                        out=yd_sb[e_ix][:], in0=x_act[:, c * 2048 + hh * 128:c * 2048 + (hh + 1) * 128],
                        scalar=dvec[:, c * 16 + hh:c * 16 + hh + 1], in1=ps_Y[bk][:, e_ix * 128:(e_ix + 1) * 128],
                        op0=ALU.mult, op1=ALU.add), wait=[m_s1, m_dv, M["ydfree"][e_ix]])
                    ybn.read(m_yd[e_ix])
                m_h = OP("dv", lambda E_, pr=pr, bk=bk: E_.scalar_tensor_tensor(
                    out=H2f[pr][:], in0=H2f[pr][:], scalar=edec2[bk][:, 127:128],
                    in1=ps_Y[bk][:, 256:384], op0=ALU.mult, op1=ALU.add), wait=[m_s0, m_s1, m_ed])
                ybn.read(m_h)
                M["edec2_free"][bk] = m_h
                OP("po", lambda E_, pr=pr: E_.tensor_copy(out=H2b[pr][:], in_=H2f[pr][:]), wait=[m_h, m_y0b, m_y1b], sig=False)
                m_yg = [None, None]
                for e_ix, hh in enumerate([h0, h1]):
                    gx = (pr % 2) * 2 + e_ix
                    m_yg[e_ix] = OP("po", lambda E_, c=c, hh=hh, e_ix=e_ix, gx=gx: E_.tensor_tensor(
                        out=yg_pool[gx][:], in0=yd_sb[e_ix][:],
                        in1=siluz[:, c * 2048 + hh * 128:c * 2048 + (hh + 1) * 128], op=ALU.mult), wait=[m_yd[e_ix], M["ygfree"][gx], M["siluz_m"][c]])
                M["ydfree"] = m_yg[:]
                YGQ[pr] = m_yg
                if pr >= 2:
                    emit_ygtr(pr - 2)
                if pr + 2 < 8:
                    PIPE[pr + 2] = emit_burst(pr + 2)
                    emit_chain(pr + 2, PIPE[pr + 2])
                if TAIL_DEFER:
                    for _ in range(TAIL_POPS.get(pr, 0)):
                        if M["tailq"]:
                            M["tailq"].popleft()()
            emit_ygtr(6)
            emit_ygtr(7)
            M["scan_x"][c] = [s.mark("dv"), s.mark("pe")]
            M["scan_z"][c] = [s.mark("po")]
            M["Gmask_free"] = s.mark("po")
            M["C2_free"] = s.mark("po")
            if j == 0 and c == 0:
                SNAP("Gmask", Gmask[:], [s.mark("dv")])
                SNAP("subtok", subtok[:, 0:16], [s.mark("dv")])
                SNAP("sdec", sdec_sb[:, 0:16], [s.mark("ac")])
                SNAP("dvec", dvec[:, 0:16], [s.mark("dv")])

            def make_stages(j=j, c=c, marks_yg=list(marks_yg), q=q):
                tx = {}

                def st0():
                    tx["uF"] = DMA("dV0" if c % 2 == 0 else "dV1", uF[c % 2][:], d_u[j * T + c * 128:j * T + (c + 1) * 128, :], wait=[M["uFfree"][c % 2]])
                    bn = BK["mm0"]
                    wv = marks_yg + [m_wout] + bn.pre_write()
                    for pc in range(16):
                        m_mm = s.emit("pe", lambda E_, c=c, pc=pc: E_.matmul(
                            ps_mm[0][:, 0:512], ygT[:, pc * 128:(pc + 1) * 128],
                            wout_sb[:, pc * 1024 + 0 * 512:pc * 1024 + 1 * 512],
                            start=(pc == 0), stop=(pc == 15)),
                            wait=(wv if pc == 0 else ()), sig="pe" if pc == 15 else None)
                    bn.post_write(m_mm)
                    tx["op0"] = m_mm

                def st1():
                    bn = BK["mm1"]
                    wv = marks_yg + bn.pre_write()
                    for pc in range(16):
                        m_mm = s.emit("pe", lambda E_, c=c, pc=pc: E_.matmul(
                            ps_mm[1][:, 0:512], ygT[:, pc * 128:(pc + 1) * 128],
                            wout_sb[:, pc * 1024 + 1 * 512:pc * 1024 + 2 * 512],
                            start=(pc == 0), stop=(pc == 15)),
                            wait=(wv if pc == 0 else ()), sig="pe" if pc == 15 else None)
                    bn.post_write(m_mm)
                    tx["op1"] = m_mm
                    M["ygTfree"] = m_mm

                def st2():
                    usrc = uF[c % 2][:]
                    m_l1 = OP("dv", lambda E_, usrc=usrc: E_.tensor_scalar(out=c4_sb[:], in0=usrc, scalar1=scal[:, 48:49], scalar2=scal[:, 49:50], op0=ALU.subtract, op1=ALU.mult), wait=[tx["uF"]])
                    m_mx = [None, None]
                    for mj in range(2):
                        wv = [tx["op0" if mj == 0 else "op1"], m_l1] + ([M["mixfree"]] if mj == 0 else [])
                        m_mx[mj] = OP("dv", lambda E_, c=c, mj=mj: E_.scalar_tensor_tensor(
                            out=mix_sb[:, mj * 512:(mj + 1) * 512], in0=c4_sb[:, mj * 512:(mj + 1) * 512],
                            scalar=1.0, in1=ps_mm[mj][:, 0:512], op0=ALU.mult, op1=ALU.add), wait=wv)
                        BK[f"mm{mj}"].read(m_mx[mj])
                    tx["mx"] = m_mx
                    tx["usrc"] = usrc

                def st3():
                    m_mx = tx["mx"]
                    mixv = mix_sb[:]
                    OP("dv", lambda E_, mixv=mixv: E_.tensor_reduce(out=scal[:, 41:42], in_=mixv, axis=AX.X, op=ALU.add), wait=[m_mx[0], m_mx[1]], sig=False)
                    m_s3b = OP("ac", lambda E_, mixv=mixv: E_.activation(out=sq_scr[:], in_=mixv, func=AF.Square, accum_out=scal[:, 42:43]), wait=[m_mx[0], m_mx[1]])
                    OP("dv", lambda E_: E_.tensor_scalar(out=scal[:, 43:44], in0=scal[:, 41:42], scalar1=1.0 / DM, scalar2=None, op0=ALU.mult), wait=[m_s3b], sig=False)
                    OP("dv", lambda E_: E_.tensor_scalar(out=scal[:, 44:45], in0=scal[:, 42:43], scalar1=1.0 / DM, scalar2=None, op0=ALU.mult), sig=False)
                    OP("dv", lambda E_: E_.tensor_tensor(out=scal[:, 45:46], in0=scal[:, 43:44], in1=scal[:, 43:44], op=ALU.mult), sig=False)
                    OP("dv", lambda E_: E_.tensor_tensor(out=scal[:, 44:45], in0=scal[:, 44:45], in1=scal[:, 45:46], op=ALU.subtract), sig=False)
                    OP("dv", lambda E_: E_.tensor_scalar(out=scal[:, 44:45], in0=scal[:, 44:45], scalar1=1.0, scalar2=1e-5, op0=ALU.mult, op1=ALU.add), sig=False)
                    m_rc = OP("ac", lambda E_: E_.activation(out=scal[:, 45:46], in_=scal[:, 44:45], func=AF.Ln), wait=[s.mark("dv")])
                    m_r3 = OP("ac", lambda E_: E_.activation(out=scal[:, 46:47], in_=scal[:, 45:46], func=AF.Exp, scale=-0.5))
                    m_mr = OP("dv", lambda E_: E_.tensor_tensor(out=scal[:, 47:48], in0=scal[:, 43:44], in1=scal[:, 46:47], op=ALU.mult), wait=[m_r3])
                    m_x2a = OP("dv", lambda E_, mixv=mixv: E_.tensor_scalar(out=sq_scrF[:], in0=mixv, scalar1=scal[:, 46:47], scalar2=scal[:, 47:48], op0=ALU.mult, op1=ALU.subtract), wait=[m_mr, M["x2free"]])
                    M["mixfree"] = m_x2a
                    m_x2 = OP("dv", lambda E_, c=c: E_.tensor_tensor(out=x2_sb[:], in0=sq_scrF[:], in1=tx["usrc"], op=ALU.add))
                    M["uFfree"][c % 2] = m_x2
                    tx["x2"] = m_x2
                    tx["x2a"] = m_x2a

                def st4():
                    x2v = x2_sb[:]
                    m_x2 = tx["x2"]
                    OP("dv", lambda E_, x2v=x2v: E_.tensor_reduce(out=qst[:, 0:1], in_=x2v, axis=AX.X, op=ALU.add), wait=[m_x2], sig=False)
                    m_f2 = OP("ac", lambda E_, x2v=x2v: E_.activation(out=sq_scr[:], in_=x2v, func=AF.Square, accum_out=qst[:, 1:2]), wait=[m_x2, tx["x2a"]])
                    OP("dv", lambda E_: E_.tensor_scalar(out=qst[:, 2:3], in0=qst[:, 0:1], scalar1=1.0 / DM, scalar2=None, op0=ALU.mult), wait=[m_f2], sig=False)
                    OP("dv", lambda E_: E_.tensor_scalar(out=qst[:, 3:4], in0=qst[:, 1:2], scalar1=1.0 / DM, scalar2=None, op0=ALU.mult), sig=False)
                    OP("dv", lambda E_: E_.tensor_tensor(out=qst[:, 4:5], in0=qst[:, 2:3], in1=qst[:, 2:3], op=ALU.mult), sig=False)
                    OP("dv", lambda E_: E_.tensor_tensor(out=qst[:, 3:4], in0=qst[:, 3:4], in1=qst[:, 4:5], op=ALU.subtract), sig=False)
                    OP("dv", lambda E_: E_.tensor_scalar(out=qst[:, 3:4], in0=qst[:, 3:4], scalar1=1.0, scalar2=1e-5, op0=ALU.mult, op1=ALU.add), sig=False)
                    m_f3 = OP("ac", lambda E_: E_.activation(out=qst[:, 4:5], in_=qst[:, 3:4], func=AF.Ln), wait=[s.mark("dv")])
                    m_f4 = OP("ac", lambda E_: E_.activation(out=qst[:, 5:6], in_=qst[:, 4:5], func=AF.Exp, scale=-0.5))
                    m_c4 = OP("dv", lambda E_, x2v=x2v: E_.tensor_scalar(out=c4_sb[:], in0=x2v, scalar1=qst[:, 2:3], scalar2=None, op0=ALU.subtract), wait=[m_f2])
                    m_am = OP("dv", lambda E_: E_.tensor_reduce(out=qst[:, 6:7], in_=c4_sb[:], axis=AX.X, op=ALU.max, apply_absolute_value=True))
                    OP("dv", lambda E_: E_.tensor_tensor(out=qst[:, 7:8], in0=qst[:, 5:6], in1=qst[:, 6:7], op=ALU.mult), wait=[m_f4, m_am], sig=False)
                    OP("dv", lambda E_: E_.tensor_scalar(out=qst[:, 7:8], in0=qst[:, 7:8], scalar1=1e-8, scalar2=None, op0=ALU.max), sig=False)
                    OP("dv", lambda E_: E_.reciprocal(out=qst[:, 8:9], in_=qst[:, 7:8]), sig=False)
                    OP("dv", lambda E_: E_.tensor_scalar(out=qst[:, 9:10], in0=qst[:, 5:6], scalar1=127.0, scalar2=None, op0=ALU.mult), sig=False)
                    OP("dv", lambda E_: E_.tensor_tensor(out=qst[:, 10:11], in0=qst[:, 8:9], in1=qst[:, 9:10], op=ALU.mult), sig=False)
                    m_xs = OP("dv", lambda E_: E_.tensor_scalar(out=qst[:, 11:12], in0=qst[:, 7:8], scalar1=1.0 / 127.0, scalar2=None, op0=ALU.mult))
                    OP("dv", lambda E_: E_.tensor_scalar(out=sq_scrF[:], in0=c4_sb[:], scalar1=qst[:, 10:11], scalar2=MAGIC, op0=ALU.mult, op1=ALU.add), wait=[m_xs], sig=False)
                    m_q2 = OP("dv", lambda E_: E_.tensor_scalar(out=xq_tok[:], in0=sq_scrF[:], scalar1=MAGIC, scalar2=None, op0=ALU.subtract))
                    tx["q2"] = m_q2
                    tx["xs"] = m_xs

                def st5():
                    wv = [tx["q2"]] + BK["trb"].pre_write()
                    for fc in range(8):
                        m_tq = s.emit("pe", lambda E_, fc=fc: E_.transpose(ps_trb[:, fc * 128:(fc + 1) * 128], xq_tok[:, fc * 128:(fc + 1) * 128], id_bf[:]),
                                      wait=(wv if fc == 0 else ()), sig="pe" if fc == 7 else None)
                    BK["trb"].post_write(m_tq)
                    m_eq = OP("dv", lambda E_, c=c: E_.tensor_copy(out=xqT[:], in_=ps_trb[:]), wait=[m_tq, M["xqTfree"]])
                    BK["trb"].read(m_eq)
                    tx["eq"] = m_eq

                def st6():
                    m_fmm = [None, None]
                    for oj in range(2):
                        bn = BK[f"mm{oj}"]
                        wv = [tx["eq"]] + bn.pre_write()
                        for fc in range(8):
                            m_mm = s.emit("pe", lambda E_, c=c, oj=oj, fc=fc: E_.matmul(
                                ps_mm[oj][:, 0:512], xqT[:, fc * 128:(fc + 1) * 128],
                                wffn_sb[:, fc * 1024 + oj * 512:fc * 1024 + (oj + 1) * 512],
                                start=(fc == 0), stop=(fc == 7)),
                                wait=(wv if fc == 0 else ()), sig="pe" if fc == 7 else None)
                        bn.post_write(m_mm)
                        m_fmm[oj] = m_mm
                    M["xqTfree"] = m_fmm[1]
                    m_fe = [None, None]
                    for oj in range(2):
                        m_fe[oj] = OP("ac", lambda E_, oj=oj: E_.activation(out=sq_scrF[:, oj * 512:(oj + 1) * 512], in_=ps_mm[oj][:, 0:512], func=AF.Copy, scale=qst[:, 11:12]), wait=[m_fmm[oj], tx["xs"]])
                        BK[f"mm{oj}"].read(m_fe[oj])
                    tx["fe"] = m_fe

                def st7():
                    m_fe = tx["fe"]
                    m_p1 = OP("po", lambda E_, c=c: E_.tensor_tensor(out=ffn_sb[:], in0=sq_scrF[:], in1=br_ws[:], op=ALU.mult), wait=[m_fe[0], m_fe[1], M["ffnfree"]])
                    m_p2 = OP("po", lambda E_, c=c: E_.tensor_tensor(out=ffn_sb[:], in0=ffn_sb[:], in1=br_fb[:], op=ALU.add))
                    fv = ffn_sb[:]
                    OP("dv", lambda E_, fv=fv: E_.tensor_reduce(out=qst[:, 12:13], in_=fv, axis=AX.X, op=ALU.add), wait=[m_p2], sig=False)
                    m_s5b = OP("ac", lambda E_, fv=fv: E_.activation(out=sq_scr[:], in_=fv, func=AF.Square, accum_out=qst[:, 13:14]), wait=[m_p2])
                    OP("dv", lambda E_: E_.tensor_scalar(out=qst[:, 12:13], in0=qst[:, 12:13], scalar1=1.0 / DM, scalar2=None, op0=ALU.mult), wait=[m_s5b], sig=False)
                    OP("dv", lambda E_: E_.tensor_scalar(out=qst[:, 13:14], in0=qst[:, 13:14], scalar1=1.0 / DM, scalar2=None, op0=ALU.mult), sig=False)
                    OP("dv", lambda E_: E_.tensor_tensor(out=qst[:, 14:15], in0=qst[:, 12:13], in1=qst[:, 12:13], op=ALU.mult), sig=False)
                    OP("dv", lambda E_: E_.tensor_tensor(out=qst[:, 13:14], in0=qst[:, 13:14], in1=qst[:, 14:15], op=ALU.subtract), sig=False)
                    OP("dv", lambda E_: E_.tensor_scalar(out=qst[:, 13:14], in0=qst[:, 13:14], scalar1=1.0, scalar2=1e-5, op0=ALU.mult, op1=ALU.add), sig=False)
                    m_r5 = OP("ac", lambda E_: E_.activation(out=qst[:, 14:15], in_=qst[:, 13:14], func=AF.Ln), wait=[s.mark("dv")])
                    m_sq5 = OP("ac", lambda E_: E_.activation(out=qst[:, 15:16], in_=qst[:, 14:15], func=AF.Exp, scale=-0.5))
                    m_m5 = OP("dv", lambda E_: E_.tensor_tensor(out=qst[:, 14:15], in0=qst[:, 12:13], in1=qst[:, 15:16], op=ALU.mult), wait=[m_sq5])
                    m_o1 = OP("dv", lambda E_, fv=fv: E_.tensor_scalar(out=sq_scrF[:], in0=fv, scalar1=qst[:, 15:16], scalar2=qst[:, 14:15], op0=ALU.mult, op1=ALU.subtract), wait=[m_m5])
                    M["ffnfree"] = m_o1
                    m_o2 = OP("dv", lambda E_, c=c: E_.tensor_tensor(out=x2_sb[:], in0=sq_scrF[:], in1=x2_sb[:], op=ALU.add), wait=[M["outst"]])
                    m_st_ = DMA("dO", d_out[j * T + c * 128:j * T + (c + 1) * 128, :], x2_sb[:], wait=[m_o2])
                    M["x2free"] = m_st_
                    M["outst"] = m_st_

                return [st0, st1, st2, st3, st4, st5, st6, st7]

            cst = STATS[(j, c)][0]
            OP("dv", lambda E_, cst=cst: E_.tensor_copy(out=scal[:, 48:49], in_=cst["mu1"]), sig=False)
            OP("dv", lambda E_, cst=cst: E_.tensor_copy(out=scal[:, 49:50], in_=cst["rstd1"]), sig=False)
            for stg in make_stages():
                M["tailq"].append(stg)
            if not TAIL_DEFER:
                while M["tailq"]:
                    M["tailq"].popleft()()
        M["stats_free"][q] = s.mark("dv")
        M["dl_free"][q] = s.mark("dv")
        M["smT_free"][q] = s.mark("dv")
        M["toks_free"][q] = s.mark("dv")
        M["BC_free"][q] = s.mark("dv")
        M["BCT_free"][q] = [s.mark("pe"), s.mark("po")]
        M["lhl_free"][q] = s.mark("pe")

    NTv = globals().get('NT_OVERRIDE', NT)
    emit_uload_stats_t2b(0, 0, extra=m_winb + [m_consts, m_thw, m_wout, m_acol, m_initpo, m_brws, m_oh8, m_win8])
    emit_uload_stats_t2b(0, 1)
    for i in range(NTv + 1):
        if i < NTv:
            emit_A1(i)
        if i >= 1:
            if i == NT:
                M["patch_zero"] = OP("dv", lambda E_: E_.memset(patchcol[:], 0.0), wait=[M["patch"]])
            emit_F(i - 1, is_last=(i == NT))
        if i < NTv:
            emit_A2(i)

    while M["tailq"]:
        M["tailq"].popleft()()
    final_waits = [(ch, s.cnt[ch]) for ch in DMA_CH]
    s.emit("sy", lambda E_: None, wait=final_waits, sig=None)

    with nc.Block("init") as b0:
        b0.sync(lambda eng: s.run(0, "sy", eng))
        b0.tensor(lambda eng: s.run(0, "pe", eng))
        b0.vector(lambda eng: s.run(0, "dv", eng))
        b0.scalar(lambda eng: s.run(0, "ac", eng))
        b0.gpsimd(lambda eng: s.run(0, "po", eng))
    with nc.Block("main") as b1:
        b1.sync(lambda eng: s.run(1, "sy", eng))
        b1.tensor(lambda eng: s.run(1, "pe", eng))
        b1.vector(lambda eng: s.run(1, "dv", eng))
        b1.scalar(lambda eng: s.run(1, "ac", eng))
        b1.gpsimd(lambda eng: s.run(1, "po", eng))

    st.close()
    return nc


def make_consts():
    bf = ml_dtypes.bfloat16
    idbf = np.eye(128, dtype=bf)
    idf = np.eye(128, dtype=np.float32)
    tri = (np.arange(128)[:, None] < np.arange(128)[None, :]).astype(np.float32)
    oh48 = np.zeros((48, NH * 128), dtype=bf)
    for h in range(NH):
        oh48[h, h * 128:(h + 1) * 128] = 1
        oh48[32 + h, h * 128:(h + 1) * 128] = 1
    pp48 = np.zeros((48, 8 * 128), dtype=bf)
    for pr in range(8):
        h0, h1 = 2 * pr, 2 * pr + 1
        pp48[h0, pr * 128:pr * 128 + 64] = 1
        pp48[32 + h0, pr * 128:pr * 128 + 64] = 1
        pp48[h1, pr * 128 + 64:pr * 128 + 128] = 1
        pp48[32 + h1, pr * 128 + 64:pr * 128 + 128] = 1
    oh8 = np.zeros((8, 8 * 128), dtype=np.float32)
    for ocb in range(8):
        oh8[ocb, ocb * 128:(ocb + 1) * 128] = 1
    return dict(idbf=idbf, idf=idf, tri=tri, oh48=oh48, pp48=pp48, oh8=oh8)


def make_inputs(inputs_dict):
    g = lambda k: np.asarray(inputs_dict[k], np.float32)
    consts = make_consts()
    scale = g("L0_attn_norm_scale")
    biasv = g("L0_attn_norm_bias")
    scale_big = np.ascontiguousarray(np.broadcast_to(scale.reshape(8, 128).T[:, :, None], (128, 8, 128)).reshape(128, DM)).astype(ml_dtypes.bfloat16)
    bcol = np.ascontiguousarray(biasv.reshape(8, 128).T).astype(ml_dtypes.bfloat16)
    common = {
        "w_in": g("L0_attn_in_proj_w"), "w_th": g("L0_attn_theta_w"),
        "w_out": g("L0_attn_out_proj_w"), "w_ffn": g("L0_ffn_weight"),
        "idbf": consts["idbf"], "idf": consts["idf"], "tri": consts["tri"],
        "oh48": consts["oh48"], "pp48": consts["pp48"], "oh8": consts["oh8"],
        "scale_big": scale_big, "bcol": bcol,
        "br_dtb": np.ascontiguousarray(np.tile(g("L0_attn_dt_bias")[None, :], (128, 1))),
        "br_D": np.ascontiguousarray(np.tile(g("L0_attn_D")[None, :], (128, 1))),
        "br_bc": np.ascontiguousarray(np.tile(np.concatenate([g("L0_attn_bcB_scale"), g("L0_attn_bcC_scale")])[None, :], (128, 1))),
        "br_bcb": np.ascontiguousarray(np.tile(np.concatenate([g("L0_attn_B_bias"), g("L0_attn_C_bias")])[None, :], (128, 1))),
        "br_fb": np.ascontiguousarray(np.tile(g("L0_ffn_bias")[None, :], (128, 1))),
        "alog": g("L0_attn_A_log").reshape(NH, 1),
    }
    u = g("L0_attn_u")
    return [dict(common, u=np.ascontiguousarray(u[b])) for b in range(8)]


def kernel(**inputs):
    nc = build_program()
    in_maps = make_inputs(inputs)
    res = run_bass_kernel_spmd(nc, in_maps, core_ids=list(range(8)))
    return np.stack([res.results[b]["out"] for b in range(8)], axis=0)


# version 11
# speedup vs baseline: 1.0417x; 1.0263x over previous
import collections
import contextlib
import math
import numpy as np
import ml_dtypes

import concourse.bass as bass
import concourse.mybir as mybir
from concourse.bass_utils import run_bass_kernel_spmd

F32 = mybir.dt.float32
BF16 = mybir.dt.bfloat16
AF = mybir.ActivationFunctionType
ALU = mybir.AluOpType
AX = mybir.AxisListType

L, DM, DI, NH, HP, NS, PROJ = 4096, 1024, 2048, 16, 128, 64, 4256
T, Q = 256, 128
NCH = T // 128
NT = L // T
TAIL_DEFER = True
TAIL_POPS = {0: 1, 1: 1, 2: 1, 3: 1, 4: 1, 5: 1, 6: 1, 7: 1}
MAGIC = 1.5 * 2.0 ** 23
PI = math.pi
INV2PI = 1.0 / (2.0 * math.pi)

ENGS = ("sy", "pe", "dv", "ac", "po")
DMA_CH = ("dC", "dW8", "dW0", "dW1", "dU0", "dU1", "dV0", "dV1", "dO",
          "dF0", "dF1", "dT", "dWo", "dG0", "dG1", "dDB")
PO_CH = {"dT", "dWo", "dG0", "dG1"}


def _maxm(marks):
    best = {}
    stack = list(marks)
    while stack:
        m = stack.pop()
        if m is None:
            continue
        if isinstance(m, list):
            stack.extend(m)
            continue
        s_, v = m
        if v and (s_ not in best or v > best[s_]):
            best[s_] = v
    return [(k, v) for k, v in best.items()]


class Sched:
    def __init__(self, nc):
        self.nc = nc
        self.q = {0: {e: [] for e in ENGS}, 1: {e: [] for e in ENGS}}
        self.cnt = {}
        self.sems = {}
        self.phase = 0

    def new_sem(self, name, st):
        self.sems[name] = st.enter_context(self.nc.semaphore(name))
        self.cnt[name] = 0

    SERIAL = ("dv", "ac", "po")

    def emit(self, eng, fn, wait=(), sig=None):
        wait = list(wait)
        if eng in self.SERIAL:
            wait.append((eng, self.cnt[eng]))
            if sig is None or sig == eng:
                sig = eng
        markv = None
        inc = 0
        if sig is not None:
            inc = 16 if sig in DMA_CH else 1
            self.cnt[sig] += inc
            markv = self.cnt[sig]
        self.q[self.phase][eng].append((_maxm(wait), fn, sig, inc))
        return (sig, markv) if sig is not None else None

    def mark(self, sig):
        return (sig, self.cnt[sig])

    def run(self, phase, eng_name, eng):
        for waits, fn, sig, inc in self.q[phase][eng_name]:
            for s_, v in waits:
                eng.wait_ge(self.sems[s_], v)
            ins = fn(eng)
            if sig is not None:
                ins.then_inc(self.sems[sig], inc)


def build_program(dbg=()):
    nc = bass.Bass("TRN2", target_bir_lowering=False, detect_race_conditions=False)
    st = contextlib.ExitStack()

    d_u = nc.dram_tensor("u", [L, DM], F32, kind="ExternalInput").ap()
    d_out = nc.dram_tensor("out", [L, DM], F32, kind="ExternalOutput").ap()
    d_win = nc.dram_tensor("w_in", [DM, PROJ], F32, kind="ExternalInput").ap()
    d_thw = nc.dram_tensor("w_th", [DM, 32], F32, kind="ExternalInput").ap()
    d_wout = nc.dram_tensor("w_out", [DI, DM], F32, kind="ExternalInput").ap()
    d_wffn = nc.dram_tensor("w_ffn", [DM, DM], F32, kind="ExternalInput").ap()
    d_idbf = nc.dram_tensor("idbf", [128, 128], BF16, kind="ExternalInput").ap()
    d_idf = nc.dram_tensor("idf", [128, 128], F32, kind="ExternalInput").ap()
    d_tri = nc.dram_tensor("tri", [128, 128], F32, kind="ExternalInput").ap()
    d_oh48 = nc.dram_tensor("oh48", [48, NH * 128], BF16, kind="ExternalInput").ap()
    d_pp48 = nc.dram_tensor("pp48", [48, (NH // 2) * 128], BF16, kind="ExternalInput").ap()
    d_scale_big = nc.dram_tensor("scale_big", [128, DM], BF16, kind="ExternalInput").ap()
    d_bcol = nc.dram_tensor("bcol", [128, 8], BF16, kind="ExternalInput").ap()
    d_brdtb = nc.dram_tensor("br_dtb", [128, NH], F32, kind="ExternalInput").ap()
    d_brD = nc.dram_tensor("br_D", [128, NH], F32, kind="ExternalInput").ap()
    d_brbc = nc.dram_tensor("br_bc", [128, 128], F32, kind="ExternalInput").ap()
    d_brbcb = nc.dram_tensor("br_bcb", [128, 128], F32, kind="ExternalInput").ap()
    d_brfb = nc.dram_tensor("br_fb", [128, DM], F32, kind="ExternalInput").ap()
    d_alog = nc.dram_tensor("alog", [NH, 1], F32, kind="ExternalInput").ap()
    d_oh8 = nc.dram_tensor("oh8", [8, 8 * 128], F32, kind="ExternalInput").ap()
    d_winb = nc.dram_tensor("w_in_bf", [DM, PROJ], BF16, kind="Internal").ap()
    d_bw = nc.dram_tensor("bw_row", [1, PROJ], BF16, kind="Internal").ap()

    dbg_set = {d[0] for d in dbg}
    dbg_d = {}
    for d in dbg:
        name, shape = d[0], d[1]
        dt = BF16 if (len(d) > 2 and d[2] == "bf16") else F32
        dbg_d[name] = nc.dram_tensor("dbg_" + name, list(shape), dt, kind="ExternalOutput").ap()

    SB = {}

    def sb(name, shape, dt):
        h = st.enter_context(nc.sbuf_tensor(name, shape, dt))
        SB[name] = h
        return h

    wout_sb = sb("wout_sb", [128, 16 * DM], BF16)
    wffn_sb = sb("wffn_sb", [128, 8 * DM], BF16)
    thw_sb = sb("thw_sb", [128, 8 * 32], BF16)
    win_buf = [sb(f"win{b}", [128, 8 * 512 + 512], BF16) for b in range(2)]
    win8 = [sb("win8a", [128, 8 * 160], BF16)]
    id_bf = sb("id_bf", [128, 128], BF16)
    id_f = sb("id_f", [128, 128], F32)
    tri_sb = sb("tri_sb", [128, 128], F32)
    oh48_sb = sb("oh48_sb", [48, NH * 128], BF16)
    pp48_sb = sb("pp48_sb", [48, (NH // 2) * 128], BF16)
    scale_big = sb("scale_big_sb", [128, DM], BF16)
    bcol_sb = sb("bcol_sb", [128, 8], BF16)
    epsc = sb("epsc", [128, 2], F32)
    ones1b = sb("ones1b", [1, 128], BF16)
    bwT_sb = sb("bwT_sb", [128, 34], BF16)
    bwT_acc = sb("bwT_acc", [128, 34], F32)
    bw8_st = sb("bw8_st", [1, 160], BF16)
    br_dtb = sb("br_dtb_sb", [128, NH], F32)
    br_D = sb("br_D_sb", [128, NH], F32)
    br_bc = sb("br_bc_sb", [128, 128], F32)
    br_bcb = sb("br_bcb_sb", [128, 128], F32)
    br_fb = sb("br_fb_sb", [128, DM], F32)
    br_ws = sb("br_ws_sb", [128, DM], F32)
    acol = sb("acol", [NH, 1], F32)
    ones32 = sb("ones32", [32, Q], F32)

    uA = [sb(f"uA{b}", [128, DM], F32) for b in range(2)]
    uF = [sb(f"uF{b}", [128, DM], F32) for b in range(2)]
    x_act = sb("x_act", [128, NCH * DI], BF16)
    siluz = sb("siluz", [128, NCH * DI], BF16)
    t2_tok = [sb(f"t2tok{b}", [128, DM], BF16) for b in range(2)]
    t2T = [sb(f"t2T{k}", [128, DM], BF16) for k in range(2)]
    xinT = [sb(f"xinT{k}", [128, DM], BF16) for k in range(NCH)]
    bcdl = [sb(f"bcdl{b}", [128, NCH * 160], F32) for b in range(2)]
    dl_sb = [sb(f"dl{b}", [128, NCH * 32], F32) for b in range(2)]
    theta_tok = sb("theta_tok", [128, NCH * 32], F32)
    stats = [sb(f"stats{b}", [128, 64], F32) for b in range(2)]
    sq_scr = sb("sq_scr", [128, DM], BF16)
    sq_scrF = sb("sq_scrF", [128, DM], F32)
    BCbf = [sb(f"BCbf{b}", [128, NCH * 128], BF16) for b in range(2)]
    BCT = [sb(f"BCT{b}", [128, NCH * 128], BF16) for b in range(2)]
    dtT_sb = [sb(f"dtT{b}", [16, NCH * 128], F32) for b in range(2)]
    lamT_sb = [sb(f"lamT{b}", [16, NCH * 128], F32) for b in range(2)]
    termsT_sb = sb("termsT", [32, NCH * 128], F32)
    lamt_sb = sb("lamt", [16, NCH * 128], F32)
    lamrev_sb = sb("lamrev", [16, NCH * 128], F32)
    lhl = [sb(f"lhl{b}", [48, NCH * 128], BF16) for b in range(2)]
    toks = [sb(f"toks{b}", [128, NCH * 48], F32) for b in range(2)]
    phiT = sb("phiT", [32, T], F32)
    phicarry = sb("phicarry", [32, 1], F32)
    phitok = sb("phitok", [128, NCH * 32], F32)
    cs_tile = sb("cs_tile", [128, NCH * 96], F32)
    bctmp = sb("bctmp", [128, 128], F32)
    ropeA = sb("ropeA", [128, 128], F32)
    ropeB = sb("ropeB", [128, 128], F32)
    wT_sb = sb("wT_sb", [16, T], F32)
    dtlT_sb = sb("dtlT_sb", [16, T], F32)
    rT_sb = sb("rT_sb", [16, T], F32)
    lnrT_sb = sb("lnrT_sb", [16, T], F32)
    patchcol = sb("patchcol", [16, 1], F32)
    wtok = sb("wtok", [128, NCH * NH], F32)
    dvec = sb("dvec", [128, NCH * NH], F32)
    sdec_sb = sb("sdec_sb", [128, NCH * NH], F32)
    subtok = sb("subtok", [128, NCH * NH], F32)
    dgtmp = sb("dgtmp", [128, NS], F32)
    Gmask = sb("GmaskSB", [128, 128], BF16)
    nE_sb = [sb(f"nE{b}", [128, 128], F32) for b in range(4)]
    expE = [sb(f"expE{b}", [128, 128], BF16) for b in range(4)]
    Mt_sb = [sb(f"Mt{b}", [128, 128], BF16) for b in range(4)]
    edec2 = [sb(f"edec2{b}", [128, 128], F32) for b in range(2)]
    C2_sb = sb("C2_sb", [128, 128], BF16)
    Cdec2 = [sb(f"Cdec2{b}", [128, 128], BF16) for b in range(2)]
    Bw_sb = [sb(f"Bw{b}", [128, NS], BF16) for b in range(4)]
    H2f = [sb(f"H2f_{p}", [128, 128], F32) for p in range(8)]
    H2b = [sb(f"H2b_{p}", [128, 128], BF16) for p in range(8)]
    yd_sb = [sb(f"yd{b}", [128, 128], F32) for b in range(2)]
    yg_pool = [sb(f"ygp{b}", [128, 128], BF16) for b in range(4)]
    ygT = sb("ygT", [128, 16 * 128], BF16)
    mix_sb = sb("mix_sb", [128, DM], F32)
    x2_sb = sb("x2_sb", [128, DM], F32)
    c4_sb = sb("c4_sb", [128, DM], F32)
    xq_tok = sb("xq_tok", [128, DM], BF16)
    xqT = sb("xqT", [128, DM], BF16)
    ffn_sb = sb("ffn_sb", [128, DM], F32)
    scal = sb("scal", [128, 64], F32)
    sm16 = sb("sm16", [128, 96], F32)
    qst = sb("qst", [128, 16], F32)
    fsm = sb("fsm", [128, 16], F32)

    stg_w = [uA[0], uA[1]]
    stg_q = xq_tok
    stg_s = sb("stg_s", [128, 16], F32)
    stg_wt = sb("stg_wt", [8, 128], F32)

    ps_mm = [st.enter_context(nc.psum_tensor(f"ps_mm{b}", [128, 512], F32)) for b in range(2)]
    ps_trb = st.enter_context(nc.psum_tensor("ps_trb", [128, 1024], BF16))
    ps_trf = st.enter_context(nc.psum_tensor("ps_trf", [128, 512], F32))
    ps_br = [st.enter_context(nc.psum_tensor(f"ps_br{b}", [128, 512], F32)) for b in range(2)]
    ps_Y = [st.enter_context(nc.psum_tensor(f"ps_Y{b}", [128, 512], F32)) for b in range(2)]

    s = Sched(nc)
    for sem in ("pe", "dv", "ac", "po") + DMA_CH:
        s.new_sem(sem, st)

    def DMA(qname, out, in_, wait=(), slow=False):
        e = "po" if qname in PO_CH else "sy"
        if slow:
            return s.emit(e, lambda E, o=out, i=in_: E.dma_start(out=o, in_=i, allow_slow_non_contiguous=True), wait=wait, sig=qname)
        return s.emit(e, lambda E, o=out, i=in_: E.dma_start(out=o, in_=i), wait=wait, sig=qname)

    def OP(eng, fn, wait=(), sig=True):
        return s.emit(eng, fn, wait=wait, sig=(eng if sig else None))

    def SNAP(tag, src, wait):
        if tag in dbg_set:
            M["dbg_last"] = DMA("dDB", dbg_d[tag], src, wait=list(wait) + [M.get("dbg_last")])

    class Bank:
        def __init__(self):
            self.readers = []
            self.writer = None

        def pre_write(self):
            w = list(self.readers)
            self.readers = []
            return w

        def post_write(self, mark):
            self.writer = mark

        def read(self, mark):
            self.readers.append(mark)

    BK = {"trb": Bank(), "trf": Bank(), "mm0": Bank(), "mm1": Bank(),
          "br0": Bank(), "br1": Bank(), "Y0": Bank(), "Y1": Bank()}

    s.phase = 0
    for o, i in [
        (id_bf[:], d_idbf), (id_f[:], d_idf), (tri_sb[:], d_tri),
        (oh48_sb[:], d_oh48), (pp48_sb[:], d_pp48),
        (scale_big[:], d_scale_big), (bcol_sb[:], d_bcol),
        (br_dtb[:], d_brdtb), (br_D[:], d_brD), (br_bc[:], d_brbc),
        (br_bcb[:], d_brbcb), (br_fb[:], d_brfb), (acol[:], d_alog),
    ]:
        DMA("dC", o, i)
    m_consts = s.mark("dC")

    m_oh8 = DMA("dC", uF[0][0:8, 0:1024], d_oh8, wait=[m_consts])
    m_thw = DMA("dT", thw_sb[:].rearrange("p (c m) -> p c m", c=8), d_thw.rearrange("(c p) m -> p c m", p=128))
    m_wout = DMA("dWo", wout_sb[:].rearrange("p (c m) -> p c m", c=16), d_wout.rearrange("(c p) m -> p c m", p=128))

    m_bz = OP("dv", lambda E: E.memset(bwT_acc[:], 0.0))
    pst1 = pst2 = None
    m_bwadd = None
    for fc in range(8):
        m_l1_ = DMA("dG0", win_buf[0][:, 0:4096], d_win[fc * 128:(fc + 1) * 128, 0:4096], wait=[pst1, m_bwadd])
        m_l2_ = DMA("dG1", win8[0][:, 0:160], d_win[fc * 128:(fc + 1) * 128, 4096:4256], wait=[pst2])
        wv0 = [m_l1_, m_consts, m_bz] + BK["trf"].pre_write()
        for sub in range(32):
            m_bwmm = s.emit("pe", lambda E, fc=fc, sub=sub: E.matmul(
                ps_trf[0:128, 440 + sub:441 + sub], win_buf[0][:, sub * 128:(sub + 1) * 128],
                bcol_sb[:, fc:fc + 1], start=True, stop=True),
                wait=(wv0 if sub == 0 else ()), sig="pe" if sub == 31 else None)
        BK["trf"].post_write(m_bwmm)
        m_bwadd = OP("dv", lambda E: E.tensor_tensor(out=bwT_acc[:, 0:32], in0=bwT_acc[:, 0:32], in1=ps_trf[:, 440:472], op=ALU.add), wait=[m_bwmm])
        BK["trf"].read(m_bwadd)
        pst1 = DMA("dW0", d_winb[fc * 128:(fc + 1) * 128, 0:4096], win_buf[0][:, 0:4096], wait=[m_l1_])
        pst2 = DMA("dW1", d_winb[fc * 128:(fc + 1) * 128, 4096:4256], win8[0][:, 0:160], wait=[m_l2_])
    m_winb = [pst1, pst2]
    m_win8 = DMA("dW8", win8[0][:].rearrange("p (c m) -> p c m", c=8), d_winb[:, 4096:4256].rearrange("(c p) m -> p c m", p=128), wait=[pst2])
    m_bw8add = None
    for fc in range(8):
        wv8 = [m_win8] + BK["trf"].pre_write()
        m_bw8a = s.emit("pe", lambda E, fc=fc: E.matmul(
            ps_trf[0:128, 472:473], win8[0][:, fc * 160:fc * 160 + 128],
            bcol_sb[:, fc:fc + 1], start=True, stop=True), wait=wv8, sig=None)
        m_bw8b = s.emit("pe", lambda E, fc=fc: E.matmul(
            ps_trf[0:32, 473:474], win8[0][:, fc * 160 + 128:fc * 160 + 160],
            bcol_sb[:, fc:fc + 1], start=True, stop=True), sig="pe")
        BK["trf"].post_write(m_bw8b)
        OP("dv", lambda E: E.tensor_tensor(out=bwT_acc[:, 32:33], in0=bwT_acc[:, 32:33], in1=ps_trf[:, 472:473], op=ALU.add), wait=[m_bw8b], sig=False)
        m_bw8add = OP("dv", lambda E: E.tensor_tensor(out=bwT_acc[0:32, 33:34], in0=bwT_acc[0:32, 33:34], in1=ps_trf[0:32, 473:474], op=ALU.add))
        BK["trf"].read(m_bw8add)
    m_bwe = OP("dv", lambda E: E.tensor_copy(out=bwT_sb[:], in_=bwT_acc[:]), wait=[m_bw8add])
    m_bws1 = DMA("dG0", d_bw[0:1, 0:4224].rearrange("r (j p) -> p (r j)", p=128), bwT_sb[:, 0:33], wait=[m_bwe], slow=True)
    m_bws2 = DMA("dG1", d_bw[0:1, 4224:4256].rearrange("r (j p) -> p (r j)", p=32), bwT_sb[0:32, 33:34], wait=[m_bwe], slow=True)
    m_bw8st = DMA("dW8", bw8_st[:], d_bw[0:1, 4096:4256], wait=[m_bws1, m_bws2])

    m_ae = OP("ac", lambda E: E.activation(out=acol[:], in_=acol[:], func=AF.Exp), wait=[m_consts])
    m_acol = OP("dv", lambda E: E.tensor_scalar(out=acol[:], in0=acol[:], scalar1=-1.0, scalar2=None, op0=ALU.mult), wait=[m_ae])

    OP("po", lambda E: E.memset(ones32[:], 1.0), wait=[], sig=False)
    OP("po", lambda E: E.memset(phicarry[:], 0.0), sig=False)
    for p in range(8):
        OP("po", lambda E, p=p: E.memset(H2f[p][:], 0.0), sig=False)
        OP("po", lambda E, p=p: E.memset(H2b[p][:], 0.0), sig=False)
    OP("po", lambda E: E.memset(toks[0][:], 0.0), sig=False)
    OP("po", lambda E: E.memset(toks[1][:], 0.0), sig=False)
    OP("po", lambda E: E.memset(lhl[0][:], 0.0), sig=False)
    OP("po", lambda E: E.memset(lhl[1][:], 0.0), sig=False)
    OP("po", lambda E: E.memset(ones1b[:], 1.0), sig=False)
    OP("po", lambda E: E.memset(epsc[:, 0:1], 1e-5), sig=False)
    OP("po", lambda E: E.memset(epsc[:, 1:2], 1e-6), sig=False)
    m_initpo = OP("po", lambda E: E.memset(patchcol[:], 0.0))

    m_qread = [None, None]
    m_wffn_ev = None
    for oc in range(8):
        b = oc % 2
        m_lw = DMA("dF0" if b == 0 else "dF1", stg_w[b][:], d_wffn[oc * 128:(oc + 1) * 128, :], wait=[m_qread[b]])
        OP("dv", lambda E, b=b: E.tensor_reduce(out=scal[:, 0:1], in_=stg_w[b][:], axis=AX.X, op=ALU.max, apply_absolute_value=True), wait=[m_lw], sig=False)
        OP("dv", lambda E: E.tensor_scalar(out=scal[:, 1:2], in0=scal[:, 0:1], scalar1=1e-8, scalar2=None, op0=ALU.max), sig=False)
        OP("dv", lambda E: E.reciprocal(out=scal[:, 2:3], in_=scal[:, 1:2]), sig=False)
        m_qs = OP("dv", lambda E: E.tensor_scalar(out=scal[:, 3:4], in0=scal[:, 2:3], scalar1=127.0, scalar2=None, op0=ALU.mult))
        OP("dv", lambda E, oc=oc: E.tensor_scalar(out=stg_s[:, oc:oc + 1], in0=scal[:, 1:2], scalar1=1.0 / 127.0, scalar2=None, op0=ALU.mult), sig=False)
        OP("dv", lambda E, b=b: E.tensor_scalar(out=sq_scrF[:], in0=stg_w[b][:], scalar1=scal[:, 3:4], scalar2=MAGIC, op0=ALU.mult, op1=ALU.add), wait=[m_qs], sig=False)
        m_qq = OP("dv", lambda E: E.tensor_scalar(out=stg_q[:], in0=sq_scrF[:], scalar1=MAGIC, scalar2=None, op0=ALU.subtract))
        m_qread[b] = m_qq
        wv = [m_qq, m_consts] + BK["trb"].pre_write()
        for fc in range(8):
            m_tr = s.emit("pe", lambda E, fc=fc: E.transpose(ps_trb[:, fc * 128:(fc + 1) * 128], stg_q[:, fc * 128:(fc + 1) * 128], id_bf[:]),
                          wait=(wv if fc == 0 else ()), sig="pe" if fc == 7 else None)
        BK["trb"].post_write(m_tr)
        dst = wffn_sb[:].rearrange("p (f o) -> p f o", f=8)[:, :, oc * 128:(oc + 1) * 128]
        m_wffn_ev = OP("dv", lambda E, dst=dst: E.tensor_copy(out=dst, in_=ps_trb[:].rearrange("p (f t) -> p f t", f=8)), wait=[m_tr])
        BK["trb"].read(m_wffn_ev)
    m_wst = s.emit("pe", lambda E: E.transpose(ps_trf[0:8, 0:128], stg_s[:, 0:8], id_f[:]), wait=[m_wffn_ev] + BK["trf"].pre_write(), sig="pe")
    BK["trf"].post_write(m_wst)
    m_wse = OP("dv", lambda E: E.tensor_copy(out=stg_wt[:], in_=ps_trf[0:8, 0:128]), wait=[m_wst])
    BK["trf"].read(m_wse)
    m_brws = None
    for ocb in range(8):
        bn = BK[f"mm{ocb % 2}"]
        mmm = s.emit("pe", lambda E, ocb=ocb: E.matmul(ps_mm[ocb % 2][:, 0:128], uF[0][0:8, ocb * 128:(ocb + 1) * 128], stg_wt[:], start=True, stop=True),
                     wait=[m_wse, m_oh8] + bn.pre_write(), sig="pe")
        bn.post_write(mmm)
        m_brws = OP("dv", lambda E, ocb=ocb: E.tensor_copy(out=br_ws[:, ocb * 128:(ocb + 1) * 128], in_=ps_mm[ocb % 2][:, 0:128]), wait=[mmm])
        bn.read(m_brws)

    s.phase = 1

    M = {
        "uAfree": [None, None], "uFfree": [None, None],
        "win8read": [None, None], "winread": [None, None],
        "t2b": [None, None],
        "win8load": [None, None],
        "scan_x": [None] * 4, "scan_z": [None] * 4,
        "t2tok": [None, None],
        "patch": None, "patch_zero": None,
        "outst": None, "pcarry": None,
        "lhl_free": [None, None], "smT_free": [None, None], "toks_free": [None, None],
        "BC_free": [None, None], "BCT_free": [None, None],
        "bcdl_free": [None, None], "dl_free": [None, None], "stats_free": [None, None],
        "mixfree": None, "x2free": None, "ffnfree": None,
        "ygTfree": None, "xqTfree": None,
        "ydfree": [None, None], "ygfree": [None] * 4,
        "tailq": collections.deque(),
        "E_free": [None] * 4, "expE_free": [None] * 4, "Mt_free": [None] * 4,
        "edec2_free": [None, None], "Cdec2_free": [None, None], "Bw_free": [None] * 4,
        "Gmask_free": None, "C2_free": None,
        "sinlast": None, "coslast": None,
        "BCT_m": [[None] * 4, [None] * 4],
        "lhl_m": [[None] * 4, [None] * 4],
        "xact_m": [None] * 4, "siluz_m": [None] * 4,
        "phiscan": None,
        "xinT_m": [None] * 4,
        "theta_m": [None] * 4,
        "t2T_m": [None] * 4,
        "dlm": [None] * 4,
        "bcdl_m": [None] * 4,
        "t2T_free": [None] * 2,
        "xinT_free": [None] * 4,
    }
    STATS = {}

    def stat_cols(p, k):
        b = stats[p]
        return {n: b[:, k * 16 + j: k * 16 + j + 1] for j, n in enumerate(
            ["mu1", "rstd1", "sa", "sbv", "irstd2", "t0", "t1", "t2", "t3", "t4", "t5"])}

    def emit_stats(i, k, usrc, m_uload):
        p = i % 2
        c = stat_cols(p, k)
        wfree = [M["stats_free"][p]] if k == 0 else []
        m1 = OP("dv", lambda E: E.tensor_reduce(out=c["t0"], in_=usrc, axis=AX.X, op=ALU.add), wait=[m_uload] + wfree)
        m2 = OP("ac", lambda E: E.activation(out=sq_scr[:], in_=usrc, func=AF.Square, accum_out=c["t1"]), wait=[m_uload] + wfree)
        m_mu = OP("dv", lambda E: E.tensor_scalar(out=c["mu1"], in0=c["t0"], scalar1=1.0 / DM, scalar2=None, op0=ALU.mult), wait=[m1])
        OP("dv", lambda E: E.tensor_tensor(out=c["t3"], in0=c["mu1"], in1=c["mu1"], op=ALU.mult), sig=False)
        m2b = OP("dv", lambda E: E.scalar_tensor_tensor(out=c["t4"], in0=c["t1"], scalar=1.0 / DM, in1=c["t3"], op0=ALU.mult, op1=ALU.subtract), wait=[m2])
        m3l = OP("ac", lambda E: E.activation(out=c["t2"], in_=c["t4"], func=AF.Ln, bias=epsc[:, 0:1]), wait=[m2b])
        m4 = OP("ac", lambda E: E.activation(out=c["rstd1"], in_=c["t2"], func=AF.Exp, scale=-0.5), wait=[m3l])
        OP("dv", lambda E: E.tensor_tensor(out=c["t3"], in0=c["rstd1"], in1=c["rstd1"], op=ALU.mult), wait=[m4], sig=False)
        m5 = OP("dv", lambda E: E.tensor_tensor(out=c["t5"], in0=c["t4"], in1=c["t3"], op=ALU.mult))
        m5l = OP("ac", lambda E: E.activation(out=c["t2"], in_=c["t5"], func=AF.Ln, bias=epsc[:, 1:2]), wait=[m5])
        m6 = OP("ac", lambda E: E.activation(out=c["irstd2"], in_=c["t2"], func=AF.Exp, scale=0.5))
        m8 = OP("ac", lambda E: E.activation(out=c["t3"], in_=c["t2"], func=AF.Exp, scale=-0.5))
        OP("dv", lambda E: E.tensor_tensor(out=c["sa"], in0=c["rstd1"], in1=c["t3"], op=ALU.mult), wait=[m8], sig=False)
        m9 = OP("dv", lambda E: E.tensor_tensor(out=c["sbv"], in0=c["sa"], in1=c["irstd2"], op=ALU.mult))
        STATS[(i, k)] = (c, m9, m6, m_mu)

    def emit_t2build(i, k, usrc):
        c, m_sa, _, m_mu = STATS[(i, k)]
        tb = k % 2
        m_t2 = OP("dv", lambda E, tb=tb: E.tensor_scalar(out=t2_tok[tb][:], in0=usrc, scalar1=c["mu1"], scalar2=None, op0=ALU.subtract),
                  wait=[m_mu, M["t2tok"][tb]])
        M["t2b"][tb] = m_t2

    def emit_t2T(i, k):
        tb = k % 2
        wv = [M["t2b"][tb]] + BK["trb"].pre_write()
        for fc in range(8):
            m_tr = s.emit("pe", lambda E, fc=fc, tb=tb: E.transpose(ps_trb[:, fc * 128:(fc + 1) * 128], t2_tok[tb][:, fc * 128:(fc + 1) * 128], id_bf[:]),
                          wait=(wv if fc == 0 else ()), sig="pe" if fc == 7 else None)
        BK["trb"].post_write(m_tr)
        M["t2tok"][tb] = m_tr
        m_ev1 = OP("ac", lambda E, k=k: E.activation(out=t2T[k % 2][:], in_=ps_trb[:], func=AF.Copy), wait=[m_tr, M["t2T_free"][k % 2]])
        m_ev2 = OP("dv", lambda E, k=k: E.tensor_tensor(out=xinT[k][:], in0=ps_trb[:], in1=scale_big[:], op=ALU.mult), wait=[m_tr, m_ev1, M["xinT_free"][k]])
        BK["trb"].read(m_ev1)
        BK["trb"].read(m_ev2)
        M["xinT_m"][k] = m_ev2
        M["t2T_m"][k] = m_ev1

    def emit_m8(i, k):
        p = i % 2
        c, m_sa = STATS[(i, k)][0], STATS[(i, k)][1]
        bn = BK["mm0"]
        wv = [M["xinT_m"][k], m_win8] + bn.pre_write()
        for fc in range(8):
            m_mm = s.emit("pe", lambda E, fc=fc, k=k, p=p: E.matmul(
                ps_mm[0][:, 0:160], xinT[k][:, fc * 128:(fc + 1) * 128],
                win8[0][:, fc * 160:(fc + 1) * 160], start=(fc == 0), stop=False),
                wait=(wv if fc == 0 else ()), sig="pe" if fc == 7 else None)
        m_mm = s.emit("pe", lambda E: E.matmul(ps_mm[0][:, 0:160], ones1b[0:1, 0:128], bw8_st[0:1, 0:160], start=False, stop=True),
                      wait=[m_bw8st], sig="pe")
        bn.post_write(m_mm)
        wdst = [m_mm, m_sa] + ([M["bcdl_free"][p]] if k == 0 else [])
        m_ev = OP("dv", lambda E, k=k, p=p, c=c: E.tensor_scalar(out=bcdl[p][:, k * 160:(k + 1) * 160], in0=ps_mm[0][:, 0:160], scalar1=c["sa"], scalar2=None, op0=ALU.mult), wait=wdst)
        bn.read(m_ev)
        M["bcdl_m"][k] = m_ev
        dtr = bcdl[p][:, k * 160 + 128: k * 160 + 144]
        lmr = bcdl[p][:, k * 160 + 144: k * 160 + 160]
        wdl = [M["dl_free"][p]] if k == 0 else []
        m_a = OP("dv", lambda E, dtr=dtr: E.tensor_tensor(out=sm16[:, 0:16], in0=dtr, in1=br_dtb[:], op=ALU.add), wait=[m_ev, M["sinlast"]] + wdl)
        m_b = OP("ac", lambda E: E.activation(out=sm16[:, 16:32], in_=sm16[:, 0:16], func=AF.Abs), wait=[m_a])
        m_c2 = OP("ac", lambda E: E.activation(out=sm16[:, 16:32], in_=sm16[:, 16:32], func=AF.Exp, scale=-1.0))
        m_d = OP("ac", lambda E: E.activation(out=sm16[:, 16:32], in_=sm16[:, 16:32], func=AF.Ln, bias=1.0))
        m_e = OP("ac", lambda E: E.activation(out=sm16[:, 0:16], in_=sm16[:, 0:16], func=AF.Relu))
        m_dt = OP("dv", lambda E, k=k, p=p: E.tensor_tensor(out=dl_sb[p][:, k * 32:k * 32 + 16], in0=sm16[:, 0:16], in1=sm16[:, 16:32], op=ALU.add), wait=[m_d, m_e])
        m_l1_ = OP("ac", lambda E, lmr=lmr: E.activation(out=sm16[:, 32:48], in_=lmr, func=AF.Exp, scale=-1.0), wait=[m_ev, M["coslast"]])
        m_l2_ = OP("dv", lambda E: E.tensor_scalar(out=sm16[:, 32:48], in0=sm16[:, 32:48], scalar1=1.0, scalar2=None, op0=ALU.add), wait=[m_l1_])
        m_lm = OP("dv", lambda E, k=k, p=p: E.reciprocal(out=dl_sb[p][:, k * 32 + 16:k * 32 + 32], in_=sm16[:, 32:48]))
        M["dlm"][k] = (m_dt, m_lm)

    def emit_patch(i):
        p = i % 2
        m_dt, m_lm = M["dlm"][0]
        dtv = dl_sb[p][:, 0:16]
        lmv = dl_sb[p][:, 16:32]
        m1 = OP("dv", lambda E: E.tensor_tensor(out=sm16[:, 32:48], in0=dtv, in1=lmv, op=ALU.mult), wait=[m_dt, m_lm, M["coslast"]])
        m2 = OP("dv", lambda E: E.tensor_tensor(out=sm16[:, 48:64], in0=dtv, in1=sm16[:, 32:48], op=ALU.subtract))
        wv = [m2] + BK["trf"].pre_write()
        m3 = s.emit("pe", lambda E: E.transpose(ps_trf[0:16, 0:128], sm16[:, 48:64], id_f[:]), wait=wv, sig="pe")
        BK["trf"].post_write(m3)
        m4 = OP("dv", lambda E: E.tensor_copy(out=patchcol[:], in_=ps_trf[0:16, 0:1]), wait=[m3])
        BK["trf"].read(m4)
        M["patch"] = m4

    def emit_uload_stats_t2b(i, k, extra=()):
        m_u = DMA("dU0" if k == 0 else "dU1", uA[k % 2][:], d_u[i * T + k * 128:i * T + (k + 1) * 128, :],
                  wait=[M["uAfree"][k % 2]] + list(extra))
        emit_stats(i, k, uA[k % 2][:], m_u)
        emit_t2build(i, k, uA[k % 2][:])
        M["uAfree"][k % 2] = [s.mark("dv"), s.mark("ac")]

    def emit_A1(i):
        emit_t2T(i, 0)
        emit_m8(i, 0)
        emit_patch(i)

    def emit_theta(i, k):
        bn = BK["mm1"]
        c, m_sa = STATS[(i, k)][0], STATS[(i, k)][1]
        wv = [M["t2T_m"][k]] + bn.pre_write()
        for fc in range(8):
            m_mm = s.emit("pe", lambda E, fc=fc, k=k: E.matmul(
                ps_mm[1][:, 0:32], t2T[k % 2][:, fc * 128:(fc + 1) * 128],
                thw_sb[:, fc * 32:(fc + 1) * 32], start=(fc == 0), stop=(fc == 7)),
                wait=(wv if fc == 0 else ()), sig="pe" if fc == 7 else None)
        bn.post_write(m_mm)
        M["t2T_free"][k % 2] = m_mm
        m_ev = OP("dv", lambda E, k=k, c=c: E.tensor_scalar(
            out=theta_tok[:, k * 32:(k + 1) * 32], in0=ps_mm[1][:, 0:32],
            scalar1=c["sbv"], scalar2=None, op0=ALU.mult), wait=[m_mm, m_sa])
        bn.read(m_ev)
        M["theta_m"][k] = m_ev

    def trf_transpose(out_ap, src_ap, kdim, wait):
        wv = list(wait) + BK["trf"].pre_write()
        b0 = src_ap.base_partition() if kdim != 128 else 0
        ident = id_f[:] if kdim == 128 else id_f[b0:b0 + kdim, b0:b0 + kdim]
        m = s.emit("pe", lambda E: E.transpose(out_ap, src_ap, ident), wait=wv, sig="pe")
        BK["trf"].post_write(m)
        return m

    def emit_smalls_T(i, k):
        p = i % 2
        m_dt, m_lm = M["dlm"][k]
        m_theta = M["theta_m"][k]
        m_s1 = OP("dv", lambda E, p=p, k=k: E.tensor_reduce(out=scal[:, 32:33], in_=dl_sb[p][:, k * 32:k * 32 + 16], axis=AX.X, op=ALU.add), wait=[m_dt])
        m_s2 = OP("dv", lambda E: E.tensor_scalar(out=scal[:, 33:34], in0=scal[:, 32:33], scalar1=1.0 / NH, scalar2=None, op0=ALU.mult))
        m_tm = OP("dv", lambda E, k=k: E.tensor_scalar(out=sm16[:, 0:32], in0=theta_tok[:, k * 32:(k + 1) * 32], scalar1=scal[:, 33:34], scalar2=None, op0=ALU.mult), wait=[m_s2, m_theta, M["sinlast"]])
        wfree = [M["smT_free"][p]] if k == 0 else []
        m_t1 = trf_transpose(ps_trf[0:16, 0:128], dl_sb[p][:, k * 32:k * 32 + 16], 128, [m_dt])
        m_e1 = OP("dv", lambda E, p=p, k=k: E.tensor_copy(out=dtT_sb[p][:, k * 128:(k + 1) * 128], in_=ps_trf[0:16, 0:128]), wait=[m_t1] + wfree)
        BK["trf"].read(m_e1)
        m_t2_ = trf_transpose(ps_trf[0:16, 128:256], dl_sb[p][:, k * 32 + 16:k * 32 + 32], 128, [m_lm])
        m_e2 = OP("dv", lambda E, p=p, k=k: E.tensor_copy(out=lamT_sb[p][:, k * 128:(k + 1) * 128], in_=ps_trf[0:16, 128:256]), wait=[m_t2_])
        BK["trf"].read(m_e2)
        m_t3 = trf_transpose(ps_trf[0:32, 256:384], sm16[:, 0:32], 128, [m_tm])
        m_e3 = OP("dv", lambda E, p=p, k=k: E.tensor_copy(out=termsT_sb[:, k * 128:(k + 1) * 128], in_=ps_trf[0:32, 256:384]), wait=[m_t3])
        BK["trf"].read(m_e3)
        m_sc = OP("dv", lambda E, p=p, k=k: E.tensor_tensor_scan(
            out=lamt_sb[:, k * 128:(k + 1) * 128], data0=ones32[0:16, :], data1=dtT_sb[p][:, k * 128:(k + 1) * 128],
            initial=0.0, op0=ALU.mult, op1=ALU.add), wait=[m_e1])
        m_lam = OP("dv", lambda E, k=k: E.tensor_scalar(out=lamt_sb[:, k * 128:(k + 1) * 128], in0=lamt_sb[:, k * 128:(k + 1) * 128], scalar1=acol[:], scalar2=None, op0=ALU.mult), wait=[m_sc])
        m_lr = OP("dv", lambda E, k=k: E.tensor_scalar(
            out=lamrev_sb[:, k * 128:(k + 1) * 128], in0=lamt_sb[:, k * 128:(k + 1) * 128],
            scalar1=lamt_sb[:, k * 128 + 127:k * 128 + 128], scalar2=-1.0, op0=ALU.subtract, op1=ALU.mult), wait=[m_lam])
        wl = [M["lhl_free"][p]] if k == 0 else []
        m_hi = OP("dv", lambda E, p=p, k=k: E.tensor_copy(out=lhl[p][0:16, k * 128:(k + 1) * 128], in_=lamt_sb[:, k * 128:(k + 1) * 128]), wait=wl + [m_lam])
        m_lo = OP("dv", lambda E, p=p, k=k: E.tensor_tensor(out=lhl[p][32:48, k * 128:(k + 1) * 128], in0=lamt_sb[:, k * 128:(k + 1) * 128], in1=lhl[p][0:16, k * 128:(k + 1) * 128], op=ALU.subtract))
        M["lhl_m"][p][k] = m_lo
        wt = [M["toks_free"][p]] if k == 0 else []
        m_t4 = trf_transpose(ps_trf[:, 0:16], lamt_sb[:, k * 128:(k + 1) * 128], 16, [m_lam])
        m_e4 = OP("dv", lambda E, p=p, k=k: E.tensor_copy(out=toks[p][:, k * 48:k * 48 + 16], in_=ps_trf[:, 0:16]), wait=[m_t4] + wt)
        BK["trf"].read(m_e4)
        m_t5 = trf_transpose(ps_trf[:, 16:32], lamrev_sb[:, k * 128:(k + 1) * 128], 16, [m_lr])
        m_e5 = OP("dv", lambda E, p=p, k=k: E.tensor_copy(out=toks[p][:, k * 48 + 16:k * 48 + 32], in_=ps_trf[:, 16:32]), wait=[m_t5])
        BK["trf"].read(m_e5)

    def emit_phi_cs(i, k):
        p = i % 2
        init = phicarry[:] if k == 0 else phiT[:, k * 128 - 1:k * 128]
        wv = [s.mark("dv")]
        if k == 0 and M["pcarry"]:
            wv.append(M["pcarry"])
        if k > 0:
            wv.append(M["phiscan"])
        m_sc = OP("dv", lambda E, k=k, init=init: E.tensor_tensor_scan(
            out=phiT[:, k * 128:(k + 1) * 128], data0=ones32[:], data1=termsT_sb[:, k * 128:(k + 1) * 128],
            initial=init, op0=ALU.mult, op1=ALU.add), wait=wv)
        M["phiscan"] = m_sc
        if k == NCH - 1:
            M["pcarry"] = OP("dv", lambda E: E.tensor_copy(out=phicarry[:], in_=phiT[:, T - 1:T]), wait=[m_sc])
        m_tr = trf_transpose(ps_trf[:, 0:32], phiT[0:32, k * 128:(k + 1) * 128], 32, [m_sc])
        m_ev = OP("dv", lambda E, k=k: E.tensor_copy(out=phitok[:, k * 32:(k + 1) * 32], in_=ps_trf[:, 0:32]), wait=[m_tr])
        BK["trf"].read(m_ev)
        ph = phitok[:, k * 32:(k + 1) * 32]
        cb = cs_tile[:, k * 96:k * 96 + 96]
        OP("dv", lambda E, ph=ph: E.tensor_scalar(out=sm16[:, 0:32], in0=ph, scalar1=INV2PI, scalar2=None, op0=ALU.mult), wait=[m_ev, M["sinlast"]], sig=False)
        OP("dv", lambda E: E.tensor_scalar(out=sm16[:, 0:32], in0=sm16[:, 0:32], scalar1=MAGIC, scalar2=MAGIC, op0=ALU.add, op1=ALU.subtract), sig=False)
        OP("dv", lambda E, ph=ph: E.scalar_tensor_tensor(out=sm16[:, 0:32], in0=sm16[:, 0:32], scalar=-2.0 * PI, in1=ph, op0=ALU.mult, op1=ALU.add), sig=False)
        m_sr = OP("dv", lambda E: E.tensor_scalar(out=sm16[:, 0:32], in0=sm16[:, 0:32], scalar1=PI, scalar2=-PI, op0=ALU.min, op1=ALU.max))
        m_sin = OP("ac", lambda E, cb=cb: E.activation(out=cb[:, 32:64], in_=sm16[:, 0:32], func=AF.Sin), wait=[m_sr])
        M["sinlast"] = m_sin
        OP("dv", lambda E, ph=ph: E.tensor_scalar(out=sm16[:, 32:64], in0=ph, scalar1=INV2PI, scalar2=0.25, op0=ALU.mult, op1=ALU.add), wait=[M["coslast"]], sig=False)
        OP("dv", lambda E: E.tensor_scalar(out=sm16[:, 64:96], in0=sm16[:, 32:64], scalar1=MAGIC, scalar2=MAGIC, op0=ALU.add, op1=ALU.subtract), sig=False)
        OP("dv", lambda E: E.tensor_tensor(out=sm16[:, 64:96], in0=sm16[:, 32:64], in1=sm16[:, 64:96], op=ALU.subtract), sig=False)
        OP("dv", lambda E: E.tensor_scalar(out=sm16[:, 64:96], in0=sm16[:, 64:96], scalar1=2.0 * PI, scalar2=PI, op0=ALU.mult, op1=ALU.min), sig=False)
        m_cr = OP("dv", lambda E: E.tensor_scalar(out=sm16[:, 64:96], in0=sm16[:, 64:96], scalar1=-PI, scalar2=None, op0=ALU.max))
        m_cos = OP("ac", lambda E, cb=cb: E.activation(out=cb[:, 0:32], in_=sm16[:, 64:96], func=AF.Sin), wait=[m_cr])
        M["coslast"] = m_cos
        m_ns = OP("dv", lambda E, cb=cb: E.tensor_scalar(out=cb[:, 64:96], in0=cb[:, 32:64], scalar1=-1.0, scalar2=None, op0=ALU.mult), wait=[m_sin])
        return (m_sin, m_cos, m_ns)

    def _rep4(off, w):
        ap = cs_tile[:, off:off + w]
        return bass.AP(ap.tensor, ap.offset, [list(ap.ap[0]), [0, 4], [1, w]])

    def _swapview():
        ap = bctmp[:, 0:128]
        return bass.AP(ap.tensor, ap.offset + 32, [list(ap.ap[0]), [64, 2], [-32, 2], [1, 32]])

    def _negssview(off):
        ap = cs_tile[:, 0:128]
        return bass.AP(ap.tensor, ap.offset + off + 64, [list(ap.ap[0]), [0, 2], [-32, 2], [1, 32]])

    def emit_BC(i, k, m_cs):
        p = i % 2
        m_bcdl = M["bcdl_m"][k]
        Braw = bcdl[p][:, k * 160:k * 160 + 64]
        Craw = bcdl[p][:, k * 160 + 64:k * 160 + 128]
        OP("ac", lambda E, Braw=Braw: E.activation(out=sq_scr[:, 0:64], in_=Braw, func=AF.Square, accum_out=scal[:, 34:35]), wait=[m_bcdl], sig=False)
        OP("ac", lambda E, Craw=Craw: E.activation(out=sq_scr[:, 64:128], in_=Craw, func=AF.Square, accum_out=scal[:, 35:36]), sig=False)
        m_r1a = OP("ac", lambda E: E.activation(out=scal[:, 36:37], in_=scal[:, 34:35], func=AF.Ln, scale=1.0 / NS, bias=epsc[:, 1:2]))
        m_r2 = OP("ac", lambda E: E.activation(out=scal[:, 38:39], in_=scal[:, 36:37], func=AF.Exp, scale=-0.5))
        m_r3a = OP("ac", lambda E: E.activation(out=scal[:, 37:38], in_=scal[:, 35:36], func=AF.Ln, scale=1.0 / NS, bias=epsc[:, 1:2]))
        m_r4 = OP("ac", lambda E: E.activation(out=scal[:, 39:40], in_=scal[:, 37:38], func=AF.Exp, scale=-0.5))
        OP("dv", lambda E, Braw=Braw: E.tensor_scalar(out=bctmp[:, 0:64], in0=Braw, scalar1=scal[:, 38:39], scalar2=None, op0=ALU.mult), wait=[m_r2], sig=False)
        OP("dv", lambda E, Craw=Craw: E.tensor_scalar(out=bctmp[:, 64:128], in0=Craw, scalar1=scal[:, 39:40], scalar2=None, op0=ALU.mult), wait=[m_r4], sig=False)
        OP("dv", lambda E: E.tensor_tensor(out=ropeA[:], in0=bctmp[:], in1=br_bc[:], op=ALU.mult), sig=False)
        m_n4 = OP("dv", lambda E: E.tensor_tensor(out=bctmp[:], in0=ropeA[:], in1=br_bcb[:], op=ALU.add))
        m_T1 = OP("dv", lambda E, k=k: E.tensor_tensor(out=ropeA[:], in0=bctmp[:], in1=_rep4(k * 96, 32), op=ALU.mult), wait=[m_n4] + list(m_cs))
        m_T2 = OP("dv", lambda E, k=k: E.tensor_tensor(out=ropeB[:], in0=_swapview(), in1=_negssview(k * 96), op=ALU.mult))
        wbc = [M["BC_free"][p]] if k == 0 else []
        m_bc = OP("dv", lambda E, p=p, k=k: E.tensor_tensor(out=BCbf[p][:, k * 128:(k + 1) * 128], in0=ropeA[:], in1=ropeB[:], op=ALU.add), wait=wbc)
        wv = [m_bc] + BK["trb"].pre_write()
        m_tr = s.emit("pe", lambda E, p=p, k=k: E.transpose(ps_trb[:, 0:128], BCbf[p][:, k * 128:(k + 1) * 128], id_bf[:]), wait=wv, sig="pe")
        BK["trb"].post_write(m_tr)
        wct = [M["BCT_free"][p]] if k == 0 else []
        m_ev = OP("dv", lambda E, p=p, k=k: E.tensor_copy(out=BCT[p][:, k * 128:(k + 1) * 128], in_=ps_trb[:, 0:128]), wait=[m_tr] + wct)
        BK["trb"].read(m_ev)
        M["BCT_m"][p][k] = m_ev

    def emit_inproj(i):
        p = i % 2
        for mj in range(8):
            b = mj % 2
            m_wl = DMA("dW0" if b == 0 else "dW1", win_buf[b][:, 0:4096].rearrange("p (c m) -> p c m", c=8), d_winb[:, mj * 512:(mj + 1) * 512].rearrange("(c p) m -> p c m", p=128),
                       wait=[M["winread"][b]] + (m_winb if i == 0 else []))
            m_wb = DMA("dG0" if b == 0 else "dG1", win_buf[b][0:1, 4096:4608], d_bw[0:1, mj * 512:(mj + 1) * 512],
                       wait=[M["winread"][b]] + ([m_bw8st] if i == 0 else []))
            last_mm = None
            for k in range(NCH):
                cst, m_sa = STATS[(i, k)][0], STATS[(i, k)][1]
                bn = BK[f"mm{k % 2}"]
                wv = [m_wl, M["xinT_m"][k]] + bn.pre_write()
                for fc in range(8):
                    m_mmf = s.emit("pe", lambda E, k=k, fc=fc, b=b: E.matmul(
                        ps_mm[k % 2][:, 0:512], xinT[k][:, fc * 128:(fc + 1) * 128],
                        win_buf[b][:, fc * 512:(fc + 1) * 512], start=(fc == 0), stop=False),
                        wait=(wv if fc == 0 else ()), sig="pe" if fc == 7 else None)
                m_mm = s.emit("pe", lambda E, k=k, b=b: E.matmul(
                    ps_mm[k % 2][:, 0:512], ones1b[0:1, 0:128], win_buf[b][0:1, 4096:4608], start=False, stop=True),
                    wait=[m_wb] + ([m_initpo] if i == 0 and mj == 0 and k == 0 else []), sig="pe")
                bn.post_write(m_mm)
                last_mm = m_mm
                if mj == 7:
                    M["xinT_free"][k] = m_mmf
                if mj < 4:
                    dst = x_act[:, k * 2048 + mj * 512: k * 2048 + (mj + 1) * 512]
                    gate = M["scan_x"][k] or []
                else:
                    dst = siluz[:, k * 2048 + (mj - 4) * 512: k * 2048 + (mj - 3) * 512]
                    gate = M["scan_z"][k] or []
                m_ev = OP("ac", lambda E, dst=dst, k=k, cst=cst: E.activation(out=dst, in_=ps_mm[k % 2][:, 0:512], func=AF.Silu, scale=cst["sa"]),
                          wait=[m_mm, m_sa] + list(gate))
                bn.read(m_ev)
                if mj == 3:
                    M["xact_m"][k] = m_ev
                if mj == 7:
                    M["siluz_m"][k] = m_ev
            M["winread"][b] = last_mm

    def emit_A2(i):
        emit_t2T(i, 1)
        emit_m8(i, 1)
        if i + 1 < NTv:
            for k in range(NCH):
                emit_uload_stats_t2b(i + 1, k)
        emit_inproj(i)
        for k in range(NCH):
            emit_theta(i, k)
            emit_smalls_T(i, k)
            m_cs = emit_phi_cs(i, k)
            emit_BC(i, k, m_cs)
        p = i % 2
        M["bcdl_free"][p] = s.mark("dv")
        if i != 0:
            return
        SNAP("x_act", x_act[:], [s.mark("ac")])
        SNAP("siluz", siluz[:], [s.mark("ac")])
        SNAP("bcdl", bcdl[p][:], [s.mark("dv")])
        SNAP("dl", dl_sb[p][:], [s.mark("dv"), s.mark("ac")])
        SNAP("theta", theta_tok[:], [s.mark("dv")])
        SNAP("phitok", phitok[:], [s.mark("dv")])
        SNAP("cs", cs_tile[:], [s.mark("dv"), s.mark("ac")])
        SNAP("BCbf", BCbf[p][:], [s.mark("dv")])
        SNAP("lampk", lamt_sb[:], [s.mark("dv")])
        SNAP("toks_a", toks[p][:], [s.mark("dv")])
        SNAP("xinT0", xinT[0][:], [s.mark("dv")])
        SNAP("uA0", uA[0][:], [s.mark("dv")])
        SNAP("t2T0", t2T[0][:], [s.mark("ac")])
        SNAP("stats0", stats[0][:], [s.mark("dv"), s.mark("ac")])
        SNAP("scaleb", scale_big[:], [s.mark("dv")])
        SNAP("idbf_t", id_bf[:], [s.mark("dv")])
        SNAP("winb0", win_buf[0][:], [s.mark("pe")])

    def emit_F(j, is_last):
        q = j % 2
        m_w1 = OP("dv", lambda E, q=q: E.tensor_tensor(out=wT_sb[:], in0=dtT_sb[q][:], in1=lamT_sb[q][:], op=ALU.mult))
        OP("dv", lambda E, q=q: E.tensor_tensor(out=dtlT_sb[:], in0=dtT_sb[q][:], in1=wT_sb[:], op=ALU.subtract), sig=False)
        m_r1 = OP("dv", lambda E: E.tensor_tensor(out=rT_sb[:, 0:T - 1], in0=wT_sb[:, 0:T - 1], in1=dtlT_sb[:, 1:T], op=ALU.add))
        wpatch = [M["patch_zero"] if is_last else M["patch"]]
        m_r2 = OP("dv", lambda E: E.tensor_tensor(out=rT_sb[:, T - 1:T], in0=wT_sb[:, T - 1:T], in1=patchcol[:], op=ALU.add), wait=wpatch)
        m_rcl = OP("dv", lambda E: E.tensor_scalar(out=rT_sb[:], in0=rT_sb[:], scalar1=1e-38, scalar2=None, op0=ALU.max), wait=[m_r1, m_r2])
        m_ln = OP("ac", lambda E: E.activation(out=lnrT_sb[:], in_=rT_sb[:], func=AF.Ln), wait=[m_rcl])
        if j == 0:
            SNAP("fin1", wT_sb[:], [s.mark("dv")])
            SNAP("fin2", lnrT_sb[:], [s.mark("ac"), s.mark("dv")])

        for c in range(NCH):
            m_t = trf_transpose(ps_trf[:, 0:16], lnrT_sb[:, c * 128:(c + 1) * 128], 16, [m_ln])
            m_lt = OP("dv", lambda E, q=q, c=c: E.tensor_copy(out=toks[q][:, c * 48 + 32:c * 48 + 48], in_=ps_trf[:, 0:16]), wait=[m_t])
            BK["trf"].read(m_lt)
            m_st = OP("dv", lambda E, q=q, c=c: E.tensor_tensor(out=subtok[:, c * 16:(c + 1) * 16], in0=toks[q][:, c * 48:c * 48 + 16], in1=toks[q][:, c * 48 + 32:c * 48 + 48], op=ALU.subtract), wait=[m_lt])
            m_sd1 = OP("dv", lambda E, q=q, c=c: E.tensor_tensor(out=fsm[:], in0=toks[q][:, c * 48 + 16:c * 48 + 32], in1=toks[q][:, c * 48 + 32:c * 48 + 48], op=ALU.add))
            m_sd = OP("ac", lambda E, c=c: E.activation(out=sdec_sb[:, c * 16:(c + 1) * 16], in_=fsm[:], func=AF.Exp), wait=[m_sd1])
            m_wt = OP("dv", lambda E, q=q, c=c: E.tensor_tensor(out=wtok[:, c * 16:(c + 1) * 16], in0=dl_sb[q][:, c * 32:c * 32 + 16], in1=dl_sb[q][:, c * 32 + 16:c * 32 + 32], op=ALU.mult))
            OP("dv", lambda E, q=q, c=c: E.tensor_tensor(out=dgtmp[:], in0=BCbf[q][:, c * 128:c * 128 + 64], in1=BCbf[q][:, c * 128 + 64:c * 128 + 128], op=ALU.mult), sig=False)
            m_dg2 = OP("dv", lambda E: E.tensor_reduce(out=scal[:, 40:41], in_=dgtmp[:], axis=AX.X, op=ALU.add))
            m_dv = OP("dv", lambda E, c=c: E.scalar_tensor_tensor(out=dvec[:, c * 16:(c + 1) * 16], in0=wtok[:, c * 16:(c + 1) * 16], scalar=scal[:, 40:41], in1=br_D[:], op0=ALU.mult, op1=ALU.add), wait=[m_dg2, m_wt])
            m_c2a = OP("po", lambda E, q=q, c=c: E.tensor_copy(out=C2_sb[0:64, :], in_=BCT[q][64:128, c * 128:(c + 1) * 128]), wait=[M["C2_free"], M["BCT_m"][q][c]])
            m_c2b = OP("po", lambda E, q=q, c=c: E.tensor_copy(out=C2_sb[64:128, :], in_=BCT[q][64:128, c * 128:(c + 1) * 128]))
            wv = BK["trf"].pre_write()
            m_G = s.emit("pe", lambda E, q=q, c=c: E.matmul(ps_trf[:, 0:128], BCT[q][0:64, c * 128:(c + 1) * 128], C2_sb[0:64, :], start=True, stop=True), wait=wv + [m_c2a, M["BCT_m"][q][c]], sig="pe")
            BK["trf"].post_write(m_G)
            m_gm = OP("dv", lambda E: E.tensor_tensor(out=Gmask[:], in0=ps_trf[:, 0:128], in1=tri_sb[:], op=ALU.mult), wait=[m_G, M["Gmask_free"]])
            BK["trf"].read(m_gm)

            marks_yg = []
            PIPE = [None] * 8
            YGQ = [None] * 8

            def emit_ygtr(tp):
                g0, g1 = YGQ[tp]
                gx0, gx1 = (tp % 2) * 2, (tp % 2) * 2 + 1
                wv = ([] if tp % 4 else BK["trb"].pre_write())
                m_t0 = s.emit("pe", lambda E_, tp=tp, gx0=gx0: E_.transpose(ps_trb[:, (tp % 4) * 256:(tp % 4) * 256 + 128], yg_pool[gx0][:], id_bf[:]), wait=[g0] + wv, sig="pe")
                m_t1 = s.emit("pe", lambda E_, tp=tp, gx1=gx1: E_.transpose(ps_trb[:, (tp % 4) * 256 + 128:(tp % 4) * 256 + 256], yg_pool[gx1][:], id_bf[:]), wait=[g1], sig="pe")
                BK["trb"].post_write(m_t1)
                M["ygfree"][gx0] = m_t0
                M["ygfree"][gx1] = m_t1
                if tp in (3, 7):
                    half = 0 if tp == 3 else 1
                    wyt = [M["ygTfree"]] if tp == 3 else []
                    m_evy = OP("dv", lambda E_, c=c, half=half: E_.tensor_copy(out=ygT[:, half * 1024:(half + 1) * 1024], in_=ps_trb[:]), wait=[m_t1] + wyt)
                    BK["trb"].read(m_evy)
                    marks_yg.append(m_evy)

            def emit_burst(pr):
                bk = pr % 2
                bbn = BK[f"br{bk}"]
                h0, h1 = 2 * pr, 2 * pr + 1
                wb = bbn.pre_write()
                m_b0 = s.emit("pe", lambda E, q=q, c=c, h0=h0, bk=bk: E.matmul(ps_br[bk][:, 0:128], oh48_sb[:, h0 * 128:(h0 + 1) * 128], lhl[q][:, c * 128:(c + 1) * 128], start=True, stop=True), wait=wb + [M["lhl_m"][q][c]], sig="pe")
                m_b1 = s.emit("pe", lambda E, q=q, c=c, h1=h1, bk=bk: E.matmul(ps_br[bk][:, 128:256], oh48_sb[:, h1 * 128:(h1 + 1) * 128], lhl[q][:, c * 128:(c + 1) * 128], start=True, stop=True), sig="pe")
                m_b2 = s.emit("pe", lambda E, q=q, c=c, pr=pr, bk=bk: E.matmul(ps_br[bk][:, 256:384], pp48_sb[:, pr * 128:(pr + 1) * 128], lhl[q][:, c * 128:(c + 1) * 128], start=True, stop=True), sig="pe")
                bbn.post_write(m_b2)
                return dict(bk=bk, bbn=bbn, h0=h0, h1=h1, m_b0=m_b0, m_b1=m_b1, m_b2=m_b2)

            def emit_chain(pr, P):
                bk, bbn, h0, h1 = P["bk"], P["bbn"], P["h0"], P["h1"]
                m_Mt = [None, None]
                m_E = [None, None]
                for e_ix, (hh, m_bb) in enumerate([(h0, P["m_b0"]), (h1, P["m_b1"])]):
                    r_ix = (pr * 2 + e_ix) % 4
                    m_E[e_ix] = OP("ac", lambda E_, c=c, hh=hh, bk=bk, e_ix=e_ix, r_ix=r_ix: E_.activation(
                        out=nE_sb[r_ix][:], in_=ps_br[bk][:, e_ix * 128:(e_ix + 1) * 128], func=AF.Relu,
                        scale=-1.0, bias=subtok[:, c * 16 + hh:c * 16 + hh + 1]),
                        wait=[P["m_b2"], m_st, M["E_free"][r_ix]])
                    bbn.read(m_E[e_ix])
                    m_x = OP("ac", lambda E_, r_ix=r_ix: E_.activation(out=expE[r_ix][:], in_=nE_sb[r_ix][:], func=AF.Exp, scale=-1.0), wait=[M["expE_free"][r_ix]])
                    M["E_free"][r_ix] = m_x
                    m_Mt[e_ix] = OP("dv", lambda E_, e_ix=e_ix, r_ix=r_ix: E_.tensor_tensor(out=Mt_sb[r_ix][:], in0=expE[r_ix][:], in1=Gmask[:], op=ALU.mult), wait=[m_x, m_gm, M["Mt_free"][r_ix]])
                    M["expE_free"][r_ix] = m_Mt[e_ix]
                m_ed = OP("ac", lambda E_, bk=bk: E_.activation(out=edec2[bk][:], in_=ps_br[bk][:, 256:384], func=AF.Exp), wait=[P["m_b2"], m_E[0], m_E[1], M["edec2_free"][bk]])
                bbn.read(m_ed)
                m_cd = OP("po", lambda E_, bk=bk: E_.tensor_tensor(out=Cdec2[bk][:], in0=C2_sb[:], in1=edec2[bk][:], op=ALU.mult), wait=[m_ed, m_c2b, M["Cdec2_free"][bk]])
                m_Bw = [None, None]
                for e_ix, hh in enumerate([h0, h1]):
                    bw = bk * 2 + e_ix
                    m_Bw[e_ix] = OP("dv", lambda E_, q=q, c=c, hh=hh, bw=bw: E_.tensor_scalar(
                        out=Bw_sb[bw][:], in0=BCbf[q][:, c * 128:c * 128 + 64],
                        scalar1=sdec_sb[:, c * 16 + hh:c * 16 + hh + 1], scalar2=None, op0=ALU.mult), wait=[m_sd, M["Bw_free"][bw]])
                P.update(m_Mt=m_Mt, m_ed=m_ed, m_cd=m_cd, m_Bw=m_Bw)

            for pp in (0, 1):
                PIPE[pp] = emit_burst(pp)
                emit_chain(pp, PIPE[pp])

            for pr in range(8):
                P = PIPE[pr]
                bk = P["bk"]
                ybn = BK[f"Y{bk}"]
                h0, h1 = P["h0"], P["h1"]
                m_Mt, m_ed, m_cd, m_Bw = P["m_Mt"], P["m_ed"], P["m_cd"], P["m_Bw"]
                wy = ybn.pre_write()
                r0, r1 = (pr * 2) % 4, (pr * 2 + 1) % 4
                bw0, bw1 = bk * 2, bk * 2 + 1
                m_y0a = s.emit("pe", lambda E_, c=c, h0=h0, bk=bk, r0=r0: E_.matmul(ps_Y[bk][:, 0:128], Mt_sb[r0][:], x_act[:, c * 2048 + h0 * 128:c * 2048 + (h0 + 1) * 128], start=True, stop=False), wait=[m_Mt[0], M["xact_m"][c]] + wy, sig="pe")
                m_y0b = s.emit("pe", lambda E_, bk=bk, pr=pr: E_.matmul(ps_Y[bk][:, 0:128], Cdec2[bk][0:64, :], H2b[pr][0:64, :], start=False, stop=True), wait=[m_cd], sig="pe")
                m_y1a = s.emit("pe", lambda E_, c=c, h1=h1, bk=bk, r1=r1: E_.matmul(ps_Y[bk][:, 128:256], Mt_sb[r1][:], x_act[:, c * 2048 + h1 * 128:c * 2048 + (h1 + 1) * 128], start=True, stop=False), wait=[m_Mt[1]], sig="pe")
                m_y1b = s.emit("pe", lambda E_, bk=bk, pr=pr: E_.matmul(ps_Y[bk][:, 128:256], Cdec2[bk][64:128, :], H2b[pr][64:128, :], start=False, stop=True), sig="pe")
                m_s0 = s.emit("pe", lambda E_, c=c, h0=h0, bk=bk, bw0=bw0: E_.matmul(ps_Y[bk][0:64, 256:384], Bw_sb[bw0][:], x_act[:, c * 2048 + h0 * 128:c * 2048 + (h0 + 1) * 128], start=True, stop=True), wait=[m_Bw[0]], sig="pe")
                m_s1 = s.emit("pe", lambda E_, c=c, h1=h1, bk=bk, bw1=bw1: E_.matmul(ps_Y[bk][64:128, 256:384], Bw_sb[bw1][:], x_act[:, c * 2048 + h1 * 128:c * 2048 + (h1 + 1) * 128], start=True, stop=True), wait=[m_Bw[1]], sig="pe")
                ybn.post_write(m_s1)
                M["Mt_free"][r0] = m_y0a
                M["Mt_free"][r1] = m_y1a
                M["Bw_free"][bw0] = m_s0
                M["Bw_free"][bw1] = m_s1
                M["Cdec2_free"][bk] = m_y1b
                m_yd = [None, None]
                for e_ix, (hh, m_yb) in enumerate([(h0, m_y0b), (h1, m_y1b)]):
                    m_yd[e_ix] = OP("dv", lambda E_, c=c, hh=hh, bk=bk, e_ix=e_ix: E_.scalar_tensor_tensor(
                        out=yd_sb[e_ix][:], in0=x_act[:, c * 2048 + hh * 128:c * 2048 + (hh + 1) * 128],
                        scalar=dvec[:, c * 16 + hh:c * 16 + hh + 1], in1=ps_Y[bk][:, e_ix * 128:(e_ix + 1) * 128],
                        op0=ALU.mult, op1=ALU.add), wait=[m_s1, m_dv, M["ydfree"][e_ix]])
                    ybn.read(m_yd[e_ix])
                m_h = OP("dv", lambda E_, pr=pr, bk=bk: E_.scalar_tensor_tensor(
                    out=H2f[pr][:], in0=H2f[pr][:], scalar=edec2[bk][:, 127:128],
                    in1=ps_Y[bk][:, 256:384], op0=ALU.mult, op1=ALU.add), wait=[m_s0, m_s1, m_ed])
                ybn.read(m_h)
                M["edec2_free"][bk] = m_h
                OP("po", lambda E_, pr=pr: E_.tensor_copy(out=H2b[pr][:], in_=H2f[pr][:]), wait=[m_h, m_y0b, m_y1b], sig=False)
                m_yg = [None, None]
                for e_ix, hh in enumerate([h0, h1]):
                    gx = (pr % 2) * 2 + e_ix
                    m_yg[e_ix] = OP("po", lambda E_, c=c, hh=hh, e_ix=e_ix, gx=gx: E_.tensor_tensor(
                        out=yg_pool[gx][:], in0=yd_sb[e_ix][:],
                        in1=siluz[:, c * 2048 + hh * 128:c * 2048 + (hh + 1) * 128], op=ALU.mult), wait=[m_yd[e_ix], M["ygfree"][gx], M["siluz_m"][c]])
                M["ydfree"] = m_yg[:]
                YGQ[pr] = m_yg
                if pr >= 2:
                    emit_ygtr(pr - 2)
                if pr + 2 < 8:
                    PIPE[pr + 2] = emit_burst(pr + 2)
                    emit_chain(pr + 2, PIPE[pr + 2])
                if TAIL_DEFER:
                    for _ in range(TAIL_POPS.get(pr, 0)):
                        if M["tailq"]:
                            M["tailq"].popleft()()
            emit_ygtr(6)
            emit_ygtr(7)
            M["scan_x"][c] = [s.mark("dv"), s.mark("pe")]
            M["scan_z"][c] = [s.mark("po")]
            M["Gmask_free"] = s.mark("po")
            M["C2_free"] = s.mark("po")
            if j == 0 and c == 0:
                SNAP("Gmask", Gmask[:], [s.mark("dv")])
                SNAP("subtok", subtok[:, 0:16], [s.mark("dv")])
                SNAP("sdec", sdec_sb[:, 0:16], [s.mark("ac")])
                SNAP("dvec", dvec[:, 0:16], [s.mark("dv")])

            def make_stages(j=j, c=c, marks_yg=list(marks_yg), q=q):
                tx = {}

                def st0():
                    tx["uF"] = DMA("dV0" if c % 2 == 0 else "dV1", uF[c % 2][:], d_u[j * T + c * 128:j * T + (c + 1) * 128, :], wait=[M["uFfree"][c % 2]])
                    bn = BK["mm0"]
                    wv = marks_yg + [m_wout] + bn.pre_write()
                    for pc in range(16):
                        m_mm = s.emit("pe", lambda E_, c=c, pc=pc: E_.matmul(
                            ps_mm[0][:, 0:512], ygT[:, pc * 128:(pc + 1) * 128],
                            wout_sb[:, pc * 1024 + 0 * 512:pc * 1024 + 1 * 512],
                            start=(pc == 0), stop=(pc == 15)),
                            wait=(wv if pc == 0 else ()), sig="pe" if pc == 15 else None)
                    bn.post_write(m_mm)
                    tx["op0"] = m_mm

                def st1():
                    bn = BK["mm1"]
                    wv = marks_yg + bn.pre_write()
                    for pc in range(16):
                        m_mm = s.emit("pe", lambda E_, c=c, pc=pc: E_.matmul(
                            ps_mm[1][:, 0:512], ygT[:, pc * 128:(pc + 1) * 128],
                            wout_sb[:, pc * 1024 + 1 * 512:pc * 1024 + 2 * 512],
                            start=(pc == 0), stop=(pc == 15)),
                            wait=(wv if pc == 0 else ()), sig="pe" if pc == 15 else None)
                    bn.post_write(m_mm)
                    tx["op1"] = m_mm
                    M["ygTfree"] = m_mm

                def st2():
                    usrc = uF[c % 2][:]
                    m_l1 = OP("dv", lambda E_, usrc=usrc: E_.tensor_scalar(out=c4_sb[:], in0=usrc, scalar1=scal[:, 48:49], scalar2=scal[:, 49:50], op0=ALU.subtract, op1=ALU.mult), wait=[tx["uF"]])
                    m_mx = [None, None]
                    for mj in range(2):
                        wv = [tx["op0" if mj == 0 else "op1"], m_l1] + ([M["mixfree"]] if mj == 0 else [])
                        m_mx[mj] = OP("dv", lambda E_, c=c, mj=mj: E_.scalar_tensor_tensor(
                            out=mix_sb[:, mj * 512:(mj + 1) * 512], in0=c4_sb[:, mj * 512:(mj + 1) * 512],
                            scalar=1.0, in1=ps_mm[mj][:, 0:512], op0=ALU.mult, op1=ALU.add), wait=wv)
                        BK[f"mm{mj}"].read(m_mx[mj])
                    tx["mx"] = m_mx
                    tx["usrc"] = usrc

                def st3():
                    m_mx = tx["mx"]
                    mixv = mix_sb[:]
                    OP("dv", lambda E_, mixv=mixv: E_.tensor_reduce(out=scal[:, 41:42], in_=mixv, axis=AX.X, op=ALU.add), wait=[m_mx[0], m_mx[1]], sig=False)
                    m_s3b = OP("ac", lambda E_, mixv=mixv: E_.activation(out=sq_scr[:], in_=mixv, func=AF.Square, accum_out=scal[:, 42:43]), wait=[m_mx[0], m_mx[1]])
                    OP("dv", lambda E_: E_.tensor_scalar(out=scal[:, 43:44], in0=scal[:, 41:42], scalar1=1.0 / DM, scalar2=None, op0=ALU.mult), wait=[m_s3b], sig=False)
                    OP("dv", lambda E_: E_.tensor_tensor(out=scal[:, 45:46], in0=scal[:, 43:44], in1=scal[:, 43:44], op=ALU.mult), sig=False)
                    OP("dv", lambda E_: E_.scalar_tensor_tensor(out=scal[:, 44:45], in0=scal[:, 42:43], scalar=1.0 / DM, in1=scal[:, 45:46], op0=ALU.mult, op1=ALU.subtract), sig=False)
                    m_rc = OP("ac", lambda E_: E_.activation(out=scal[:, 45:46], in_=scal[:, 44:45], func=AF.Ln, bias=epsc[:, 0:1]), wait=[s.mark("dv")])
                    m_r3 = OP("ac", lambda E_: E_.activation(out=scal[:, 46:47], in_=scal[:, 45:46], func=AF.Exp, scale=-0.5))
                    m_mr = OP("dv", lambda E_: E_.tensor_tensor(out=scal[:, 47:48], in0=scal[:, 43:44], in1=scal[:, 46:47], op=ALU.mult), wait=[m_r3])
                    m_x2a = OP("dv", lambda E_, mixv=mixv: E_.tensor_scalar(out=sq_scrF[:], in0=mixv, scalar1=scal[:, 46:47], scalar2=scal[:, 47:48], op0=ALU.mult, op1=ALU.subtract), wait=[m_mr, M["x2free"]])
                    M["mixfree"] = m_x2a
                    m_x2 = OP("dv", lambda E_, c=c: E_.tensor_tensor(out=x2_sb[:], in0=sq_scrF[:], in1=tx["usrc"], op=ALU.add))
                    M["uFfree"][c % 2] = m_x2
                    tx["x2"] = m_x2
                    tx["x2a"] = m_x2a

                def st4():
                    x2v = x2_sb[:]
                    m_x2 = tx["x2"]
                    OP("dv", lambda E_, x2v=x2v: E_.tensor_reduce(out=qst[:, 0:1], in_=x2v, axis=AX.X, op=ALU.add), wait=[m_x2], sig=False)
                    m_f2 = OP("ac", lambda E_, x2v=x2v: E_.activation(out=sq_scr[:], in_=x2v, func=AF.Square, accum_out=qst[:, 1:2]), wait=[m_x2, tx["x2a"]])
                    OP("dv", lambda E_: E_.tensor_scalar(out=qst[:, 2:3], in0=qst[:, 0:1], scalar1=1.0 / DM, scalar2=None, op0=ALU.mult), wait=[m_f2], sig=False)
                    OP("dv", lambda E_: E_.tensor_tensor(out=qst[:, 4:5], in0=qst[:, 2:3], in1=qst[:, 2:3], op=ALU.mult), sig=False)
                    OP("dv", lambda E_: E_.scalar_tensor_tensor(out=qst[:, 3:4], in0=qst[:, 1:2], scalar=1.0 / DM, in1=qst[:, 4:5], op0=ALU.mult, op1=ALU.subtract), sig=False)
                    m_f3 = OP("ac", lambda E_: E_.activation(out=qst[:, 4:5], in_=qst[:, 3:4], func=AF.Ln, bias=epsc[:, 0:1]), wait=[s.mark("dv")])
                    m_f4 = OP("ac", lambda E_: E_.activation(out=qst[:, 5:6], in_=qst[:, 4:5], func=AF.Exp, scale=-0.5))
                    m_c4 = OP("dv", lambda E_, x2v=x2v: E_.tensor_scalar(out=c4_sb[:], in0=x2v, scalar1=qst[:, 2:3], scalar2=None, op0=ALU.subtract), wait=[m_f2])
                    m_am = OP("dv", lambda E_: E_.tensor_reduce(out=qst[:, 6:7], in_=c4_sb[:], axis=AX.X, op=ALU.max, apply_absolute_value=True))
                    OP("dv", lambda E_: E_.tensor_scalar(out=qst[:, 7:8], in0=qst[:, 6:7], scalar1=qst[:, 5:6], scalar2=1e-8, op0=ALU.mult, op1=ALU.max), wait=[m_f4, m_am], sig=False)
                    OP("dv", lambda E_: E_.reciprocal(out=qst[:, 8:9], in_=qst[:, 7:8]), sig=False)
                    OP("dv", lambda E_: E_.scalar_tensor_tensor(out=qst[:, 10:11], in0=qst[:, 8:9], scalar=127.0, in1=qst[:, 5:6], op0=ALU.mult, op1=ALU.mult), sig=False)
                    m_xs = OP("dv", lambda E_: E_.tensor_scalar(out=qst[:, 11:12], in0=qst[:, 7:8], scalar1=1.0 / 127.0, scalar2=None, op0=ALU.mult))
                    OP("dv", lambda E_: E_.tensor_scalar(out=sq_scrF[:], in0=c4_sb[:], scalar1=qst[:, 10:11], scalar2=MAGIC, op0=ALU.mult, op1=ALU.add), wait=[m_xs], sig=False)
                    m_q2 = OP("dv", lambda E_: E_.tensor_scalar(out=xq_tok[:], in0=sq_scrF[:], scalar1=MAGIC, scalar2=None, op0=ALU.subtract))
                    tx["q2"] = m_q2
                    tx["xs"] = m_xs

                def st5():
                    wv = [tx["q2"]] + BK["trb"].pre_write()
                    for fc in range(8):
                        m_tq = s.emit("pe", lambda E_, fc=fc: E_.transpose(ps_trb[:, fc * 128:(fc + 1) * 128], xq_tok[:, fc * 128:(fc + 1) * 128], id_bf[:]),
                                      wait=(wv if fc == 0 else ()), sig="pe" if fc == 7 else None)
                    BK["trb"].post_write(m_tq)
                    m_eq = OP("dv", lambda E_, c=c: E_.tensor_copy(out=xqT[:], in_=ps_trb[:]), wait=[m_tq, M["xqTfree"]])
                    BK["trb"].read(m_eq)
                    tx["eq"] = m_eq

                def st6():
                    m_fmm = [None, None]
                    for oj in range(2):
                        bn = BK[f"mm{oj}"]
                        wv = [tx["eq"]] + bn.pre_write()
                        for fc in range(8):
                            m_mm = s.emit("pe", lambda E_, c=c, oj=oj, fc=fc: E_.matmul(
                                ps_mm[oj][:, 0:512], xqT[:, fc * 128:(fc + 1) * 128],
                                wffn_sb[:, fc * 1024 + oj * 512:fc * 1024 + (oj + 1) * 512],
                                start=(fc == 0), stop=(fc == 7)),
                                wait=(wv if fc == 0 else ()), sig="pe" if fc == 7 else None)
                        bn.post_write(m_mm)
                        m_fmm[oj] = m_mm
                    M["xqTfree"] = m_fmm[1]
                    m_fe = [None, None]
                    for oj in range(2):
                        m_fe[oj] = OP("ac", lambda E_, oj=oj: E_.activation(out=sq_scrF[:, oj * 512:(oj + 1) * 512], in_=ps_mm[oj][:, 0:512], func=AF.Copy, scale=qst[:, 11:12]), wait=[m_fmm[oj], tx["xs"]])
                        BK[f"mm{oj}"].read(m_fe[oj])
                    tx["fe"] = m_fe

                def st7():
                    m_fe = tx["fe"]
                    m_p1 = OP("po", lambda E_, c=c: E_.tensor_tensor(out=ffn_sb[:], in0=sq_scrF[:], in1=br_ws[:], op=ALU.mult), wait=[m_fe[0], m_fe[1], M["ffnfree"]])
                    m_p2 = OP("po", lambda E_, c=c: E_.tensor_tensor(out=ffn_sb[:], in0=ffn_sb[:], in1=br_fb[:], op=ALU.add))
                    fv = ffn_sb[:]
                    OP("dv", lambda E_, fv=fv: E_.tensor_reduce(out=qst[:, 12:13], in_=fv, axis=AX.X, op=ALU.add), wait=[m_p2], sig=False)
                    m_s5b = OP("ac", lambda E_, fv=fv: E_.activation(out=sq_scr[:], in_=fv, func=AF.Square, accum_out=qst[:, 13:14]), wait=[m_p2])
                    OP("dv", lambda E_: E_.tensor_scalar(out=qst[:, 12:13], in0=qst[:, 12:13], scalar1=1.0 / DM, scalar2=None, op0=ALU.mult), wait=[m_s5b], sig=False)
                    OP("dv", lambda E_: E_.tensor_tensor(out=qst[:, 14:15], in0=qst[:, 12:13], in1=qst[:, 12:13], op=ALU.mult), sig=False)
                    OP("dv", lambda E_: E_.scalar_tensor_tensor(out=qst[:, 13:14], in0=qst[:, 13:14], scalar=1.0 / DM, in1=qst[:, 14:15], op0=ALU.mult, op1=ALU.subtract), sig=False)
                    m_r5 = OP("ac", lambda E_: E_.activation(out=qst[:, 14:15], in_=qst[:, 13:14], func=AF.Ln, bias=epsc[:, 0:1]), wait=[s.mark("dv")])
                    m_sq5 = OP("ac", lambda E_: E_.activation(out=qst[:, 15:16], in_=qst[:, 14:15], func=AF.Exp, scale=-0.5))
                    m_m5 = OP("dv", lambda E_: E_.tensor_tensor(out=qst[:, 14:15], in0=qst[:, 12:13], in1=qst[:, 15:16], op=ALU.mult), wait=[m_sq5])
                    m_o1 = OP("dv", lambda E_, fv=fv: E_.tensor_scalar(out=sq_scrF[:], in0=fv, scalar1=qst[:, 15:16], scalar2=qst[:, 14:15], op0=ALU.mult, op1=ALU.subtract), wait=[m_m5])
                    M["ffnfree"] = m_o1
                    m_o2 = OP("dv", lambda E_, c=c: E_.tensor_tensor(out=x2_sb[:], in0=sq_scrF[:], in1=x2_sb[:], op=ALU.add), wait=[M["outst"]])
                    m_st_ = DMA("dO", d_out[j * T + c * 128:j * T + (c + 1) * 128, :], x2_sb[:], wait=[m_o2])
                    M["x2free"] = m_st_
                    M["outst"] = m_st_

                return [st0, st1, st2, st3, st4, st5, st6, st7]

            cst = STATS[(j, c)][0]
            OP("dv", lambda E_, cst=cst: E_.tensor_copy(out=scal[:, 48:49], in_=cst["mu1"]), sig=False)
            OP("dv", lambda E_, cst=cst: E_.tensor_copy(out=scal[:, 49:50], in_=cst["rstd1"]), sig=False)
            for stg in make_stages():
                M["tailq"].append(stg)
            if not TAIL_DEFER:
                while M["tailq"]:
                    M["tailq"].popleft()()
        M["stats_free"][q] = s.mark("dv")
        M["dl_free"][q] = s.mark("dv")
        M["smT_free"][q] = s.mark("dv")
        M["toks_free"][q] = s.mark("dv")
        M["BC_free"][q] = s.mark("dv")
        M["BCT_free"][q] = [s.mark("pe"), s.mark("po")]
        M["lhl_free"][q] = s.mark("pe")

    NTv = globals().get('NT_OVERRIDE', NT)
    emit_uload_stats_t2b(0, 0, extra=m_winb + [m_consts, m_thw, m_wout, m_acol, m_initpo, m_brws, m_oh8, m_win8])
    emit_uload_stats_t2b(0, 1)
    for i in range(NTv + 1):
        if i < NTv:
            emit_A1(i)
        if i >= 1:
            if i == NT:
                M["patch_zero"] = OP("dv", lambda E_: E_.memset(patchcol[:], 0.0), wait=[M["patch"]])
            emit_F(i - 1, is_last=(i == NT))
        if i < NTv:
            emit_A2(i)

    while M["tailq"]:
        M["tailq"].popleft()()
    final_waits = [(ch, s.cnt[ch]) for ch in DMA_CH]
    s.emit("sy", lambda E_: None, wait=final_waits, sig=None)

    with nc.Block("init") as b0:
        b0.sync(lambda eng: s.run(0, "sy", eng))
        b0.tensor(lambda eng: s.run(0, "pe", eng))
        b0.vector(lambda eng: s.run(0, "dv", eng))
        b0.scalar(lambda eng: s.run(0, "ac", eng))
        b0.gpsimd(lambda eng: s.run(0, "po", eng))
    with nc.Block("main") as b1:
        b1.sync(lambda eng: s.run(1, "sy", eng))
        b1.tensor(lambda eng: s.run(1, "pe", eng))
        b1.vector(lambda eng: s.run(1, "dv", eng))
        b1.scalar(lambda eng: s.run(1, "ac", eng))
        b1.gpsimd(lambda eng: s.run(1, "po", eng))

    st.close()
    return nc


def make_consts():
    bf = ml_dtypes.bfloat16
    idbf = np.eye(128, dtype=bf)
    idf = np.eye(128, dtype=np.float32)
    tri = (np.arange(128)[:, None] < np.arange(128)[None, :]).astype(np.float32)
    oh48 = np.zeros((48, NH * 128), dtype=bf)
    for h in range(NH):
        oh48[h, h * 128:(h + 1) * 128] = 1
        oh48[32 + h, h * 128:(h + 1) * 128] = 1
    pp48 = np.zeros((48, 8 * 128), dtype=bf)
    for pr in range(8):
        h0, h1 = 2 * pr, 2 * pr + 1
        pp48[h0, pr * 128:pr * 128 + 64] = 1
        pp48[32 + h0, pr * 128:pr * 128 + 64] = 1
        pp48[h1, pr * 128 + 64:pr * 128 + 128] = 1
        pp48[32 + h1, pr * 128 + 64:pr * 128 + 128] = 1
    oh8 = np.zeros((8, 8 * 128), dtype=np.float32)
    for ocb in range(8):
        oh8[ocb, ocb * 128:(ocb + 1) * 128] = 1
    return dict(idbf=idbf, idf=idf, tri=tri, oh48=oh48, pp48=pp48, oh8=oh8)


def make_inputs(inputs_dict):
    g = lambda k: np.asarray(inputs_dict[k], np.float32)
    consts = make_consts()
    scale = g("L0_attn_norm_scale")
    biasv = g("L0_attn_norm_bias")
    scale_big = np.ascontiguousarray(np.broadcast_to(scale.reshape(8, 128).T[:, :, None], (128, 8, 128)).reshape(128, DM)).astype(ml_dtypes.bfloat16)
    bcol = np.ascontiguousarray(biasv.reshape(8, 128).T).astype(ml_dtypes.bfloat16)
    common = {
        "w_in": g("L0_attn_in_proj_w"), "w_th": g("L0_attn_theta_w"),
        "w_out": g("L0_attn_out_proj_w"), "w_ffn": g("L0_ffn_weight"),
        "idbf": consts["idbf"], "idf": consts["idf"], "tri": consts["tri"],
        "oh48": consts["oh48"], "pp48": consts["pp48"], "oh8": consts["oh8"],
        "scale_big": scale_big, "bcol": bcol,
        "br_dtb": np.ascontiguousarray(np.tile(g("L0_attn_dt_bias")[None, :], (128, 1))),
        "br_D": np.ascontiguousarray(np.tile(g("L0_attn_D")[None, :], (128, 1))),
        "br_bc": np.ascontiguousarray(np.tile(np.concatenate([g("L0_attn_bcB_scale"), g("L0_attn_bcC_scale")])[None, :], (128, 1))),
        "br_bcb": np.ascontiguousarray(np.tile(np.concatenate([g("L0_attn_B_bias"), g("L0_attn_C_bias")])[None, :], (128, 1))),
        "br_fb": np.ascontiguousarray(np.tile(g("L0_ffn_bias")[None, :], (128, 1))),
        "alog": g("L0_attn_A_log").reshape(NH, 1),
    }
    u = g("L0_attn_u")
    return [dict(common, u=np.ascontiguousarray(u[b])) for b in range(8)]


def kernel(**inputs):
    nc = build_program()
    in_maps = make_inputs(inputs)
    res = run_bass_kernel_spmd(nc, in_maps, core_ids=list(range(8)))
    return np.stack([res.results[b]["out"] for b in range(8)], axis=0)
